# version 06
# baseline (speedup 1.0000x reference)
import numpy as np
import ml_dtypes
from contextlib import ExitStack

import concourse.bass as bass
import concourse.mybir as mybir
from concourse.bass_utils import run_bass_kernel_spmd
from concourse.bass_core_rust import IndirectOffsetOnAxis

F32 = mybir.dt.float32
BF16 = mybir.dt.bfloat16
AF = mybir.ActivationFunctionType
ALU = mybir.AluOpType
AX = mybir.AxisListType
I32 = mybir.dt.int32

NCORES = 8
T = 4096
D = 1024
NT = T // 128
DC = D // 128
E = 8

BFNP = ml_dtypes.bfloat16

ENGINES = ("sync", "gpsimd", "tensor", "scalar", "vector")



class Step:
    __slots__ = ("idx", "eng", "fn", "reads", "writes", "dma", "deps",
                 "need_inc", "seq", "fam", "memberidx")

    def __init__(self, idx, eng, fn, reads, writes, dma):
        self.idx = idx
        self.eng = eng
        self.fn = fn
        self.reads = reads
        self.writes = writes
        self.dma = dma
        self.deps = []
        self.need_inc = False
        self.seq = 0
        self.fam = None
        self.memberidx = 0


class Sched:

    def __init__(self, nc, ctx, serialize=False):
        self.nc = nc
        self.ctx = ctx
        self.steps = []
        self.last_w = {}
        self.readers = {}
        self.serialize = serialize
        self.sems = {e: ctx.enter_context(nc.semaphore(f"sched_{e}")) for e in ENGINES}
        self.fams = {}
        self.prefix = ""
        self.nbar = 0

    def _fam(self, key, eng, bulk):
        f = self.fams.get(key)
        if f is None:
            f = {"sem": self.ctx.enter_context(self.nc.semaphore(f"fam_{len(self.fams)}")),
                 "eng": eng, "count": 0, "bulk": bulk, "key": key}
            self.fams[key] = f
        if f["eng"] != eng:
            raise RuntimeError(f"fam {key} used from {eng} and {f['eng']}")
        if bulk and not f["bulk"]:
            raise RuntimeError(f"fam {key} bulk mismatch")
        return f

    def step(self, eng, fn, reads=(), writes=(), dma=False, fam=None, bulk=False):
        i = len(self.steps)
        p = self.prefix
        reads = tuple((p, t) for t in reads)
        writes = tuple((p, t) for t in writes)
        st = Step(i, eng, fn, reads, writes, dma)
        if dma:
            if fam is None:
                w0 = writes[0][1] if writes else ("anon", i)
                fam = w0[0] if isinstance(w0, tuple) else w0
                fam = f"{eng}:{fam}" + ("" if not isinstance(w0, tuple) or len(w0) < 2
                                        else f":{w0[1]}")
            fam = f"{p}/{fam}"
            f = self._fam(fam, eng, bulk)
            f["count"] += 1
            f["last_idx"] = i
            st.fam = fam
            st.memberidx = f["count"]
        deps = set()
        for t in st.reads:
            if t in self.last_w:
                deps.add(self.last_w[t])
        for t in st.writes:
            if t in self.last_w:
                deps.add(self.last_w[t])
            for r in self.readers.get(t, ()):
                deps.add(r)
        if self.serialize and i > 0:
            deps.add(i - 1)
        st.deps = sorted(deps)
        self.steps.append(st)
        for t in st.reads:
            self.readers.setdefault(t, []).append(i)
        for t in st.writes:
            self.last_w[t] = i
            self.readers[t] = []
        return i

    def barrier(self):
        deps = set()
        last_eng = {}
        for st in self.steps:
            if st.fn is not None:
                last_eng[st.eng] = st.idx
        for f in self.fams.values():
            if f.get("last_idx") is not None:
                deps.add(f["last_idx"])
        deps.update(last_eng.values())
        self.nbar += 1
        for e in ENGINES:
            i = len(self.steps)
            st = Step(i, e, None, (), ((f"__bar{self.nbar}", e),), False)
            st.deps = sorted(d for d in deps if d != i)
            self.steps.append(st)

    def finalize(self):
        for st in self.steps:
            for d in st.deps:
                p = self.steps[d]
                if p.fn is None:
                    raise RuntimeError("dependency on a barrier step")
                if not (p.eng == st.eng == "tensor" and not p.dma):
                    p.need_inc = True
        counts = {e: 0 for e in ENGINES}
        for st in self.steps:
            if st.dma:
                st.need_inc = True
                continue
            if st.need_inc:
                counts[st.eng] += 1
                st.seq = counts[st.eng]
        self.final_counts = counts

    def _wait_of(self, p):
        if p.dma:
            f = self.fams[p.fam]
            v = 16 * (f["count"] if f["bulk"] else p.memberidx)
            return (p.fam, f["sem"], v)
        return (p.eng, self.sems[p.eng], p.seq)

    def emit_engine(self, name, eng):
        watermark = {}
        for st in self.steps:
            if st.eng != name:
                continue
            for d in st.deps:
                p = self.steps[d]
                if p.eng == name == "tensor" and not p.dma:
                    continue
                if not p.need_inc:
                    raise RuntimeError("dep without inc")
                key, sem, val = self._wait_of(p)
                if val > watermark.get(key, 0):
                    eng.wait_ge(sem, val)
                    watermark[key] = val
            if st.fn is None:
                continue
            inst = st.fn(eng)
            if st.need_inc:
                if inst is None:
                    raise RuntimeError(f"step {st.idx} on {name} returned no inst")
                if st.dma:
                    inst.then_inc(self.fams[st.fam]["sem"], 16)
                else:
                    inst.then_inc(self.sems[name], 1)
        for f in self.fams.values():
            if f["eng"] == name and f["count"]:
                eng.wait_ge(f["sem"], 16 * f["count"])

    def build_block(self, ctx):
        self.finalize()
        block = ctx.enter_context(self.nc.Block())

        @block.sync
        def _(e):
            self.emit_engine("sync", e)

        @block.gpsimd
        def _(e):
            self.emit_engine("gpsimd", e)

        @block.tensor
        def _(e):
            self.emit_engine("tensor", e)

        @block.scalar
        def _(e):
            self.emit_engine("scalar", e)

        @block.vector
        def _(e):
            self.emit_engine("vector", e)



def host_consts(core_id):
    ident = np.eye(128, dtype=np.float32)
    ltri = (np.arange(128)[:, None] < np.arange(128)[None, :]).astype(np.float32)
    bmask = np.zeros((128, 128), np.float32)
    for g in range(32):
        bmask[g * 4:(g + 1) * 4, g * 4:(g + 1) * 4] = 1.0
    corelt = np.zeros((8, 8), np.float32)
    corelt[:core_id, 0] = 1.0
    return {"c_ident": ident, "c_ltri": ltri, "c_bmask": bmask,
            "c_corelt": corelt}



def record_ln_tile(S, c, bufs, x_dram, extra_stage_readers=()):
    nb = len(bufs["xt"])
    xt = bufs["xt"][c % nb]
    xn = bufs["xn"][c % nb]
    stage = bufs["stage"][c % nb]
    stat = bufs["stat"]
    sqd = bufs["sqd"]
    ps_tr = bufs["ps_tr"]
    ident_bf = bufs["ident_bf"]
    tk_xt = ("xt", c % nb)
    tk_xn = ("xn", c % nb)
    tk_st = ("stage", c % nb)

    S.step("sync", lambda e: e.dma_start(xt[:], x_dram[c * 128:(c + 1) * 128, :]),
           reads=(("x_dram", c),), writes=(tk_xt,), dma=True)
    S.step("scalar", lambda e: e.activation(sqd[:], xt[:], AF.Square,
                                            accum_out=stat[:, 1:2]),
           reads=(tk_xt,), writes=("lnsqd", "lnssq"))
    S.step("vector", lambda e: e.reduce_sum(stat[:, 0:1], xt[:], axis=AX.X),
           reads=(tk_xt,), writes=("lnssum",))
    S.step("vector", lambda e: e.tensor_scalar_mul(stat[:, 2:3], stat[:, 0:1], 1.0 / D),
           reads=("lnssum",), writes=("lnmean",))
    S.step("vector", lambda e: e.tensor_mul(stat[:, 7:8], stat[:, 2:3], stat[:, 2:3]),
           reads=("lnmean",), writes=("lnmsq",))
    S.step("vector", lambda e: e.tensor_scalar(stat[:, 3:4], stat[:, 1:2], 1.0 / D,
                                               stat[:, 7:8], ALU.mult, ALU.subtract),
           reads=("lnssq", "lnmsq"), writes=("lnvar",))
    S.step("scalar", lambda e: e.activation(stat[:, 4:5], stat[:, 3:4], AF.Sqrt,
                                            bias=bufs["epsc"][:, 0:1]),
           reads=("lnvar", "lneps"), writes=("lnsd",))
    S.step("vector", lambda e: e.reciprocal(stat[:, 5:6], stat[:, 4:5]),
           reads=("lnsd",), writes=("lnrstd",))
    S.step("vector", lambda e: e.tensor_scalar(xn[:], xt[:], stat[:, 2:3],
                                               stat[:, 5:6], ALU.subtract, ALU.mult),
           reads=(tk_xt, "lnmean", "lnrstd"), writes=(tk_xn,))

    def tr(e):
        for g in range(DC):
            i = e.transpose(ps_tr[:, g * 128:(g + 1) * 128],
                            xn[:, g * 128:(g + 1) * 128], ident_bf[:])
        return i
    S.step("tensor", tr, reads=(tk_xn, "ident_bf"), writes=("ps_tr",))
    if bufs.get("evac_dsts") is not None:
        dsts, dtok = bufs["evac_dsts"](c)
        def ev(e, dsts=dsts):
            for g in range(DC):
                i = e.copy(dsts[g], ps_tr[:, g * 128:(g + 1) * 128])
            return i
        S.step("scalar", ev, reads=("ps_tr",) + tuple(bufs.get("evac_extra_reads", ())),
               writes=(dtok,))
        return dtok
    S.step("scalar", lambda e: e.copy(stage[:], ps_tr[:]),
           reads=("ps_tr",), writes=(tk_st,))
    return tk_st


def alloc_ln_bufs(sbf, psf, nbuf=2):
    return {
        "xt": [sbf(f"ln_xt{i}", [128, D], F32) for i in range(nbuf)],
        "xn": [sbf(f"ln_xn{i}", [128, D], BF16) for i in range(nbuf)],
        "stage": [sbf(f"ln_st{i}", [128, D], BF16) for i in range(nbuf)],
        "sqd": sbf("ln_sqd", [128, D], F32),
        "stat": sbf("ln_stat", [128, 8], F32),
        "epsc": sbf("ln_eps", [128, 1], F32),
        "ps_tr": psf("ps_tr", [128, 1024], BF16),
        "ident_bf": sbf("ident_bf", [128, 128], BF16),
    }




class BufAlloc:
    ALIAS = {
        "w1_sb": ("wpool", 0), "w2_sb": ("wpool", 16384),
        "wkT_sb": ("wpool", 0), "wvT_sb": ("wpool", 8192),
        "wrT_sb": ("wpool", 16384), "woT_sb": ("wpool", 24576),
        "wq_sb": ("wpool", 0), "wo_sb": ("wpool", 24576),
        "h_sb": ("apool", 0), "xnG": ("apool", 16384),
        "xnTa": ("apool", 0),
        "xk_a": ("apool", 4232), "xv_a": ("apool", 8328), "xr_a": ("apool", 12424),
        "r_a": ("apool", 16520), "y_a": ("apool", 20616), "dbuf": ("apool", 24712),
        "qT": ("apool", 4128), "kT": ("apool", 8256),
        "vA": ("apool", 12384), "vB": ("apool", 16480),
        "Pb": ("apool", 20576), "PTb": ("apool", 21600), "OT": ("apool", 22624),
        "acc": ("fpool", 0),
        "E1": ("fpool", 0), "K1": ("fpool", 512), "EV": ("fpool", 1024),
        "KV": ("fpool", 1536), "Abuf": ("fpool", 2048), "Pbuf": ("fpool", 2564),
        "dnb": ("fpool", 3080), "recb": ("fpool", 3592), "wkvb": ("fpool", 4104),
        "Pf": ("fpool", 0), "ybst": ("fpool", 1024),
        "yb3_0": ("fpool", 2048), "yb3_1": ("fpool", 3072),
        "ln_sqd": ("fpool", 9216),
        "xgT": ("apool", 0), "h_sg": ("apool", 12288),
        "rowb0": ("apool", 20480), "rowb1": ("apool", 21520),
        "grow0": ("apool", 22560), "grow1": ("apool", 23600),
        "tokhl": ("apool", 24640),
        "ysb0": ("fpool", 0), "ysb1": ("fpool", 1024),
        "rank_sb": ("fpool", 2048), "cum": ("fpool", 2056),
        "selA": ("fpool", 2064), "selB": ("fpool", 2072),
        "scr8": ("fpool", 2080), "ercol": ("fpool", 2088),
        "slotf": ("fpool", 2094), "ecap8": ("fpool", 2096),
        "iota8f": ("fpool", 2104), "mwc": ("fpool", 2112),
        "tokf": ("fpool", 2124), "onescol2": ("fpool", 2136),
        "gsb": ("fpool", 2140), "m8": ("fpool", 2148), "esb": ("fpool", 2156),
        "selsb": ("fpool", 2164), "srow": ("fpool", 2172),
        "rkst0": ("fpool", 2180), "rkst1": ("fpool", 2188),
        "cb_row": ("fpool", 2196), "negcap": ("fpool", 2204),
        "nthr_sb": ("fpool", 2212), "ag_sb": ("fpool", 2220),
        "corelt_sb": ("fpool", 2228), "cnt_sb": ("fpool", 2236),
        "pref_sb": ("fpool", 2268), "nstag0": ("fpool", 2301),
        "nstag1": ("fpool", 2309), "keep_sb": ("fpool", 2317),
        "maskw_bf": ("apool", 24704),
    }
    POOLS = {"wpool": (BF16, 32768), "apool": (BF16, 25344), "fpool": (F32, 10240)}
    PSMAP = {
        "ffn": {"ps_g": "f0", "ps_h0": "f1", "ps_h1": "f2", "ps_y0": "f3",
                "ps_y1": "f4", "ps_cnt": "f5", "ps_rk": "f5",
                "ps_tr": "b0", "ps_mwt": "b1"},
        "l1a": {"ps_k": "f0", "ps_v": "f1", "ps_r": "f2", "ps_o0": "f3",
                "ps_o1": "f4", "ps_tr": "b0"},
        "l0a": {"ps_s0": "f0", "ps_s1": "f1", "ps_av0": "f2", "ps_av1": "f3",
                "ps_y0": "f4", "ps_y1": "f5", "ps_tr": "b0", "ps_pt": "b1"},
    }

    def __init__(self, nc, ctx):
        self.nc = nc
        self.ctx = ctx
        self.pools = {k: ctx.enter_context(nc.sbuf_tensor(k, [128, n], dt))
                      for k, (dt, n) in self.POOLS.items()}
        self.banks = {}
        for i in range(6):
            self.banks[f"f{i}"] = ctx.enter_context(
                nc.psum_tensor(f"bank_f{i}", [128, 512], F32))
        for i in range(2):
            self.banks[f"b{i}"] = ctx.enter_context(
                nc.psum_tensor(f"bank_b{i}", [128, 1024], BF16))
        self.memo = {}
        self.kind = None

    def buf(self, name, shape, dt):
        if name in self.ALIAS:
            pool, start = self.ALIAS[name]
            pdt, pn = self.POOLS[pool]
            assert dt == pdt and shape[0] <= 128 and start + shape[1] <= pn, \
                (name, shape, dt)
            return self.pools[pool][0:shape[0], start:start + shape[1]]
        if name in self.memo:
            t, s, d = self.memo[name]
            assert s == tuple(shape) and d == dt, (name, shape, s)
            return t
        t = self.ctx.enter_context(self.nc.sbuf_tensor(name, shape, dt))
        self.memo[name] = (t, tuple(shape), dt)
        return t

    def psum(self, name, shape, dt):
        key = self.PSMAP[self.kind][name]
        b = self.banks[key]
        assert (dt == F32 and key.startswith("f") and shape == [128, 512]) or \
               (dt == BF16 and key.startswith("b") and shape == [128, 1024]), \
               (name, shape, dt)
        return b


def record_ffn(S, dr, B, layer, split="all", debug=False, sparse=False):
    assert not debug
    H = 2048 if layer == 0 else 4096
    NFB = H // 2048
    FCB = 16
    G = 1024
    NG = T // G
    GT = G // 128
    NFI = H // 128
    MWSCALE = 0.5 if layer == 0 else 0.0625
    ACTF = AF.Gelu if layer == 0 else AF.Relu
    CAPTHR = 10239.5
    do_p1 = split in ("all", "count", "cnt1")
    do_p2 = split in ("all", "main")
    cap = (layer == 0)
    do_counts = cap or split == "cnt1"

    do_p1 = split in ("all", "count", "cnt1")
    do_p2 = split in ("all", "main")
    cap = (layer == 0)
    do_counts = cap or split == "cnt1"
    nc = S.nc
    sbf = B.buf
    psf = B.psum
    c_ident = dr["c_ident"]
    x_in = dr.get("x_in")
    x_out = dr.get("x_out")
    wg = dr.get("wg")
    rb = dr.get("rb")
    w1 = dr.get("w1")
    b1T = dr.get("b1T")
    w2 = dr.get("w2")
    b2 = dr.get("b2")
    xnT_hbm = dr.get("xnT")
    mw_in = dr.get("mw_in"); sel_in = dr.get("sel_in")
    agcnt = dr.get("agcnt"); pref_in = dr.get("pref_in")
    c_ltri = dr.get("c_ltri"); c_corelt = dr.get("c_corelt")
    nthr_hbm = dr.get("nthr")
    mw_out = dr.get("mw_out"); sel_out = dr.get("sel_out")
    pref_out = dr.get("pref_out")
    if True:

        ident_bf = sbf("ident_bf2", [128, 128], BF16)
        S.step("gpsimd", lambda e: e.dma_start(ident_bf[:], c_ident[:, :]),
               writes=("ident_bf",), dma=True)
        maskw = sbf("maskw", [128, NT * E], F32)
        maskw_bf = sbf("maskw_bf", [128, 8], BF16)
        maskwT = sbf("maskwT", [8, T], BF16)
        ps_g = psf("ps_g", [128, 512], F32)
        ps_mwt = psf("ps_mwt", [128, 1024], BF16)

        if do_p1:
            lnb = alloc_ln_bufs(sbf, psf)
            lnb["ident_bf"] = ident_bf
            wg_sb = sbf("wg_sb", [128, DC * E], BF16)
            gsb = sbf("gsb", [128, 8], F32)
            m8 = sbf("m8", [128, 8], F32)
            srow = sbf("srow", [128, 4], F32)
            esb = sbf("esb", [128, 8], F32)
            selsb = sbf("selsb", [128, 8], F32)
            S.step("vector", lambda e: e.memset(lnb["epsc"][:], 1e-5),
                   writes=("lneps",))
            S.step("gpsimd", lambda e: e.dma_start(
                wg_sb[:].rearrange("p (g e) -> p g e", e=E),
                wg[:, :].rearrange("(g p) e -> p g e", p=128)),
                writes=("wg_sb",), dma=True)
            if do_counts:
                ones_r = sbf("ones_r", [1, 128], F32)
                sel_a = sbf("sel_a", [128, NT * E], F32)
                onescol = sbf("onescol", [128, 1], F32)
                cnt_sb = sbf("cnt_sb", [8, NT], F32)
                pref_sb = sbf("pref_sb", [8, NT + 1], F32)
                ps_cnt = psf("ps_cnt", [128, 512], F32)
                S.step("vector", lambda e: e.memset(ones_r[:], 1.0), writes=("ones_r",))
                S.step("vector", lambda e: e.memset(onescol[:], 1.0), writes=("onescol",))
            if cap:
                rb_sb = sbf("rb_sb", [1, E], F32)
                S.step("sync", lambda e: e.dma_start(rb_sb[:], rb[:, :]),
                       writes=("rb_sb",), dma=True)

            for c in range(NT):
                tk_st = record_ln_tile(S, c, lnb, x_in)
                stage = lnb["stage"][c % 2]
                dst = xnT_hbm[:, c * 128:(c + 1) * 128].rearrange("(g p) k -> p g k", p=128)
                S.step("sync", lambda e, dst=dst, stage=stage: e.dma_start(
                    dst, stage[:].rearrange("p (g k) -> p g k", k=128)),
                    reads=(tk_st,), writes=(("xnT", c),), dma=True, fam=f"xnT{c % 2}")

                def gmm(e, stage=stage):
                    for g in range(DC):
                        i = e.matmul(ps_g[:, 0:8], stage[:, g * 128:(g + 1) * 128],
                                     wg_sb[:, g * E:(g + 1) * E],
                                     start=(g == 0), stop=(g == DC - 1 and not cap))
                    if cap:
                        i = e.matmul(ps_g[:, 0:8], ones_r[:], rb_sb[:],
                                     start=False, stop=True)
                    return i
                rds = (tk_st, "wg_sb") + (("ones_r", "rb_sb") if cap else ())
                S.step("tensor", gmm, reads=rds, writes=("ps_g",))
                S.step("vector", lambda e: e.tensor_copy(gsb[:], ps_g[:, 0:8]),
                       reads=("ps_g",), writes=("gsb",))
                S.step("vector", lambda e: e.max(out=m8[:], in_=gsb[:]),
                       reads=("gsb",), writes=("m8",))
                S.step("vector", lambda e: e.tensor_scalar_mul(srow[:, 0:1], m8[:, 0:1], -1.0),
                       reads=("m8",), writes=("negm1",))
                S.step("scalar", lambda e: e.activation(esb[:], gsb[:], AF.Exp,
                                                        bias=srow[:, 0:1]),
                       reads=("gsb", "negm1"), writes=("esb",))
                S.step("vector", lambda e: e.tensor_scalar(selsb[:], gsb[:], m8[:, 1:2],
                                                           None, ALU.subtract),
                       reads=("gsb", "m8"), writes=("selsb",))
                S.step("scalar", lambda e: e.activation(selsb[:], selsb[:], AF.Sign),
                       reads=("selsb",), writes=("selsb",))
                S.step("vector", lambda e: e.tensor_scalar(selsb[:], selsb[:], 1.0, 1.0,
                                                           ALU.add, ALU.min),
                       reads=("selsb",), writes=("selsb",))
                S.step("vector", lambda e: e.tensor_mul(esb[:], esb[:], selsb[:]),
                       reads=("esb", "selsb"), writes=("esb",))
                S.step("vector", lambda e: e.reduce_sum(srow[:, 1:2], esb[:], axis=AX.X),
                       reads=("esb",), writes=("essum",))
                S.step("vector", lambda e: e.reciprocal(srow[:, 2:3], srow[:, 1:2]),
                       reads=("essum",), writes=("esrs",))
                S.step("vector", lambda e, c=c: e.tensor_scalar(
                    maskw[:, c * E:(c + 1) * E], esb[:], srow[:, 2:3], MWSCALE,
                    ALU.mult, ALU.mult),
                    reads=("esb", "esrs"), writes=(("maskw", c),))
                if dr.get("xntok") is not None and split in ("count", "cnt1"):
                    xnt = lnb["xn"][c % 2]
                    S.step("sync", lambda e, c=c, xnt=xnt: e.dma_start(
                        dr["xntok"][c * 128:(c + 1) * 128, :], xnt[:]),
                        reads=(("xn", c % 2),), writes=(("xntok", c),), dma=True,
                        fam=f"xntok{c % 2}")
                if do_counts:
                    S.step("vector", lambda e, c=c: e.tensor_copy(
                        sel_a[:, c * E:(c + 1) * E], selsb[:]),
                        reads=("selsb",), writes=(("sel", c),))
                    S.step("tensor", lambda e, c=c: e.matmul(
                        ps_cnt[0:8, c:c + 1], sel_a[:, c * E:(c + 1) * E], onescol[:],
                        start=(c == 0), stop=(c == NT - 1), skip_group_check=True),
                        reads=(("sel", c), "onescol"), writes=(("ps_cnt", c),))
                if not do_counts:
                    S.step("scalar", lambda e, c=c: e.copy(
                        maskw_bf[:], maskw[:, c * E:(c + 1) * E]),
                        reads=(("maskw", c),), writes=("maskw_bf",))
                    S.step("tensor", lambda e: e.transpose(ps_mwt[0:8, 0:128], maskw_bf[:],
                                                           ident_bf[:]),
                           reads=("maskw_bf", "ident_bf"), writes=("ps_mwt",))
                    S.step("scalar", lambda e, c=c: e.copy(maskwT[0:8, c * 128:(c + 1) * 128],
                                                           ps_mwt[0:8, 0:128]),
                           reads=("ps_mwt",), writes=(("maskwT", c),))
            if do_counts:
                cnt_all = tuple(("ps_cnt", c) for c in range(NT))
                S.step("scalar", lambda e: e.copy(cnt_sb[:], ps_cnt[0:8, 0:NT]),
                       reads=cnt_all, writes=("cnt_sb",))
                S.step("vector", lambda e: e.memset(pref_sb[0:8, 0:1], 0.0),
                       writes=("pref0",))
                S.step("vector", lambda e: e.tensor_tensor_scan(
                    pref_sb[0:8, 1:NT + 1], onescol[0:8, 0:1].to_broadcast([8, NT]),
                    cnt_sb[0:8, :], 0.0, ALU.mult, ALU.add),
                    reads=("cnt_sb", "onescol"), writes=("pref_sb",))

        if split == "count":
            mw_all = tuple(("maskw", c) for c in range(NT))
            S.step("sync", lambda e: e.dma_start(mw_out[:, :], maskw[:]),
                   reads=mw_all, dma=True)
            S.step("sync", lambda e: e.dma_start(sel_out[:, :], sel_a[:]),
                   reads=tuple(("sel", c) for c in range(NT)), dma=True)
            S.step("sync", lambda e: e.dma_start(pref_out[:, :], pref_sb[:]),
                   reads=("pref_sb", "pref0"), dma=True)
            return
        if split == "cnt1":
            S.step("sync", lambda e: e.dma_start(pref_out[:, :], pref_sb[:]),
                   reads=("pref_sb", "pref0"), dma=True)
            ident_f32 = sbf("ident_f32", [128, 128], F32)
            S.step("sync", lambda e: e.dma_start(ident_f32[:], c_ident[:, :]),
                   writes=("ident_f32",), dma=True)
            ps_rk = psf("ps_rk", [128, 512], F32)
            nthr_sb = sbf("nthr_sb", [33, 8], F32)
            ltri_sb = sbf("ltri_sb", [128, 128], F32)
            S.step("sync", lambda e: e.dma_start(ltri_sb[:], dr["c_ltri"][:, :]),
                   writes=("ltri_sb",), dma=True)
            S.step("tensor", lambda e: e.transpose(
                ps_rk[0:33, 0:8], pref_sb[0:8, 0:33], ident_f32[0:8, 0:8]),
                reads=("pref_sb", "pref0", "ident_f32"), writes=("ps_rk",))
            S.step("scalar", lambda e: e.copy(nthr_sb[:], ps_rk[0:33, 0:8]),
                   reads=("ps_rk",), writes=("nthr_sb",))
            S.step("sync", lambda e: e.dma_start(dr["prefT"][0:33, :], nthr_sb[:]),
                   reads=("nthr_sb",), writes=("prefh",), dma=True)
            rank_src = {
                "sel_a": sel_a, "ltri_sb": ltri_sb, "ones_r": ones_r,
                "ps_rk": ps_rk,
                "xn_ap_of": lambda c: (dr["xntok"], ("xntok", c), True),
            }
            record_slot_scatter(S, dr, B, layer,
                                lambda c: (maskw[:, c * E:(c + 1) * E], ("maskw", c)),
                                rank_src, dr["gbuf"], dr["prefT"])
            return

        if split == "main":
            sel_a = sbf("sel_a", [128, NT * E], F32)
            ones_r = sbf("ones_r", [1, 128], F32)
            pref_sb = sbf("pref_sb", [8, NT + 1], F32)
            ag_sb = sbf("ag_sb", [8, 8], F32)
            corelt_sb = sbf("corelt_sb", [8, 8], F32)
            cb_row = sbf("cb_row", [1, 8], F32)
            negcap = sbf("negcap", [1, 8], F32)
            nthr_sb = sbf("nthr_sb", [33, 8], F32)
            nstag = [sbf(f"nstag{i}", [1, 8], F32) for i in range(2)]
            keep_sb = sbf("keep_sb", [128, 8], F32)
            ltri_sb = sbf("ltri_sb", [128, 128], F32)
            ident_f32 = sbf("ident_f32", [128, 128], F32)
            ps_rk = psf("ps_rk", [128, 512], F32)
            S.step("vector", lambda e: e.memset(ones_r[:], 1.0), writes=("ones_r",))
            S.step("sync", lambda e: e.dma_start(maskw[:], mw_in[:, :]),
                   writes=tuple(("maskw", c) for c in range(NT)), dma=True)
            S.step("sync", lambda e: e.dma_start(sel_a[:], sel_in[:, :]),
                   writes=tuple(("sel", c) for c in range(NT)), dma=True)
            S.step("sync", lambda e: e.dma_start(pref_sb[:], pref_in[:, :]),
                   writes=("pref_sb", "pref0"), dma=True)
            S.step("sync", lambda e: e.dma_start(ag_sb[:], agcnt[:, :]),
                   writes=("ag_sb",), dma=True)
            S.step("sync", lambda e: e.dma_start(corelt_sb[:], c_corelt[:, :]),
                   writes=("corelt_sb",), dma=True)
            S.step("sync", lambda e: e.dma_start(ltri_sb[:], c_ltri[:, :]),
                   writes=("ltri_sb",), dma=True)
            S.step("sync", lambda e: e.dma_start(ident_f32[:], c_ident[:, :]),
                   writes=("ident_f32",), dma=True)
            S.step("tensor", lambda e: e.matmul(ps_g[0:1, 0:8], corelt_sb[0:8, 0:1],
                                                ag_sb[0:8, :], start=True, stop=True),
                   reads=("corelt_sb", "ag_sb"), writes=("ps_g",))
            S.step("scalar", lambda e: e.copy(cb_row[:], ps_g[0:1, 0:8]),
                   reads=("ps_g",), writes=("cb_row",))
            S.step("vector", lambda e: e.tensor_scalar(negcap[:], cb_row[:], 1.0,
                                                       -CAPTHR, ALU.mult, ALU.add),
                   reads=("cb_row",), writes=("negcap",))

            S.step("tensor", lambda e: e.transpose(
                ps_rk[0:33, 0:8], pref_sb[0:8, 0:33], ident_f32[0:8, 0:8]),
                reads=("pref_sb", "pref0", "ident_f32"), writes=("ps_rk",))
            S.step("scalar", lambda e: e.copy(nthr_sb[:], ps_rk[0:33, 0:8]),
                   reads=("ps_rk",), writes=("nthr_sb",))
            S.step("tensor", lambda e: e.matmul(
                ps_g[0:33, 0:8], ones_r[0:1, 0:33], negcap[:], start=True, stop=True),
                reads=("negcap", "ones_r"), writes=("ps_g",))
            S.step("vector", lambda e: e.tensor_add(
                nthr_sb[:], nthr_sb[:], ps_g[0:33, 0:8]),
                reads=("nthr_sb", "ps_g"), writes=("nthr_sb",))
            S.step("sync", lambda e: e.dma_start(nthr_hbm[0:33, :], nthr_sb[:]),
                   reads=("nthr_sb",), writes=("nthr_hbm",), dma=True)
            for c in range(NT):
                S.step("gpsimd", lambda e, c=c: e.dma_start(
                    nstag[c % 2][:], nthr_hbm[c:c + 1, :]),
                    reads=("nthr_hbm",), writes=(("nstag", c % 2),), dma=True)

                def rkmm(e, c=c):
                    e.matmul(ps_rk[:, 0:8], ltri_sb[:], sel_a[:, c * E:(c + 1) * E],
                             start=True, stop=False)
                    return e.matmul(ps_rk[:, 0:8], ones_r[:], nstag[c % 2][:],
                                    start=False, stop=True)
                S.step("tensor", rkmm, reads=(("sel", c), ("nstag", c % 2),
                                              "ltri_sb", "ones_r"), writes=("ps_rk",))
                S.step("scalar", lambda e: e.activation(keep_sb[:], ps_rk[:, 0:8],
                                                        AF.Sign),
                       reads=("ps_rk",), writes=("keep_sb",))
                S.step("vector", lambda e: e.tensor_scalar(keep_sb[:], keep_sb[:],
                                                           -1.0, 0.0, ALU.mult, ALU.max),
                       reads=("keep_sb",), writes=("keep_sb",))
                S.step("vector", lambda e, c=c: e.tensor_mul(
                    maskw[:, c * E:(c + 1) * E], maskw[:, c * E:(c + 1) * E],
                    keep_sb[:]),
                    reads=(("maskw", c), "keep_sb"), writes=(("maskw", c),))
                if not sparse:
                    S.step("scalar", lambda e, c=c: e.copy(
                        maskw_bf[:], maskw[:, c * E:(c + 1) * E]),
                        reads=(("maskw", c),), writes=("maskw_bf",))
                    S.step("tensor", lambda e: e.transpose(ps_mwt[0:8, 0:128], maskw_bf[:],
                                                           ident_bf[:]),
                           reads=("maskw_bf", "ident_bf"), writes=("ps_mwt",))
                    S.step("scalar", lambda e, c=c: e.copy(
                        maskwT[0:8, c * 128:(c + 1) * 128], ps_mwt[0:8, 0:128]),
                        reads=("ps_mwt",), writes=(("maskwT", c),))
            if sparse:
                rank_src = {
                    "sel_a": sel_a, "ltri_sb": ltri_sb, "ones_r": ones_r,
                    "ps_rk": ps_rk,
                    "xn_ap_of": lambda c: (dr["xntok"], ("xntok_ext",), True),
                }
                record_slot_scatter(
                    S, dr, B, layer,
                    lambda c: (maskw[:, c * E:(c + 1) * E], ("maskw", c)),
                    rank_src, dr["gbuf"], dr["prefT"])
                record_ffn_sparse_main(S, dr, B, layer)
                return

        b1T_sb = sbf(f"b1T_sb{layer}", [128, NFI * E], F32)
        b2_bf = sbf("b2_bf", [8, D], BF16)
        xnG = sbf("xnG", [128, DC * G], BF16)
        h_sb = sbf("h_sb", [128, FCB * G], BF16)
        w1_sb = sbf("w1_sb", [128, DC * FCB * 128], BF16)
        w2_sb = sbf("w2_sb", [128, FCB * D], BF16)
        acc = sbf("acc", [128, GT * D], F32)
        xres = [sbf(f"xres{i}", [128, D], F32) for i in range(2)]
        xo = [sbf(f"xo{i}", [128, D], F32) for i in range(2)]
        ps_h = [psf(f"ps_h{i}", [128, 512], F32) for i in range(2)]
        ps_y = [psf(f"ps_y{i}", [128, 512], F32) for i in range(2)]
        S.step("gpsimd", lambda e: e.dma_start(b2_bf[:], b2[:, :]),
               writes=("b2_bf",), dma=True)
        S.step("sync", lambda e: e.dma_start(b1T_sb[:], b1T[:, :]),
               writes=("b1T_sb",), dma=True)

        mwT_all = tuple(("maskwT", c) for c in range(NT))
        if debug:
            S.step("sync", lambda e: e.dma_start(dbg_mw[:, :], maskw[:]),
                   reads=tuple(("maskw", c) for c in range(NT)), dma=True)
        xnT_rd = (lambda c: ("xnT", c)) if split != "main" else (lambda c: "xnT_ext")
        if split == "main":
            S.step("vector", lambda e: e.engine_nop(), reads=(), writes=("xnT_ext",))

        for g4 in range(NG):
            src = xnT_hbm[:, g4 * G:(g4 + 1) * G].rearrange("(g p) t -> p g t", p=128)
            S.step("sync", lambda e, src=src: e.dma_start(
                xnG[:].rearrange("p (g t) -> p g t", t=G), src),
                reads=tuple(xnT_rd(c) for c in range(g4 * GT, (g4 + 1) * GT)),
                writes=("xnG",), dma=True)
            for g8 in range(GT):
                c = g4 * GT + g8
                for dt in range(2):
                    S.step("tensor", lambda e, c=c, dt=dt: e.matmul(
                        ps_y[dt][:], maskwT[0:8, c * 128:(c + 1) * 128],
                        b2_bf[0:8, dt * 512:(dt + 1) * 512], start=True, stop=True),
                        reads=mwT_all + ("b2_bf",), writes=(f"ps_y{dt}",))
                    S.step("scalar", lambda e, g8=g8, dt=dt: e.copy(
                        acc[:, g8 * D + dt * 512: g8 * D + (dt + 1) * 512], ps_y[dt][:]),
                        reads=(f"ps_y{dt}",), writes=(("acc", g8, dt),))
            for e_i in range(E):
                for fb in range(NFB):
                    src1 = w1[e_i, :, fb * 2048:(fb + 1) * 2048].rearrange(
                        "(dc p) (fc k) -> p dc fc k", p=128, k=128)
                    S.step("gpsimd", lambda e, src1=src1: e.dma_start(
                        w1_sb[:].rearrange("p (dc fc k) -> p dc fc k", fc=FCB, k=128),
                        src1),
                        writes=("w1_sb",), dma=True)
                    src2 = w2[e_i, fb * 2048:(fb + 1) * 2048, :].rearrange(
                        "(fc p) d -> p fc d", p=128)
                    S.step("gpsimd", lambda e, src2=src2: e.dma_start(
                        w2_sb[:].rearrange("p (fc d) -> p fc d", d=D), src2),
                        writes=("w2_sb",), dma=True)
                    for fc in range(FCB):
                        for tb in range(2):
                            def hmm(e, fc=fc, tb=tb):
                                for dc in range(DC):
                                    i = e.matmul(
                                        ps_h[tb][:],
                                        w1_sb[:, (dc * FCB + fc) * 128:
                                              (dc * FCB + fc + 1) * 128],
                                        xnG[:, dc * G + tb * 512: dc * G + (tb + 1) * 512],
                                        start=(dc == 0), stop=(dc == DC - 1))
                                return i
                            S.step("tensor", hmm, reads=("w1_sb", "xnG"),
                                   writes=(f"ps_h{tb}",))
                            bcol = (fb * FCB + fc) * E + e_i
                            S.step("scalar", lambda e, fc=fc, tb=tb, bcol=bcol:
                                   e.activation(
                                       h_sb[:, fc * G + tb * 512: fc * G + (tb + 1) * 512],
                                       ps_h[tb][:], ACTF,
                                       bias=b1T_sb[:, bcol:bcol + 1]),
                                   reads=(f"ps_h{tb}", "b1T_sb"),
                                   writes=(("h", fc, tb),))
                    h_all = tuple(("h", fc, tb) for fc in range(FCB) for tb in range(2))
                    for g8 in range(GT):
                        for dt in range(2):
                            def ymm(e, g8=g8, dt=dt):
                                for fc in range(FCB):
                                    i = e.matmul(
                                        ps_y[dt][:],
                                        h_sb[:, fc * G + g8 * 128: fc * G + g8 * 128 + 128],
                                        w2_sb[:, fc * D + dt * 512: fc * D + (dt + 1) * 512],
                                        start=(fc == 0), stop=(fc == FCB - 1))
                                return i
                            S.step("tensor", ymm, reads=h_all + ("w2_sb",),
                                   writes=(f"ps_y{dt}",))
                            mcol = (g4 * GT + g8) * E + e_i
                            S.step("vector", lambda e, g8=g8, dt=dt, mcol=mcol:
                                   e.scalar_tensor_tensor(
                                       acc[:, g8 * D + dt * 512: g8 * D + (dt + 1) * 512],
                                       ps_y[dt][:], maskw[:, mcol:mcol + 1],
                                       acc[:, g8 * D + dt * 512: g8 * D + (dt + 1) * 512],
                                       ALU.mult, ALU.add),
                                   reads=(f"ps_y{dt}", ("maskw", g4 * GT + g8),
                                          ("acc", g8, dt)),
                                   writes=(("acc", g8, dt),))
            if debug and g4 == 0:
                S.step("sync", lambda e: e.dma_start(dbg_acc[:, :], acc[:]),
                       reads=tuple(("acc", g8, dt) for g8 in range(GT) for dt in range(2)),
                       dma=True)
            for g8 in range(GT):
                c = g4 * GT + g8
                S.step("sync", lambda e, c=c: e.dma_start(
                    xres[c % 2][:], x_in[c * 128:(c + 1) * 128, :]),
                    reads=(("x_dram", c),), writes=(("xres", c % 2),), dma=True)
                S.step("vector", lambda e, c=c, g8=g8: e.tensor_add(
                    xo[c % 2][:], xres[c % 2][:], acc[:, g8 * D:(g8 + 1) * D]),
                    reads=(("xres", c % 2), ("acc", g8, 0), ("acc", g8, 1)),
                    writes=(("xo", c % 2),))
                S.step("sync", lambda e, c=c: e.dma_start(
                    x_out[c * 128:(c + 1) * 128, :], xo[c % 2][:]),
                    reads=(("xo", c % 2),), writes=(("xout", c),), dma=True, fam=f"xout{c % 2}")







def record_l1a(S, dr, B, debug=False):
    assert not debug
    TC = 512
    NCH = T // TC
    W5 = TC + 1

    nc = S.nc
    sbf = B.buf
    psf = B.psum
    c_ident = dr["c_ident"]
    x_in = dr["x_in"]
    x_out = dr["x_out"]
    wkT = dr["wkT"]; wvT = dr["wvT"]; wrT = dr["wrT"]; woT = dr["woT"]
    tmk5 = dr["tmk5"]; tmv5 = dr["tmv5"]; tmr5 = dr["tmr5"]
    dec5 = dr["dec5"]; fst5 = dr["fst5"]
    if True:

        lnb = alloc_ln_bufs(sbf, psf)
        S.step("gpsimd", lambda e: e.dma_start(lnb["ident_bf"][:], c_ident[:, :]),
               writes=("ident_bf",), dma=True)
        xnTa = sbf("xnTa", [128, DC * W5], BF16)

        def evac_dsts(c):
            ti = c % 4
            dsts = [xnTa[:, g * W5 + 1 + ti * 128: g * W5 + 1 + (ti + 1) * 128]
                    for g in range(DC)]
            return dsts, ("xnTa", c // 4)
        lnb["evac_dsts"] = evac_dsts
        lnb["evac_extra_reads"] = ("halo",)

        wT_sb = {}
        for nm, src in (("k", wkT), ("v", wvT), ("r", wrT), ("o", woT)):
            wT_sb[nm] = sbf(f"w{nm}T_sb", [128, DC * D], BF16)
            S.step("gpsimd", lambda e, nm=nm, src=src: e.dma_start(
                wT_sb[nm][:].rearrange("p (g e) -> p g e", e=D),
                src[:, :].rearrange("(g p) e -> p g e", p=128)),
                writes=(f"w{nm}T",), dma=True)
        tm_sb = {}
        for nm, src in (("k", tmk5), ("v", tmv5), ("r", tmr5), ("d", dec5), ("u", fst5)):
            tm_sb[nm] = sbf(f"tm{nm}_sb", [128, DC], F32)
            S.step("sync", lambda e, nm=nm, src=src: e.dma_start(tm_sb[nm][:], src[:, :]),
                   writes=(f"tm{nm}",), dma=True)
        ew = sbf("ew", [128, DC], F32)
        S.step("scalar", lambda e: e.activation(ew[:], tm_sb["d"][:], AF.Exp),
               reads=("tmd",), writes=("ew",))
        S.step("vector", lambda e: e.memset(lnb["epsc"][:], 1e-5), writes=("lneps",))

        Ast = sbf("Ast", [128, DC], F32)
        Pst = sbf("Pst", [128, DC], F32)
        S.step("vector", lambda e: e.memset(Ast[:], 0.0), writes=("Ast",))
        S.step("vector", lambda e: e.memset(Pst[:], 0.0), writes=("Pst",))
        S.step("vector", lambda e: e.memset(
            xnTa[:].rearrange("p (g c) -> p g c", c=W5)[:, :, 0:1], 0.0),
            writes=("halo",))

        dbuf = sbf("dbuf", [128, TC], BF16)
        xk_a = sbf("xk_a", [128, DC * TC], BF16)
        xv_a = sbf("xv_a", [128, DC * TC], BF16)
        xr_a = sbf("xr_a", [128, DC * TC], BF16)
        E1 = sbf("E1", [128, TC], F32)
        K1 = sbf("K1", [128, TC], F32)
        EV = sbf("EV", [128, TC], F32)
        KV = sbf("KV", [128, TC], F32)
        Abuf = sbf("Abuf", [128, W5], F32)
        Pbuf = sbf("Pbuf", [128, W5], F32)
        dnb = sbf("dnb", [128, TC], F32)
        recb = sbf("recb", [128, TC], F32)
        wkvb = sbf("wkvb", [128, TC], F32)
        r_a = sbf("r_a", [128, DC * TC], BF16)
        y_a = sbf("y_a", [128, DC * TC], BF16)
        xres = [sbf(f"xres{i}", [128, D], F32) for i in range(2)]
        xo = [sbf(f"xo{i}", [128, D], F32) for i in range(2)]

        ps_k = psf("ps_k", [128, 512], F32)
        ps_v = psf("ps_v", [128, 512], F32)
        ps_r = psf("ps_r", [128, 512], F32)
        ps_o = [psf(f"ps_o{i}", [128, 512], F32) for i in range(2)]

        for q in range(NCH):
            for ti in range(4):
                record_ln_tile(S, q * 4 + ti, lnb, x_in)
            tkx = ("xnTa", q)
            for dc in range(DC):
                sl = xnTa[:, dc * W5: (dc + 1) * W5]
                S.step("vector", lambda e, sl=sl: e.tensor_sub(
                    dbuf[:], sl[:, 1:W5], sl[:, 0:TC]),
                    reads=(tkx, "halo"), writes=("dbuf",))
                for nm, arr in (("k", xk_a), ("v", xv_a), ("r", xr_a)):
                    S.step("vector", lambda e, sl=sl, nm=nm, arr=arr, dc=dc:
                           e.scalar_tensor_tensor(
                               arr[:, dc * TC:(dc + 1) * TC], dbuf[:],
                               tm_sb[nm][:, dc:dc + 1], sl[:, 0:TC],
                               ALU.mult, ALU.add),
                           reads=("dbuf", f"tm{nm}", tkx, "halo"),
                           writes=((f"x{nm}", dc),))
            if q < NCH - 1:
                def haloc(e):
                    for dc in range(DC):
                        i = e.tensor_copy(xnTa[:, dc * W5: dc * W5 + 1],
                                          xnTa[:, dc * W5 + TC: dc * W5 + TC + 1])
                    return i
                S.step("vector", haloc, reads=(tkx,), writes=("halo",))
            xk_all = tuple(("xk", dc) for dc in range(DC))
            xv_all = tuple(("xv", dc) for dc in range(DC))
            xr_all = tuple(("xr", dc) for dc in range(DC))
            for ec in range(DC):
                def kmm(e, ec=ec, arr=xk_a, w="k", ps=ps_k):
                    for dc in range(DC):
                        i = e.matmul(ps[:], wT_sb[w][:, dc * D + ec * 128:
                                                     dc * D + (ec + 1) * 128],
                                     arr[:, dc * TC:(dc + 1) * TC],
                                     start=(dc == 0), stop=(dc == DC - 1))
                    return i
                S.step("tensor", lambda e, ec=ec: kmm(e, ec, xk_a, "k", ps_k),
                       reads=xk_all + ("wkT",), writes=("ps_k",))
                S.step("tensor", lambda e, ec=ec: kmm(e, ec, xv_a, "v", ps_v),
                       reads=xv_all + ("wvT",), writes=("ps_v",))
                S.step("tensor", lambda e, ec=ec: kmm(e, ec, xr_a, "r", ps_r),
                       reads=xr_all + ("wrT",), writes=("ps_r",))
                S.step("scalar", lambda e, ec=ec: e.activation(
                    E1[:], ps_k[:], AF.Exp, bias=tm_sb["u"][:, ec:ec + 1]),
                    reads=("ps_k", "tmu"), writes=("E1",))
                S.step("scalar", lambda e: e.activation(K1[:], ps_k[:], AF.Exp),
                       reads=("ps_k",), writes=("K1",))
                S.step("vector", lambda e: e.tensor_mul(EV[:], E1[:], ps_v[:]),
                       reads=("E1", "ps_v"), writes=("EV",))
                S.step("vector", lambda e: e.tensor_mul(KV[:], K1[:], ps_v[:]),
                       reads=("K1", "ps_v"), writes=("KV",))
                S.step("scalar", lambda e, ec=ec: e.activation(
                    r_a[:, ec * TC:(ec + 1) * TC], ps_r[:], AF.Sigmoid),
                    reads=("ps_r",), writes=(("r", ec),))
                S.step("vector", lambda e, ec=ec: e.tensor_copy(
                    Abuf[:, 0:1], Ast[:, ec:ec + 1]), reads=("Ast",), writes=("Abuf0",))
                S.step("vector", lambda e, ec=ec: e.tensor_tensor_scan(
                    Abuf[:, 1:W5], ew[:, ec:ec + 1].to_broadcast([128, TC]), KV[:],
                    Ast[:, ec:ec + 1], ALU.mult, ALU.add),
                    reads=("KV", "ew", "Ast"), writes=("Abuf",))
                S.step("vector", lambda e, ec=ec: e.tensor_copy(
                    Ast[:, ec:ec + 1], Abuf[:, TC:W5]),
                    reads=("Abuf",), writes=("Ast",))
                S.step("vector", lambda e, ec=ec: e.tensor_copy(
                    Pbuf[:, 0:1], Pst[:, ec:ec + 1]), reads=("Pst",), writes=("Pbuf0",))
                S.step("vector", lambda e, ec=ec: e.tensor_tensor_scan(
                    Pbuf[:, 1:W5], ew[:, ec:ec + 1].to_broadcast([128, TC]), K1[:],
                    Pst[:, ec:ec + 1], ALU.mult, ALU.max),
                    reads=("K1", "ew", "Pst"), writes=("Pbuf",))
                S.step("vector", lambda e, ec=ec: e.tensor_copy(
                    Pst[:, ec:ec + 1], Pbuf[:, TC:W5]),
                    reads=("Pbuf",), writes=("Pst",))
                S.step("vector", lambda e: e.tensor_add(dnb[:], Pbuf[:, 0:TC], E1[:]),
                       reads=("Pbuf", "Pbuf0", "E1"), writes=("dnb",))
                S.step("vector", lambda e: e.reciprocal(recb[:], dnb[:]),
                       reads=("dnb",), writes=("recb",))
                S.step("vector", lambda e: e.tensor_add(EV[:], Abuf[:, 0:TC], EV[:]),
                       reads=("Abuf", "Abuf0", "EV"), writes=("EV",))
                S.step("vector", lambda e: e.tensor_mul(wkvb[:], EV[:], recb[:]),
                       reads=("EV", "recb"), writes=("wkvb",))
                S.step("vector", lambda e, ec=ec: e.tensor_mul(
                    y_a[:, ec * TC:(ec + 1) * TC], wkvb[:],
                    r_a[:, ec * TC:(ec + 1) * TC]),
                    reads=("wkvb", ("r", ec)), writes=(("y", ec),))
                if debug and q == 0 and ec == 0:
                    S.step("gpsimd", lambda e: e.dma_start(
                        dbg_ry[:, 0:512], r_a[:, 0:512]), reads=(("r", 0),), dma=True)
                    S.step("gpsimd", lambda e: e.dma_start(
                        dbg_ry[:, 512:1024], y_a[:, 0:512]), reads=(("y", 0),), dma=True)
                    for j, (nm, t) in enumerate(
                            (("E1", E1), ("K1", K1), ("EV", EV), ("KV", KV),
                             ("wkv", wkvb), ("dn", dnb))):
                        S.step("sync", lambda e, j=j, t=t: e.dma_start(
                            dbg_kv[:, j * 512:(j + 1) * 512], t[:]),
                            reads=(nm if nm in ("E1", "K1", "KV", "dn") else
                                   ("EV" if nm == "EV" else "wkvb"),), dma=True)
            y_all = tuple(("y", ec) for ec in range(DC))
            if debug and q == 0:
                S.step("gpsimd", lambda e: e.dma_start(dbg_ya[:, :], y_a[:]),
                       reads=y_all, dma=True)
            for ti in range(4):
                c = q * 4 + ti
                S.step("sync", lambda e, c=c, ti=ti: e.dma_start(
                    xres[ti % 2][:], x_in[c * 128:(c + 1) * 128, :]),
                    reads=(("x_dram", c),), writes=(("xres", ti % 2),), dma=True)
                for eh in range(2):
                    def omm(e, ti=ti, eh=eh):
                        for dc in range(DC):
                            i = e.matmul(ps_o[eh][:],
                                         y_a[:, dc * TC + ti * 128: dc * TC + (ti + 1) * 128],
                                         wT_sb["o"][:, dc * D + eh * 512:
                                                    dc * D + (eh + 1) * 512],
                                         start=(dc == 0), stop=(dc == DC - 1))
                        return i
                    S.step("tensor", omm, reads=y_all + ("woT",),
                           writes=(f"ps_o{eh}",))
                    S.step("vector", lambda e, ti=ti, eh=eh: e.tensor_add(
                        xo[ti % 2][:, eh * 512:(eh + 1) * 512],
                        xres[ti % 2][:, eh * 512:(eh + 1) * 512], ps_o[eh][:]),
                        reads=(f"ps_o{eh}", ("xres", ti % 2)),
                        writes=(("xo", ti % 2, eh),))
                S.step("sync", lambda e, c=c, ti=ti: e.dma_start(
                    x_out[c * 128:(c + 1) * 128, :], xo[ti % 2][:]),
                    reads=(("xo", ti % 2, 0), ("xo", ti % 2, 1)),
                    writes=(("xout", c),), dma=True, fam=f"xout{ti % 2}")
                if debug and q == 0 and ti == 0:
                    S.step("sync", lambda e: e.dma_start(dbg_xr[:, 0:D], xres[0][:]),
                           reads=(("xres", 0),), dma=True)
                    S.step("sync", lambda e: e.dma_start(dbg_xr[:, D:2*D], xo[0][:]),
                           reads=(("xo", 0, 0), ("xo", 0, 1)), dma=True)



def _l1a_inputs(x_shard, inputs, core_id):
    cons = host_consts(core_id)
    def five(v):
        return np.ascontiguousarray(
            np.asarray(v, np.float32).reshape(DC, 128).T)
    return {
        "x_in": x_shard,
        "wkT": np.ascontiguousarray(np.asarray(inputs["L1_attn_W_k"], np.float32).T),
        "wvT": np.ascontiguousarray(np.asarray(inputs["L1_attn_W_v"], np.float32).T),
        "wrT": np.ascontiguousarray(np.asarray(inputs["L1_attn_W_r"], np.float32).T),
        "woT": np.ascontiguousarray(np.asarray(inputs["L1_attn_W_o"], np.float32).T),
        "tmk5": five(inputs["L1_attn_time_mix_k"]),
        "tmv5": five(inputs["L1_attn_time_mix_v"]),
        "tmr5": five(inputs["L1_attn_time_mix_r"]),
        "dec5": five(inputs["L1_attn_time_decay"]),
        "fst5": five(inputs["L1_attn_time_first"]),
        "c_ident": cons["c_ident"],
    }




def record_l0a(S, dr, B, debug=False):
    assert not debug
    TC = 512
    NCH = T // TC
    WA = TC + 4
    NH = 8
    SCALE = D ** -0.5

    nc = S.nc
    sbf = B.buf
    psf = B.psum
    c_ident = dr["c_ident"]
    c_bmask = dr["c_bmask"]
    x_in = dr["x_in"]
    x_out = dr["x_out"]
    wqkv = dr["wqkv"]; wout = dr["wout"]; bout = dr["bout"]
    xnT_hbm = dr["xnT"]
    yB_hbm = dr["yB"]
    if True:

        lnb = alloc_ln_bufs(sbf, psf)
        S.step("gpsimd", lambda e: e.dma_start(lnb["ident_bf"][:], c_ident[:, :]),
               writes=("ident_bf",), dma=True)
        S.step("vector", lambda e: e.memset(lnb["epsc"][:], 1e-5), writes=("lneps",))

        wq_sb = sbf("wq_sb", [128, DC * 3 * D], BF16)
        wo_sb = sbf("wo_sb", [128, DC * D], BF16)
        bout_sb = sbf("bout_sb", [1, D], F32)
        ones_r = sbf("ones_r", [1, 128], F32)
        bm_sb = sbf("bm_sb", [128, 128], F32)
        S.step("gpsimd", lambda e: e.dma_start(
            wq_sb[:].rearrange("p (g f) -> p g f", f=3 * D),
            wqkv[:, :].rearrange("(g p) f -> p g f", p=128)),
            writes=("wq_sb",), dma=True)
        S.step("gpsimd", lambda e: e.dma_start(
            wo_sb[:].rearrange("p (g f) -> p g f", f=D),
            wout[:, :].rearrange("(g p) f -> p g f", p=128)),
            writes=("wo_sb",), dma=True)
        S.step("sync", lambda e: e.dma_start(bout_sb[:], bout[:, :]),
               writes=("bout_sb",), dma=True)
        S.step("sync", lambda e: e.dma_start(bm_sb[:], c_bmask[:, :]),
               writes=("bm_sb",), dma=True)
        S.step("vector", lambda e: e.memset(ones_r[:], 1.0), writes=("ones_r",))

        xnTa = sbf("xnTa", [128, DC * WA], BF16)
        qT = sbf("qT", [128, NH * WA], BF16)
        kT = sbf("kT", [128, NH * WA], BF16)
        vA = sbf("vA", [128, 4 * D], BF16)
        vB = sbf("vB", [128, 4 * D], BF16)
        Pf = sbf("Pf", [128, NH * 128], F32)
        nmax = sbf("nmax", [128, NH], F32)
        dsum = sbf("dsum", [128, NH], F32)
        rden = sbf("rden", [128, NH], F32)
        Pb = sbf("Pb", [128, NH * 128], BF16)
        PTb = sbf("PTb", [128, NH * 128], BF16)
        OT = sbf("OT", [128, NH * 128], BF16)
        ybst = sbf("ybst", [128, D], F32)
        xres = [sbf(f"xres{i}", [128, D], F32) for i in range(2)]
        xo = [sbf(f"xo{i}", [128, D], F32) for i in range(2)]
        yb3 = [sbf(f"yb3_{i}", [128, D], F32) for i in range(2)]

        ps_s = [psf(f"ps_s{i}", [128, 512], F32) for i in range(2)]
        ps_pt = psf("ps_pt", [128, 1024], BF16)
        ps_av = [psf(f"ps_av{i}", [128, 512], F32) for i in range(2)]
        ps_y = [psf(f"ps_y{i}", [128, 512], F32) for i in range(2)]

        for c in range(NT):
            tk_st = record_ln_tile(S, c, lnb, x_in)
            stage = lnb["stage"][c % 2]
            dst = xnT_hbm[:, c * 128:(c + 1) * 128].rearrange("(g p) k -> p g k", p=128)
            S.step("sync", lambda e, dst=dst, stage=stage: e.dma_start(
                dst, stage[:].rearrange("p (g k) -> p g k", k=128)),
                reads=(tk_st,), writes=(("xnT", c),), dma=True, fam=f"xnT{c % 2}")

        S.step("vector", lambda e: e.memset(ybst[:], 0.0), writes=("ybst",))
        S.step("sync", lambda e: e.dma_start(yB_hbm[0:2, :], ybst[0:2, :]),
               reads=("ybst",), writes=("yB",), dma=True)
        S.step("sync", lambda e: e.dma_start(yB_hbm[T - 2:T, :], ybst[0:2, :]),
               reads=("ybst",), writes=("yB",), dma=True)

        for q in range(NCH):
            t0 = q * TC
            w_load = TC + 4 if q < NCH - 1 else TC
            src = xnT_hbm[:, t0:t0 + w_load].rearrange("(g p) t -> p g t", p=128)
            dstv = xnTa[:].rearrange("p (g j) -> p g j", j=WA)[:, :, 0:w_load]
            S.step("sync", lambda e, src=src, dstv=dstv: e.dma_start(dstv, src),
                   reads=tuple(("xnT", c) for c in range(q * 4, min(NT, q * 4 + 5))),
                   writes=("xnTa",), dma=True)
            if q == NCH - 1:
                S.step("vector", lambda e: e.memset(
                    xnTa[:].rearrange("p (g j) -> p g j", j=WA)[:, :, TC:WA], 0.0),
                    reads=(), writes=("xnTa_pad",))
            else:
                S.step("vector", lambda e: e.engine_nop(), writes=("xnTa_pad",))

            for h in range(NH):
                for nm, arr, fof in (("q", qT, 0), ("k", kT, D)):
                    def qkmm(e, h=h, arr=arr, fof=fof, main=True):
                        lo, hi = (0, TC) if main else (TC, WA)
                        for dc in range(DC):
                            i = e.matmul(
                                ps_s[0][:, 0:hi - lo],
                                xnTa[:, dc * WA + lo: dc * WA + hi],
                                wq_sb[:, dc * 3 * D + fof + h * 128:
                                      dc * 3 * D + fof + (h + 1) * 128],
                                start=(dc == 0), stop=(dc == DC - 1))
                        return i
                    def qkmmT(e, h=h, fof=fof, lo=0, hi=TC):
                        for dc in range(DC):
                            i = e.matmul(
                                ps_s[0][:, 0:hi - lo],
                                wq_sb[:, dc * 3 * D + fof + h * 128:
                                      dc * 3 * D + fof + (h + 1) * 128],
                                xnTa[:, dc * WA + lo: dc * WA + hi],
                                start=(dc == 0), stop=(dc == DC - 1))
                        return i
                    S.step("tensor", lambda e, f=qkmmT: f(e),
                           reads=("xnTa", "xnTa_pad", "wq_sb"), writes=("ps_s0",))
                    S.step("scalar", lambda e, h=h, arr=arr: e.copy(
                        arr[:, h * WA: h * WA + TC], ps_s[0][:, 0:TC]),
                        reads=("ps_s0",), writes=((nm + "T", h),))
                    S.step("tensor", lambda e, f=qkmmT: f(e, lo=TC, hi=WA),
                           reads=("xnTa", "xnTa_pad", "wq_sb"), writes=("ps_s0",))
                    S.step("scalar", lambda e, h=h, arr=arr: e.copy(
                        arr[:, h * WA + TC: h * WA + WA], ps_s[0][:, 0:WA - TC]),
                        reads=("ps_s0",), writes=((nm + "T", h),))
            qk_all = tuple(("qT", h) for h in range(NH)) + tuple(("kT", h) for h in range(NH))

            for br, varr, off in (("A", vA, 0), ("B", vB, 2)):
                for ti in range(4):
                    for vh in range(2):
                        def vmm(e, ti=ti, vh=vh, off=off):
                            for dc in range(DC):
                                i = e.matmul(
                                    ps_av[vh][:],
                                    xnTa[:, dc * WA + off + ti * 128:
                                          dc * WA + off + (ti + 1) * 128],
                                    wq_sb[:, dc * 3 * D + 2 * D + vh * 512:
                                          dc * 3 * D + 2 * D + (vh + 1) * 512],
                                    start=(dc == 0), stop=(dc == DC - 1))
                            return i
                        S.step("tensor", vmm, reads=("xnTa", "xnTa_pad", "wq_sb"),
                               writes=(f"ps_av{vh}",))
                        S.step("scalar", lambda e, ti=ti, vh=vh, varr=varr: e.copy(
                            varr[:, ti * D + vh * 512: ti * D + (vh + 1) * 512],
                            ps_av[vh][:]),
                            reads=(f"ps_av{vh}",), writes=((f"v{br}", ti),))

            for br, varr, off in (("A", vA, 0), ("B", vB, 2)):
                for ti in range(4):
                    if br == "B" and q == NCH - 1 and ti == 3:
                        nrows = 124
                    else:
                        nrows = 128
                    for hb in range(2):
                        def smm(e, hb=hb, ti=ti, off=off):
                            for hh in range(4):
                                h = hb * 4 + hh
                                i = e.matmul(
                                    ps_s[hb][:, hh * 128:(hh + 1) * 128],
                                    qT[:, h * WA + off + ti * 128:
                                       h * WA + off + (ti + 1) * 128],
                                    kT[:, h * WA + off + ti * 128:
                                       h * WA + off + (ti + 1) * 128],
                                    start=(hh == 0), stop=(hh == 3))
                            return i
                        S.step("tensor", smm, reads=qk_all, writes=(f"ps_s{hb}",))
                        S.step("vector", lambda e, hb=hb: e.reduce_max(
                            nmax[:, hb * 4:(hb + 1) * 4],
                            ps_s[hb][:].rearrange("p (hh t) -> p hh t", t=128),
                            axis=AX.X, negate=True),
                            reads=(f"ps_s{hb}",), writes=(("nmax", hb),))
                        S.step("vector", lambda e, hb=hb: e.tensor_scalar_mul(
                            nmax[:, hb * 4:(hb + 1) * 4],
                            nmax[:, hb * 4:(hb + 1) * 4], SCALE),
                            reads=(("nmax", hb),), writes=(("nmax", hb),))
                        for hh in range(4):
                            h = hb * 4 + hh
                            S.step("scalar", lambda e, hb=hb, hh=hh, h=h: e.activation(
                                Pf[:, h * 128:(h + 1) * 128],
                                ps_s[hb][:, hh * 128:(hh + 1) * 128], AF.Exp,
                                bias=nmax[:, h:h + 1], scale=SCALE),
                                reads=(f"ps_s{hb}", ("nmax", hb)), writes=(("Pf", h),))
                    for h in range(NH):
                        S.step("vector", lambda e, h=h: e.tensor_mul(
                            Pf[:, h * 128:(h + 1) * 128], Pf[:, h * 128:(h + 1) * 128],
                            bm_sb[:]),
                            reads=(("Pf", h), "bm_sb"), writes=(("Pf", h),))
                    S.step("vector", lambda e: e.reduce_sum(
                        dsum[:], Pf[:].rearrange("p (h t) -> p h t", t=128), axis=AX.X),
                        reads=tuple(("Pf", h) for h in range(NH)), writes=("dsum",))
                    S.step("vector", lambda e: e.reciprocal(rden[:], dsum[:]),
                           reads=("dsum",), writes=("rden",))
                    for h in range(NH):
                        S.step("vector", lambda e, h=h: e.tensor_scalar_mul(
                            Pb[:, h * 128:(h + 1) * 128], Pf[:, h * 128:(h + 1) * 128],
                            rden[:, h:h + 1]),
                            reads=(("Pf", h), "rden"), writes=(("Pb", h),))
                        S.step("tensor", lambda e, h=h: e.transpose(
                            ps_pt[:, h * 128:(h + 1) * 128],
                            Pb[:, h * 128:(h + 1) * 128], lnb["ident_bf"][:]),
                            reads=(("Pb", h), "ident_bf"), writes=("ps_pt",))
                        S.step("scalar", lambda e, h=h: e.copy(
                            PTb[:, h * 128:(h + 1) * 128],
                            ps_pt[:, h * 128:(h + 1) * 128]),
                            reads=("ps_pt",), writes=(("PTb", h),))
                    for hb in range(2):
                        def avmm(e, hb=hb, ti=ti, varr=varr):
                            for hh in range(4):
                                h = hb * 4 + hh
                                i = e.matmul(
                                    ps_av[hb][:, hh * 128:(hh + 1) * 128],
                                    varr[:, ti * D + h * 128: ti * D + (h + 1) * 128],
                                    PTb[:, h * 128:(h + 1) * 128],
                                    start=(hh == 0), stop=(hh == 3))
                            return i
                        S.step("tensor", avmm,
                               reads=tuple(("PTb", hb * 4 + hh) for hh in range(4))
                               + ((f"v{br}", ti),),
                               writes=(f"ps_av{hb}",))
                        S.step("scalar", lambda e, hb=hb: e.copy(
                            OT[:, hb * 512:(hb + 1) * 512], ps_av[hb][:]),
                            reads=(f"ps_av{hb}",), writes=(("OT", hb),))
                    for eh in range(2):
                        def ymm(e, eh=eh):
                            for h in range(NH):
                                e.matmul(
                                    ps_y[eh][:], OT[:, h * 128:(h + 1) * 128],
                                    wo_sb[:, h * D + eh * 512: h * D + (eh + 1) * 512],
                                    start=(h == 0), stop=False)
                            return e.matmul(ps_y[eh][:], ones_r[:],
                                            bout_sb[:, eh * 512:(eh + 1) * 512],
                                            start=False, stop=True)
                        S.step("tensor", ymm,
                               reads=(("OT", 0), ("OT", 1), "wo_sb", "ones_r",
                                      "bout_sb"),
                               writes=(f"ps_y{eh}",))
                    c = q * 4 + ti
                    if br == "A":
                        S.step("sync", lambda e, c=c, ti=ti: e.dma_start(
                            xres[ti % 2][:], x_in[c * 128:(c + 1) * 128, :]),
                            reads=(("x_dram", c),), writes=(("xres", ti % 2),),
                            dma=True)
                        for eh in range(2):
                            S.step("vector", lambda e, ti=ti, eh=eh: e.tensor_add(
                                xo[ti % 2][:, eh * 512:(eh + 1) * 512],
                                xres[ti % 2][:, eh * 512:(eh + 1) * 512],
                                ps_y[eh][:]),
                                reads=(f"ps_y{eh}", ("xres", ti % 2)),
                                writes=(("xo", ti % 2, eh),))
                        S.step("sync", lambda e, c=c, ti=ti: e.dma_start(
                            x_out[c * 128:(c + 1) * 128, :], xo[ti % 2][:]),
                            reads=(("xo", ti % 2, 0), ("xo", ti % 2, 1)),
                            writes=(("xout1", c),), dma=True, fam=f"xout{ti % 2}")
                    else:
                        for eh in range(2):
                            S.step("scalar", lambda e, eh=eh: e.copy(
                                ybst[:, eh * 512:(eh + 1) * 512], ps_y[eh][:]),
                                reads=(f"ps_y{eh}",), writes=("ybst",))
                        r0 = t0 + 2 + ti * 128
                        S.step("sync", lambda e, r0=r0, nrows=nrows: e.dma_start(
                            yB_hbm[r0:r0 + nrows, :], ybst[0:nrows, :]),
                            reads=("ybst",), writes=("yB",), dma=True)

        for c in range(NT):
            S.step("sync", lambda e, c=c: e.dma_start(
                xres[c % 2][:], x_out[c * 128:(c + 1) * 128, :]),
                reads=(("xout1", c),), writes=(("xres", c % 2),), dma=True)
            S.step("sync", lambda e, c=c: e.dma_start(
                yb3[c % 2][:], yB_hbm[c * 128:(c + 1) * 128, :]),
                reads=("yB",), writes=(("yb3", c % 2),), dma=True)
            S.step("vector", lambda e, c=c: e.tensor_add(
                xo[c % 2][:], xres[c % 2][:], yb3[c % 2][:]),
                reads=(("xres", c % 2), ("yb3", c % 2)), writes=(("xo", c % 2, 0),))
            S.step("sync", lambda e, c=c: e.dma_start(
                x_out[c * 128:(c + 1) * 128, :], xo[c % 2][:]),
                reads=(("xo", c % 2, 0),), writes=(("xout2", c),), dma=True, fam=f"xoutP3{c % 2}")



def _l0a_inputs(x_shard, inputs, core_id):
    cons = host_consts(core_id)
    return {
        "x_in": x_shard,
        "wqkv": np.ascontiguousarray(inputs["L0_attn_w_qkv"], np.float32),
        "wout": np.ascontiguousarray(inputs["L0_attn_w_out"], np.float32),
        "bout": np.ascontiguousarray(
            np.asarray(inputs["L0_attn_b_out"], np.float32)).reshape(1, D),
        "c_ident": cons["c_ident"],
        "c_bmask": cons["c_bmask"],
    }



def host_consts_sparse():
    iota8 = np.broadcast_to(np.arange(8, dtype=np.float32), (128, 8)).copy()
    tok = np.arange(T, dtype=np.int64)
    hi = (tok // 64).astype(np.float32)
    lo = (tok % 64).astype(np.float32)
    tokhl = np.zeros((128, 2 * NT), np.float32)
    for c in range(NT):
        tokhl[:, 2 * c] = hi[c * 128:(c + 1) * 128]
        tokhl[:, 2 * c + 1] = lo[c * 128:(c + 1) * 128]
    return {"c_iota8": iota8,
            "c_tokhl": tokhl.astype(BFNP)}


def _sparse_extra_dr(decl, dr, gbuf_name, gbuf_kind, x_res):
    dr["c_iota8"] = decl("c_iota8", [128, 8], F32, "ExternalInput")
    dr["c_tokhl"] = decl("c_tokhl", [128, 2 * NT], BF16, "ExternalInput")
    dr["gbuf"] = decl(gbuf_name, [E * CAP, GW], BF16, gbuf_kind)
    dr["x_res"] = x_res
    return dr


def _xcopy(S, src_t, dst_t, wtok=("xsc",)):
    for i in range(4):
        S.step("sync", lambda e, i=i: e.dma_start(
            dst_t[i * 1024:(i + 1) * 1024, :], src_t[i * 1024:(i + 1) * 1024, :]),
            reads=(("xcp_src", i),), writes=wtok + ((f"xcp{i}",),), dma=True,
            fam="xcopy")


def build_sa(serialize=False):
    def b(S, B, decl):
        x_in = decl("x_in", [T, D], F32, "ExternalInput")
        B.kind = "ffn"
        S.prefix = "a0"
        dr = _ffn_dr(decl, 0, "count", x_in, None)
        dr["xntok"] = decl("xntok_out", [T, D], BF16, "ExternalOutput")
        record_ffn(S, dr, B, 0, split="count")
    return _mk_composite(b, serialize)


def build_sb(serialize=False):
    def b(S, B, decl):
        x_in = decl("x_in", [T, D], F32, "ExternalInput")
        x_out = decl("x_out", [T, D], F32, "ExternalOutput")
        x_scr = decl("x_scr", [T, D], F32)
        B.kind = "ffn"
        S.prefix = "b0"
        _xcopy(S, x_in, x_scr)
        dr = _ffn_dr(decl, 0, "main", x_in, None)
        dr["xntok"] = decl("xntok_in", [T, D], BF16, "ExternalInput")
        dr["prefT"] = decl("prefT_in", [NT + 1, E], F32, "ExternalInput")
        _sparse_extra_dr(decl, dr, "gbuf_scr", "ExternalOutput", x_scr)
        record_ffn(S, dr, B, 0, split="main", sparse=True)
        S.barrier()
        B.kind = "l0a"
        S.prefix = "b1"
        record_l0a(S, _l0a_dr(decl, x_scr, x_out), B)
        S.barrier()
        B.kind = "ffn"
        S.prefix = "b2"
        dr2 = _ffn_dr(decl, 0, "count", x_out, None)
        dr2["xntok"] = decl("xntok_out", [T, D], BF16, "ExternalOutput")
        record_ffn(S, dr2, B, 0, split="count")
    return _mk_composite(b, serialize)


def build_sc(serialize=False):
    def b(S, B, decl):
        x_in = decl("x_in", [T, D], F32, "ExternalInput")
        x_out = decl("x_out", [T, D], F32, "ExternalOutput")
        B.kind = "ffn"
        S.prefix = "c0"
        _xcopy(S, x_in, x_out)
        dr = _ffn_dr(decl, 0, "main", x_in, None)
        dr["xntok"] = decl("xntok_in", [T, D], BF16, "ExternalInput")
        dr["prefT"] = decl("prefT_in", [NT + 1, E], F32, "ExternalInput")
        _sparse_extra_dr(decl, dr, "gbuf_scr", "ExternalOutput", x_out)
        record_ffn(S, dr, B, 0, split="main", sparse=True)
        S.barrier()
        S.prefix = "c1"
        dr2 = _ffn_dr(decl, 1, "cnt1", x_out, None)
        dr2["pref_out"] = decl("pref1_out", [8, NT + 1], F32, "ExternalOutput")
        dr2["c_ltri"] = decl("c_ltri", [128, 128], F32, "ExternalInput")
        dr2["xntok"] = decl("scr_xntok", [T, D], BF16)
        dr2["prefT"] = decl("scr_prefT", [NT + 1, E], F32)
        _sparse_extra_dr(decl, dr2, "gbuf1_out", "ExternalOutput", None)
        record_ffn(S, dr2, B, 1, split="cnt1")
    return _mk_composite(b, serialize)


def build_sd(serialize=False):
    def b(S, B, decl):
        x_in = decl("x_in", [T, D], F32, "ExternalInput")
        x_out = decl("x_out", [T, D], F32, "ExternalOutput")
        x_scr = decl("x_scr", [T, D], F32)
        B.kind = "ffn"
        S.prefix = "d0"
        _xcopy(S, x_in, x_scr)
        dr = {"c_ident": decl("c_ident", [128, 128], F32, "ExternalInput"),
              "w1": decl("w1_1", [E, D, 4096], F32, "ExternalInput"),
              "b1T": decl("b1T_1", [128, 32 * E], F32, "ExternalInput"),
              "w2": decl("w2_1", [E, 4096, D], F32, "ExternalInput"),
              "b2": decl("b2_1", [E, D], F32, "ExternalInput"),
              "gbuf": decl("gbuf1_in", [E * CAP, GW], BF16, "ExternalInput"),
              "x_res": x_scr}
        record_ffn_sparse_main(S, dr, B, 1)
        S.barrier()
        B.kind = "l1a"
        S.prefix = "d1"
        record_l1a(S, _l1a_dr(decl, x_scr, x_out), B)
        S.barrier()
        B.kind = "ffn"
        S.prefix = "d2"
        dr2 = _ffn_dr(decl, 1, "cnt1", x_out, None)
        dr2["pref_out"] = decl("pref1_out", [8, NT + 1], F32, "ExternalOutput")
        dr2["c_ltri"] = decl("c_ltri", [128, 128], F32, "ExternalInput")
        dr2["xntok"] = decl("scr_xntok", [T, D], BF16)
        dr2["prefT"] = decl("scr_prefT", [NT + 1, E], F32)
        _sparse_extra_dr(decl, dr2, "gbuf2_out", "ExternalOutput", None)
        record_ffn(S, dr2, B, 1, split="cnt1")
    return _mk_composite(b, serialize)


def build_se(serialize=False):
    def b(S, B, decl):
        x_in = decl("x_in", [T, D], F32, "ExternalInput")
        x_out = decl("x_out", [T, D], F32, "ExternalOutput")
        B.kind = "ffn"
        S.prefix = "e0"
        _xcopy(S, x_in, x_out)
        dr = {"c_ident": decl("c_ident", [128, 128], F32, "ExternalInput"),
              "w1": decl("w1_1", [E, D, 4096], F32, "ExternalInput"),
              "b1T": decl("b1T_1", [128, 32 * E], F32, "ExternalInput"),
              "w2": decl("w2_1", [E, 4096, D], F32, "ExternalInput"),
              "b2": decl("b2_1", [E, D], F32, "ExternalInput"),
              "gbuf": decl("gbuf2_in", [E * CAP, GW], BF16, "ExternalInput"),
              "x_res": x_out}
        record_ffn_sparse_main(S, dr, B, 1)
    return _mk_composite(b, serialize)



def _b1T(b1):
    Hh = b1.shape[1]
    return np.ascontiguousarray(
        b1.reshape(E, Hh // 128, 128).transpose(2, 1, 0).reshape(128, (Hh // 128) * E))



CAP = 1536
GW = 1032


def record_slot_scatter(S, dr, B, layer, mw_col_of, rank_src, gbuf,
                        pref_rows_hbm):
    nc = S.nc
    sbf = B.buf
    sel_a = rank_src["sel_a"]
    ltri_sb = rank_src["ltri_sb"]
    ones_r = rank_src["ones_r"]
    ps_rk = rank_src["ps_rk"]
    xn_ap_of = rank_src["xn_ap_of"]
    ecap8 = sbf("ecap8", [128, 8], F32)
    S.step("sync", lambda e: e.dma_start(ecap8[:], dr["c_iota8"][:, :]),
           writes=("ecap8i",), dma=True)
    S.step("vector", lambda e: e.tensor_scalar_mul(ecap8[:], ecap8[:], float(CAP)),
           reads=("ecap8i",), writes=("ecap8",))
    iota8f = sbf("iota8f", [128, 8], F32)
    S.step("sync", lambda e: e.dma_start(iota8f[:], dr["c_iota8"][:, :]),
           writes=("iota8f",), dma=True)
    tokhl = sbf("tokhl", [128, 2 * NT], BF16)
    S.step("gpsimd", lambda e: e.dma_start(tokhl[:], dr["c_tokhl"][:, :]),
           writes=("tokhl",), dma=True)
    rkst = [sbf(f"rkst{i}", [1, 8], F32) for i in range(2)]
    rank_sb = sbf("rank_sb", [128, 8], F32)
    cum = sbf("cum", [128, 8], F32)
    selA = sbf("selA", [128, 8], F32)
    selB = sbf("selB", [128, 8], F32)
    scr8 = sbf("scr8", [128, 8], F32)
    ercol = sbf("ercol", [128, 6], F32)
    slotf = sbf("slotf", [128, 2], F32)
    sloti = sbf("sloti", [128, 2], I32)
    onescol2 = sbf("onescol2", [128, 1], F32)
    S.step("vector", lambda e: e.memset(onescol2[:], 1.0), writes=("onescol2",))
    rowb = [sbf(f"rowb{i}", [128, GW], BF16) for i in range(2)]

    for c in range(NT):
        xn_src, tk_xn, xn_hbm = xn_ap_of(c)
        mw_ap, tk_mw = mw_col_of(c)
        rb = rowb[c % 2]
        tkrb = ("rowb", c % 2)
        S.step("gpsimd", lambda e, c=c: e.dma_start(
            rkst[c % 2][:], pref_rows_hbm[c:c + 1, :]),
            reads=("prefh",), writes=(("rkst", c % 2),), dma=True)

        def rkmm(e, c=c):
            e.matmul(ps_rk[:, 0:8], ltri_sb[:], sel_a[:, c * E:(c + 1) * E],
                     start=True, stop=False)
            return e.matmul(ps_rk[:, 0:8], ones_r[:], rkst[c % 2][:],
                            start=False, stop=True)
        S.step("tensor", rkmm, reads=(("sel", c), ("rkst", c % 2),
                                      "ltri_sb", "ones_r"), writes=("ps_rk",))
        S.step("scalar", lambda e: e.copy(rank_sb[:], ps_rk[:, 0:8]),
               reads=("ps_rk",), writes=("rank_sb",))
        S.step("vector", lambda e, c=c: e.tensor_tensor_scan(
            cum[:], onescol2[:, 0:1].to_broadcast([128, 8]),
            sel_a[:, c * E:(c + 1) * E], 0.0, ALU.mult, ALU.add),
            reads=(("sel", c), "onescol2"), writes=("cum",))
        S.step("vector", lambda e: e.tensor_scalar(scr8[:], cum[:], -1.0, 2.0,
                                                   ALU.mult, ALU.add),
               reads=("cum",), writes=("scr8",))
        S.step("vector", lambda e: e.tensor_mul(scr8[:], scr8[:], cum[:]),
               reads=("scr8", "cum"), writes=("scr8",))
        S.step("vector", lambda e, c=c: e.tensor_mul(
            selA[:], scr8[:], sel_a[:, c * E:(c + 1) * E]),
            reads=("scr8", ("sel", c)), writes=("selA",))
        S.step("vector", lambda e, c=c: e.tensor_sub(
            selB[:], sel_a[:, c * E:(c + 1) * E], selA[:]),
            reads=(("sel", c), "selA"), writes=("selB",))
        for k, selk, kt in ((0, selA, "selA"), (1, selB, "selB")):
            S.step("vector", lambda e, selk=selk: e.tensor_mul(
                scr8[:], ecap8[:], selk[:]),
                reads=("ecap8", kt), writes=("scr8",))
            S.step("vector", lambda e, k=k: e.reduce_sum(
                ercol[:, 3 * k:3 * k + 1], scr8[:], axis=AX.X),
                reads=("scr8",), writes=(("er", k),))
            S.step("vector", lambda e, selk=selk: e.tensor_mul(
                scr8[:], rank_sb[:], selk[:]),
                reads=("rank_sb", kt), writes=("scr8",))
            S.step("vector", lambda e, k=k: e.reduce_sum(
                ercol[:, 3 * k + 1:3 * k + 2], scr8[:], axis=AX.X),
                reads=("scr8",), writes=(("er", k),))
            S.step("vector", lambda e, selk=selk, mw_ap=mw_ap: e.tensor_mul(
                scr8[:], mw_ap, selk[:]),
                reads=(tk_mw, kt), writes=("scr8",))
            S.step("vector", lambda e, k=k: e.reduce_sum(
                ercol[:, 3 * k + 2:3 * k + 3], scr8[:], axis=AX.X),
                reads=("scr8",), writes=(("er", k),))
            S.step("vector", lambda e, k=k: e.tensor_add(
                slotf[:, k:k + 1], ercol[:, 3 * k:3 * k + 1],
                ercol[:, 3 * k + 1:3 * k + 2]),
                reads=(("er", k),), writes=("slotf",))
        S.step("vector", lambda e: e.tensor_copy(sloti[:], slotf[:]),
               reads=("slotf",), writes=("sloti",))
        if xn_hbm:
            S.step("sync", lambda e, rb=rb, xn_src=xn_src, c=c: e.dma_start(
                rb[:, 0:D], xn_src[c * 128:(c + 1) * 128, :]),
                reads=(tk_xn,), writes=(tkrb,), dma=True, fam=f"rowbl{c % 2}")
        else:
            S.step("scalar", lambda e, rb=rb, xn_src=xn_src: e.copy(
                rb[:, 0:D], xn_src),
                reads=(tk_xn,), writes=(tkrb,))
        S.step("scalar", lambda e, rb=rb, c=c: e.copy(
            rb[:, D + 1:D + 3], tokhl[:, 2 * c:2 * c + 2]),
            reads=("tokhl",), writes=(tkrb,))
        for k in range(2):
            S.step("scalar", lambda e, rb=rb, k=k: e.copy(
                rb[:, D:D + 1], ercol[:, 3 * k + 2:3 * k + 3]),
                reads=(("er", k),), writes=(tkrb,))
            S.step("gpsimd", lambda e, rb=rb, k=k: e.indirect_dma_start(
                gbuf[:, :], IndirectOffsetOnAxis(sloti[:, k:k + 1], 0),
                rb[:], None),
                reads=(tkrb, "sloti"), writes=(("gbuf", c % 2),), dma=True,
                fam=f"gsc{c % 2}")


def record_ffn_sparse_main(S, dr, B, layer):
    H = 2048 if layer == 0 else 4096
    NFB = H // 2048
    FCB = 16
    NSUB = CAP // 128
    NSG = CAP // 512
    ACTF = AF.Gelu if layer == 0 else AF.Relu
    nc = S.nc
    sbf = B.buf
    psf = B.psum
    gbuf = dr["gbuf"]
    x_res = dr["x_res"]
    w1 = dr["w1"]; w2 = dr["w2"]; b1T = dr["b1T"]; b2 = dr["b2"]
    c_ident = dr["c_ident"]

    ident_bf = sbf("ident_bf2", [128, 128], BF16)
    S.step("gpsimd", lambda e: e.dma_start(ident_bf[:], c_ident[:, :]),
           writes=("ident_bf",), dma=True)
    b1T_sb = sbf(f"b1T_sb{layer}", [128, (H // 128) * E], F32)
    b2_bf = sbf("b2_bf", [8, D], BF16)
    b2row = sbf("b2row", [1, D], BF16)
    ones_r2 = sbf("ones_r2", [1, 128], BF16)
    S.step("sync", lambda e: e.dma_start(b1T_sb[:], b1T[:, :]),
           writes=("b1T_sb",), dma=True)
    S.step("gpsimd", lambda e: e.dma_start(b2_bf[:], b2[:, :]),
           writes=("b2_bf",), dma=True)
    S.step("vector", lambda e: e.memset(ones_r2[:], 1.0), writes=("ones_r2",))
    xgT = sbf("xgT", [128, DC * CAP], BF16)
    h_sg = sbf("h_sg", [128, FCB * 512], BF16)
    w1_sb = sbf("w1_sb", [128, DC * FCB * 128], BF16)
    w2_sb = sbf("w2_sb", [128, FCB * D], BF16)
    grow = [sbf(f"grow{i}", [128, GW], BF16) for i in range(2)]
    mwc = sbf("mwc", [128, NSUB], F32)
    tokf = sbf("tokf", [128, NSUB], F32)
    toki = sbf("toki", [128, NSUB], I32)
    ysb = [sbf(f"ysb{i}", [128, D], F32) for i in range(2)]
    ps_tr = psf("ps_tr", [128, 1024], BF16)
    ps_h = [psf(f"ps_h{i}", [128, 512], F32) for i in range(2)]
    ps_y = [psf(f"ps_y{i}", [128, 512], F32) for i in range(2)]

    for e_i in range(E):
        S.step("gpsimd", lambda e, e_i=e_i: e.dma_start(
            b2row[:], b2_bf[e_i:e_i + 1, :]),
            reads=("b2_bf",), writes=("b2row",), dma=True, fam="b2row")
        for st in range(NSUB):
            gsl = grow[st % 2]
            S.step("sync", lambda e, e_i=e_i, st=st, gsl=gsl: e.dma_start(
                gsl[:], gbuf[e_i * CAP + st * 128: e_i * CAP + (st + 1) * 128, :]),
                reads=(("gbuf", 0), ("gbuf", 1)), writes=(("grow", st % 2),),
                dma=True, fam=f"grow{st % 2}")
            def trx(e, st=st, gsl=gsl):
                for g in range(DC):
                    i = e.transpose(ps_tr[:, g * 128:(g + 1) * 128],
                                    gsl[:, g * 128:(g + 1) * 128], ident_bf[:])
                return i
            S.step("tensor", trx, reads=(("grow", st % 2), "ident_bf"),
                   writes=("ps_tr",))
            def trev(e, st=st):
                for g in range(DC):
                    i = e.copy(xgT[:, g * CAP + st * 128: g * CAP + (st + 1) * 128],
                               ps_tr[:, g * 128:(g + 1) * 128])
                return i
            S.step("scalar", trev, reads=("ps_tr",), writes=(("xgT", st),))
            S.step("vector", lambda e, st=st, gsl=gsl: e.tensor_copy(
                mwc[:, st:st + 1], gsl[:, D:D + 1]),
                reads=(("grow", st % 2),), writes=("mwc",))
            S.step("vector", lambda e, st=st, gsl=gsl: e.tensor_copy(
                tokf[:, st:st + 1], gsl[:, D + 1:D + 2]),
                reads=(("grow", st % 2),), writes=(("tokf", st),))
            S.step("vector", lambda e, st=st, gsl=gsl: e.scalar_tensor_tensor(
                tokf[:, st:st + 1], tokf[:, st:st + 1], 64.0,
                gsl[:, D + 2:D + 3], ALU.mult, ALU.add),
                reads=(("tokf", st), ("grow", st % 2)), writes=(("tokf", st),))
        S.step("vector", lambda e: e.tensor_copy(toki[:], tokf[:]),
               reads=tuple(("tokf", st) for st in range(NSUB)), writes=("toki",))
        xg_all = tuple(("xgT", st) for st in range(NSUB))

        for fb in range(NFB):
            src1 = w1[e_i, :, fb * 2048:(fb + 1) * 2048].rearrange(
                "(dc p) (fc k) -> p dc fc k", p=128, k=128)
            S.step("gpsimd", lambda e, src1=src1: e.dma_start(
                w1_sb[:].rearrange("p (dc fc k) -> p dc fc k", fc=FCB, k=128), src1),
                writes=("w1_sb",), dma=True)
            src2 = w2[e_i, fb * 2048:(fb + 1) * 2048, :].rearrange(
                "(fc p) d -> p fc d", p=128)
            S.step("gpsimd", lambda e, src2=src2: e.dma_start(
                w2_sb[:].rearrange("p (fc d) -> p fc d", d=D), src2),
                writes=("w2_sb",), dma=True)
            for sg in range(NSG):
                for fc in range(FCB):
                    def hmm(e, fc=fc, sg=sg):
                        for dc in range(DC):
                            i = e.matmul(
                                ps_h[fc % 2][:],
                                w1_sb[:, (dc * FCB + fc) * 128:(dc * FCB + fc + 1) * 128],
                                xgT[:, dc * CAP + sg * 512: dc * CAP + (sg + 1) * 512],
                                start=(dc == 0), stop=(dc == DC - 1))
                        return i
                    S.step("tensor", hmm, reads=xg_all + ("w1_sb",),
                           writes=(f"ps_h{fc % 2}",))
                    bcol = (fb * FCB + fc) * E + e_i
                    S.step("scalar", lambda e, fc=fc, bcol=bcol: e.activation(
                        h_sg[:, fc * 512:(fc + 1) * 512], ps_h[fc % 2][:], ACTF,
                        bias=b1T_sb[:, bcol:bcol + 1]),
                        reads=(f"ps_h{fc % 2}", "b1T_sb"), writes=(("h", fc),))
                h_all = tuple(("h", fc) for fc in range(FCB))
                for su in range(4):
                    st = sg * 4 + su
                    for dt in range(2):
                        def ymm(e, su=su, dt=dt, e_i=e_i, fb=fb):
                            for fc in range(FCB):
                                last = (fc == FCB - 1) and (fb < NFB - 1)
                                i = e.matmul(
                                    ps_y[dt][:],
                                    h_sg[:, fc * 512 + su * 128: fc * 512 + su * 128 + 128],
                                    w2_sb[:, fc * D + dt * 512: fc * D + (dt + 1) * 512],
                                    start=(fc == 0), stop=last)
                            if fb == NFB - 1:
                                i = e.matmul(
                                    ps_y[dt][:], ones_r2[:],
                                    b2row[0:1, dt * 512:(dt + 1) * 512],
                                    start=False, stop=True)
                            return i
                        S.step("tensor", ymm,
                               reads=h_all + ("w2_sb", "ones_r2", "b2_bf"),
                               writes=(f"ps_y{dt}",))
                        S.step("vector", lambda e, st=st, dt=dt: e.tensor_scalar_mul(
                            ysb[st % 2][:, dt * 512:(dt + 1) * 512], ps_y[dt][:],
                            mwc[:, st:st + 1]),
                            reads=(f"ps_y{dt}", "mwc"), writes=(("ysb", st % 2, dt),))
                    S.step("gpsimd", lambda e, st=st: e.indirect_dma_start(
                        x_res[:, :], IndirectOffsetOnAxis(toki[:, st:st + 1], 0),
                        ysb[st % 2][:], None, compute_op=ALU.add),
                        reads=(("ysb", st % 2, 0), ("ysb", st % 2, 1), "toki", "xsc"),
                        writes=("xsc",), dma=True, fam=f"gadd{st % 2}")



def _mk_composite(build_fn, serialize=False):
    nc = bass.Bass("TRN2", target_bir_lowering=False, debug=False)
    with ExitStack() as ctx:
        S = Sched(nc, ctx, serialize=serialize)
        B = BufAlloc(nc, ctx)
        cache = {}

        def decl(name, shape, dt, kind=None):
            if name not in cache:
                if kind:
                    cache[name] = nc.dram_tensor(name, shape, dt, kind=kind)
                else:
                    cache[name] = nc.dram_tensor(name, shape, dt)
            return cache[name]

        build_fn(S, B, decl)
        S.build_block(ctx)
    return nc


def _ffn_dr(decl, layer, split, x_src, x_dst):
    H = 2048 if layer == 0 else 4096
    dr = {"c_ident": decl("c_ident", [128, 128], F32, "ExternalInput"),
          "x_in": x_src, "x_out": x_dst}
    if split in ("all", "count", "cnt1"):
        dr["wg"] = decl(f"wg_{layer}", [D, E], F32, "ExternalInput")
        if layer == 0:
            dr["rb"] = decl("rb_0", [1, E], F32, "ExternalInput")
    if split in ("all", "main"):
        dr["w1"] = decl(f"w1_{layer}", [E, D, H], F32, "ExternalInput")
        dr["b1T"] = decl(f"b1T_{layer}", [128, (H // 128) * E], F32, "ExternalInput")
        dr["w2"] = decl(f"w2_{layer}", [E, H, D], F32, "ExternalInput")
        dr["b2"] = decl(f"b2_{layer}", [E, D], F32, "ExternalInput")
    if split == "count":
        dr["xnT"] = decl("xnT_out", [D, T], BF16, "ExternalOutput")
        dr["mw_out"] = decl("mw_out", [128, NT * E], F32, "ExternalOutput")
        dr["sel_out"] = decl("sel_out", [128, NT * E], F32, "ExternalOutput")
        dr["pref_out"] = decl("pref_out", [8, NT + 1], F32, "ExternalOutput")
    elif split == "main":
        dr["xnT"] = decl("xnT_in", [D, T], BF16, "ExternalInput")
        dr["mw_in"] = decl("mw_in", [128, NT * E], F32, "ExternalInput")
        dr["sel_in"] = decl("sel_in", [128, NT * E], F32, "ExternalInput")
        dr["pref_in"] = decl("pref_in", [8, NT + 1], F32, "ExternalInput")
        dr["agcnt"] = decl("agcnt", [8, E], F32, "ExternalInput")
        dr["c_ltri"] = decl("c_ltri", [128, 128], F32, "ExternalInput")
        dr["c_corelt"] = decl("c_corelt", [8, 8], F32, "ExternalInput")
        dr["nthr"] = decl("scr_nthr", [NT + 1, E], F32)
    else:
        dr["xnT"] = decl("scr_xnT_ffn", [D, T], BF16)
    return dr


def _l0a_dr(decl, x_src, x_dst):
    return {
        "c_ident": decl("c_ident", [128, 128], F32, "ExternalInput"),
        "c_bmask": decl("c_bmask", [128, 128], F32, "ExternalInput"),
        "x_in": x_src, "x_out": x_dst,
        "wqkv": decl("wqkv", [D, 3 * D], F32, "ExternalInput"),
        "wout": decl("wout", [D, D], F32, "ExternalInput"),
        "bout": decl("bout", [1, D], F32, "ExternalInput"),
        "xnT": decl("scr_xnT_l0a", [D, T], BF16),
        "yB": decl("scr_yB", [T, D], F32),
    }


def _l1a_dr(decl, x_src, x_dst):
    dr = {"c_ident": decl("c_ident", [128, 128], F32, "ExternalInput"),
          "x_in": x_src, "x_out": x_dst}
    for nm in ("wkT", "wvT", "wrT", "woT"):
        dr[nm] = decl(nm, [D, D], F32, "ExternalInput")
    for nm in ("tmk5", "tmv5", "tmr5", "dec5", "fst5"):
        dr[nm] = decl(nm, [128, DC], F32, "ExternalInput")
    return dr


def build_pa(serialize=False):
    def b(S, B, decl):
        x_in = decl("x_in", [T, D], F32, "ExternalInput")
        B.kind = "ffn"
        S.prefix = "a0"
        record_ffn(S, _ffn_dr(decl, 0, "count", x_in, None), B, 0, split="count")
    return _mk_composite(b, serialize)


def build_pb(serialize=False):
    def b(S, B, decl):
        x_in = decl("x_in", [T, D], F32, "ExternalInput")
        x_out = decl("x_out", [T, D], F32, "ExternalOutput")
        x_scr = decl("x_scr", [T, D], F32)
        B.kind = "ffn"
        S.prefix = "b0"
        record_ffn(S, _ffn_dr(decl, 0, "main", x_in, x_scr), B, 0, split="main")
        S.barrier()
        B.kind = "l0a"
        S.prefix = "b1"
        record_l0a(S, _l0a_dr(decl, x_scr, x_out), B)
        S.barrier()
        B.kind = "ffn"
        S.prefix = "b2"
        record_ffn(S, _ffn_dr(decl, 0, "count", x_out, None), B, 0, split="count")
    return _mk_composite(b, serialize)


def build_pc(serialize=False):
    def b(S, B, decl):
        x_in = decl("x_in", [T, D], F32, "ExternalInput")
        x_out = decl("x_out", [T, D], F32, "ExternalOutput")
        x_scr = decl("x_scr", [T, D], F32)
        B.kind = "ffn"
        S.prefix = "c0"
        record_ffn(S, _ffn_dr(decl, 0, "main", x_in, x_scr), B, 0, split="main")
        S.barrier()
        S.prefix = "c1"
        record_ffn(S, _ffn_dr(decl, 1, "all", x_scr, x_scr), B, 1, split="all")
        S.barrier()
        B.kind = "l1a"
        S.prefix = "c2"
        record_l1a(S, _l1a_dr(decl, x_scr, x_scr), B)
        S.barrier()
        B.kind = "ffn"
        S.prefix = "c3"
        record_ffn(S, _ffn_dr(decl, 1, "all", x_scr, x_out), B, 1, split="all")
    return _mk_composite(b, serialize)



def host_consts_sparse():
    iota8 = np.broadcast_to(np.arange(8, dtype=np.float32), (128, 8)).copy()
    tok = np.arange(T, dtype=np.int64)
    hi = (tok // 64).astype(np.float32)
    lo = (tok % 64).astype(np.float32)
    tokhl = np.zeros((128, 2 * NT), np.float32)
    for c in range(NT):
        tokhl[:, 2 * c] = hi[c * 128:(c + 1) * 128]
        tokhl[:, 2 * c + 1] = lo[c * 128:(c + 1) * 128]
    return {"c_iota8": iota8,
            "c_tokhl": tokhl.astype(BFNP)}


def _sparse_extra_dr(decl, dr, gbuf_name, gbuf_kind, x_res):
    dr["c_iota8"] = decl("c_iota8", [128, 8], F32, "ExternalInput")
    dr["c_tokhl"] = decl("c_tokhl", [128, 2 * NT], BF16, "ExternalInput")
    dr["gbuf"] = decl(gbuf_name, [E * CAP, GW], BF16, gbuf_kind)
    dr["x_res"] = x_res
    return dr


def _xcopy(S, src_t, dst_t, wtok=("xsc",)):
    for i in range(4):
        S.step("sync", lambda e, i=i: e.dma_start(
            dst_t[i * 1024:(i + 1) * 1024, :], src_t[i * 1024:(i + 1) * 1024, :]),
            reads=(("xcp_src", i),), writes=wtok + ((f"xcp{i}",),), dma=True,
            fam="xcopy")


def build_sa(serialize=False):
    def b(S, B, decl):
        x_in = decl("x_in", [T, D], F32, "ExternalInput")
        B.kind = "ffn"
        S.prefix = "a0"
        dr = _ffn_dr(decl, 0, "count", x_in, None)
        dr["xntok"] = decl("xntok_out", [T, D], BF16, "ExternalOutput")
        record_ffn(S, dr, B, 0, split="count")
    return _mk_composite(b, serialize)


def build_sb(serialize=False):
    def b(S, B, decl):
        x_in = decl("x_in", [T, D], F32, "ExternalInput")
        x_out = decl("x_out", [T, D], F32, "ExternalOutput")
        x_scr = decl("x_scr", [T, D], F32)
        B.kind = "ffn"
        S.prefix = "b0"
        _xcopy(S, x_in, x_scr)
        dr = _ffn_dr(decl, 0, "main", x_in, None)
        dr["xntok"] = decl("xntok_in", [T, D], BF16, "ExternalInput")
        dr["prefT"] = decl("prefT_in", [NT + 1, E], F32, "ExternalInput")
        _sparse_extra_dr(decl, dr, "gbuf_scr", "ExternalOutput", x_scr)
        record_ffn(S, dr, B, 0, split="main", sparse=True)
        S.barrier()
        B.kind = "l0a"
        S.prefix = "b1"
        record_l0a(S, _l0a_dr(decl, x_scr, x_out), B)
        S.barrier()
        B.kind = "ffn"
        S.prefix = "b2"
        dr2 = _ffn_dr(decl, 0, "count", x_out, None)
        dr2["xntok"] = decl("xntok_out", [T, D], BF16, "ExternalOutput")
        record_ffn(S, dr2, B, 0, split="count")
    return _mk_composite(b, serialize)


def build_sc(serialize=False):
    def b(S, B, decl):
        x_in = decl("x_in", [T, D], F32, "ExternalInput")
        x_out = decl("x_out", [T, D], F32, "ExternalOutput")
        B.kind = "ffn"
        S.prefix = "c0"
        _xcopy(S, x_in, x_out)
        dr = _ffn_dr(decl, 0, "main", x_in, None)
        dr["xntok"] = decl("xntok_in", [T, D], BF16, "ExternalInput")
        dr["prefT"] = decl("prefT_in", [NT + 1, E], F32, "ExternalInput")
        _sparse_extra_dr(decl, dr, "gbuf_scr", "ExternalOutput", x_out)
        record_ffn(S, dr, B, 0, split="main", sparse=True)
        S.barrier()
        S.prefix = "c1"
        dr2 = _ffn_dr(decl, 1, "cnt1", x_out, None)
        dr2["pref_out"] = decl("pref1_out", [8, NT + 1], F32, "ExternalOutput")
        dr2["c_ltri"] = decl("c_ltri", [128, 128], F32, "ExternalInput")
        dr2["xntok"] = decl("scr_xntok", [T, D], BF16)
        dr2["prefT"] = decl("scr_prefT", [NT + 1, E], F32)
        _sparse_extra_dr(decl, dr2, "gbuf1_out", "ExternalOutput", None)
        record_ffn(S, dr2, B, 1, split="cnt1")
    return _mk_composite(b, serialize)


def build_sd(serialize=False):
    def b(S, B, decl):
        x_in = decl("x_in", [T, D], F32, "ExternalInput")
        x_out = decl("x_out", [T, D], F32, "ExternalOutput")
        x_scr = decl("x_scr", [T, D], F32)
        B.kind = "ffn"
        S.prefix = "d0"
        _xcopy(S, x_in, x_scr)
        dr = {"c_ident": decl("c_ident", [128, 128], F32, "ExternalInput"),
              "w1": decl("w1_1", [E, D, 4096], F32, "ExternalInput"),
              "b1T": decl("b1T_1", [128, 32 * E], F32, "ExternalInput"),
              "w2": decl("w2_1", [E, 4096, D], F32, "ExternalInput"),
              "b2": decl("b2_1", [E, D], F32, "ExternalInput"),
              "gbuf": decl("gbuf1_in", [E * CAP, GW], BF16, "ExternalInput"),
              "x_res": x_scr}
        record_ffn_sparse_main(S, dr, B, 1)
        S.barrier()
        B.kind = "l1a"
        S.prefix = "d1"
        record_l1a(S, _l1a_dr(decl, x_scr, x_out), B)
        S.barrier()
        B.kind = "ffn"
        S.prefix = "d2"
        dr2 = _ffn_dr(decl, 1, "cnt1", x_out, None)
        dr2["pref_out"] = decl("pref1_out", [8, NT + 1], F32, "ExternalOutput")
        dr2["c_ltri"] = decl("c_ltri", [128, 128], F32, "ExternalInput")
        dr2["xntok"] = decl("scr_xntok", [T, D], BF16)
        dr2["prefT"] = decl("scr_prefT", [NT + 1, E], F32)
        _sparse_extra_dr(decl, dr2, "gbuf2_out", "ExternalOutput", None)
        record_ffn(S, dr2, B, 1, split="cnt1")
    return _mk_composite(b, serialize)


def build_se(serialize=False):
    def b(S, B, decl):
        x_in = decl("x_in", [T, D], F32, "ExternalInput")
        x_out = decl("x_out", [T, D], F32, "ExternalOutput")
        B.kind = "ffn"
        S.prefix = "e0"
        _xcopy(S, x_in, x_out)
        dr = {"c_ident": decl("c_ident", [128, 128], F32, "ExternalInput"),
              "w1": decl("w1_1", [E, D, 4096], F32, "ExternalInput"),
              "b1T": decl("b1T_1", [128, 32 * E], F32, "ExternalInput"),
              "w2": decl("w2_1", [E, 4096, D], F32, "ExternalInput"),
              "b2": decl("b2_1", [E, D], F32, "ExternalInput"),
              "gbuf": decl("gbuf2_in", [E * CAP, GW], BF16, "ExternalInput"),
              "x_res": x_out}
        record_ffn_sparse_main(S, dr, B, 1)
    return _mk_composite(b, serialize)



def _ffn_w(inputs, layer):
    if layer == 0:
        return {
            "w1_0": inputs["L0_ffn_w1"],
            "b1T_0": _b1T(np.asarray(inputs["L0_ffn_b1"], np.float32)),
            "w2_0": inputs["L0_ffn_w2"],
            "b2_0": np.ascontiguousarray(np.asarray(inputs["L0_ffn_b2"], np.float32)),
        }
    return {
        "wg_1": np.ascontiguousarray(np.asarray(inputs["L1_ffn_W_gate"], np.float32)),
        "w1_1": inputs["L1_ffn_W1"],
        "b1T_1": _b1T(np.asarray(inputs["L1_ffn_b1"], np.float32)),
        "w2_1": inputs["L1_ffn_W2"],
        "b2_1": np.ascontiguousarray(np.asarray(inputs["L1_ffn_b2"], np.float32)),
    }


def _cnt_w(inputs):
    return {
        "wg_0": np.ascontiguousarray(np.asarray(inputs["L0_ffn_router_w"], np.float32)),
        "rb_0": np.ascontiguousarray(
            np.asarray(inputs["L0_ffn_router_b"], np.float32)).reshape(1, E),
    }


def _l0a_w(inputs, core_id):
    cons = host_consts(core_id)
    return {
        "wqkv": np.ascontiguousarray(np.asarray(inputs["L0_attn_w_qkv"], np.float32)),
        "wout": np.ascontiguousarray(np.asarray(inputs["L0_attn_w_out"], np.float32)),
        "bout": np.ascontiguousarray(
            np.asarray(inputs["L0_attn_b_out"], np.float32)).reshape(1, D),
        "c_bmask": cons["c_bmask"],
    }


def _l1a_w(inputs):
    def five(v):
        return np.ascontiguousarray(np.asarray(v, np.float32).reshape(DC, 128).T)
    return {
        "wkT": np.ascontiguousarray(np.asarray(inputs["L1_attn_W_k"], np.float32).T),
        "wvT": np.ascontiguousarray(np.asarray(inputs["L1_attn_W_v"], np.float32).T),
        "wrT": np.ascontiguousarray(np.asarray(inputs["L1_attn_W_r"], np.float32).T),
        "woT": np.ascontiguousarray(np.asarray(inputs["L1_attn_W_o"], np.float32).T),
        "tmk5": five(inputs["L1_attn_time_mix_k"]),
        "tmv5": five(inputs["L1_attn_time_mix_v"]),
        "tmr5": five(inputs["L1_attn_time_mix_r"]),
        "dec5": five(inputs["L1_attn_time_decay"]),
        "fst5": five(inputs["L1_attn_time_first"]),
    }


def _count_glue(cres, b):
    return {
        "xnT_in": cres[b]["xnT_out"],
        "mw_in": cres[b]["mw_out"],
        "sel_in": cres[b]["sel_out"],
        "pref_in": cres[b]["pref_out"],
    }


_PROGS = {}


def _get(name):
    builders = {"pa": build_pa, "pb": build_pb, "pc": build_pc,
                "sa": build_sa, "sb": build_sb, "sc": build_sc,
                "sd": build_sd, "se": build_se}
    if name not in _PROGS:
        _PROGS[name] = builders[name]()
    return _PROGS[name]


def _kernel_dense(inputs, shards):
    cores = list(range(NCORES))
    maps = [{"x_in": shards[b], "c_ident": host_consts(b)["c_ident"],
             **_cnt_w(inputs)} for b in cores]
    cres = run_bass_kernel_spmd(_get("pa"), maps, core_ids=cores).results
    ag1 = np.ascontiguousarray(
        np.stack([np.asarray(cres[b]["pref_out"])[:, NT] for b in cores]))
    maps = []
    for b in cores:
        cons = host_consts(b)
        maps.append({"x_in": shards[b], "c_ident": cons["c_ident"],
                     "c_ltri": cons["c_ltri"], "c_corelt": cons["c_corelt"],
                     "agcnt": ag1, **_count_glue(cres, b), **_ffn_w(inputs, 0),
                     **_cnt_w(inputs), **_l0a_w(inputs, b)})
    bres = run_bass_kernel_spmd(_get("pb"), maps, core_ids=cores).results
    ag2 = np.ascontiguousarray(
        np.stack([np.asarray(bres[b]["pref_out"])[:, NT] for b in cores]))
    maps = []
    for b in cores:
        cons = host_consts(b)
        maps.append({"x_in": np.ascontiguousarray(np.asarray(bres[b]["x_out"], np.float32)),
                     "c_ident": cons["c_ident"], "c_ltri": cons["c_ltri"],
                     "c_corelt": cons["c_corelt"], "agcnt": ag2,
                     **_count_glue(bres, b), **_ffn_w(inputs, 0),
                     **_ffn_w(inputs, 1), **_l1a_w(inputs)})
    fres = run_bass_kernel_spmd(_get("pc"), maps, core_ids=cores).results
    return np.stack([np.asarray(fres[b]["x_out"], np.float32)
                     for b in cores]).astype(np.float32)


def _counts_ok(res_list):
    for r in res_list:
        if np.asarray(r)[: , NT].max() > CAP:
            return False
    return True


def kernel(**inputs):
    x = np.ascontiguousarray(np.asarray(inputs["L0_attn_x"], np.float32))
    shards = [np.ascontiguousarray(x[b]) for b in range(NCORES)]
    cores = list(range(NCORES))
    spc = host_consts_sparse()

    maps = [{"x_in": shards[b], "c_ident": host_consts(b)["c_ident"],
             **_cnt_w(inputs)} for b in cores]
    ares = run_bass_kernel_spmd(_get("sa"), maps, core_ids=cores).results
    prefs = [np.asarray(ares[b]["pref_out"]) for b in cores]
    if not _counts_ok(prefs):
        return _kernel_dense(inputs, shards)
    ag1 = np.ascontiguousarray(np.stack([p[:, NT] for p in prefs]))

    def sparse_glue(res, b):
        p = np.asarray(res[b]["pref_out"])
        return {
            "xnT_in": res[b]["xnT_out"],
            "xntok_in": res[b]["xntok_out"],
            "mw_in": res[b]["mw_out"],
            "sel_in": res[b]["sel_out"],
            "pref_in": p,
            "prefT_in": np.ascontiguousarray(p.T),
        }

    maps = []
    for b in cores:
        cons = host_consts(b)
        maps.append({"x_in": shards[b], "c_ident": cons["c_ident"],
                     "c_ltri": cons["c_ltri"], "c_corelt": cons["c_corelt"],
                     "agcnt": ag1, **sparse_glue(ares, b), **_ffn_w(inputs, 0),
                     **_cnt_w(inputs), **_l0a_w(inputs, b), **spc})
    bres = run_bass_kernel_spmd(_get("sb"), maps, core_ids=cores).results
    prefs = [np.asarray(bres[b]["pref_out"]) for b in cores]
    if not _counts_ok(prefs):
        return _kernel_dense(inputs, shards)
    ag2 = np.ascontiguousarray(np.stack([p[:, NT] for p in prefs]))

    maps = []
    for b in cores:
        cons = host_consts(b)
        maps.append({"x_in": np.ascontiguousarray(np.asarray(bres[b]["x_out"], np.float32)),
                     "c_ident": cons["c_ident"], "c_ltri": cons["c_ltri"],
                     "c_corelt": cons["c_corelt"], "agcnt": ag2,
                     **sparse_glue(bres, b), **_ffn_w(inputs, 0),
                     "wg_1": np.ascontiguousarray(
                         np.asarray(inputs["L1_ffn_W_gate"], np.float32)),
                     **spc})
    crez = run_bass_kernel_spmd(_get("sc"), maps, core_ids=cores).results
    if not _counts_ok([np.asarray(crez[b]["pref1_out"]) for b in cores]):
        return _kernel_dense(inputs, shards)

    maps = []
    for b in cores:
        cons = host_consts(b)
        maps.append({"x_in": np.ascontiguousarray(np.asarray(crez[b]["x_out"], np.float32)),
                     "c_ident": cons["c_ident"], "c_ltri": cons["c_ltri"],
                     "gbuf1_in": crez[b]["gbuf1_out"],
                     **_ffn_w(inputs, 1), **_l1a_w(inputs), **spc})
    dres = run_bass_kernel_spmd(_get("sd"), maps, core_ids=cores).results
    if not _counts_ok([np.asarray(dres[b]["pref1_out"]) for b in cores]):
        return _kernel_dense(inputs, shards)

    maps = []
    for b in cores:
        maps.append({"x_in": np.ascontiguousarray(np.asarray(dres[b]["x_out"], np.float32)),
                     "c_ident": host_consts(b)["c_ident"],
                     "gbuf2_in": dres[b]["gbuf2_out"],
                     **{k: v for k, v in _ffn_w(inputs, 1).items() if k != "wg_1"}})
    eres = run_bass_kernel_spmd(_get("se"), maps, core_ids=cores).results
    return np.stack([np.asarray(eres[b]["x_out"], np.float32)
                     for b in cores]).astype(np.float32)


# version 07
# speedup vs baseline: 1.1294x; 1.1294x over previous
import numpy as np
import ml_dtypes
from contextlib import ExitStack

import concourse.bass as bass
import concourse.mybir as mybir
from concourse.bass_utils import run_bass_kernel_spmd
from concourse.bass_core_rust import IndirectOffsetOnAxis

F32 = mybir.dt.float32
BF16 = mybir.dt.bfloat16
AF = mybir.ActivationFunctionType
ALU = mybir.AluOpType
AX = mybir.AxisListType
I32 = mybir.dt.int32

NCORES = 8
T = 4096
D = 1024
NT = T // 128
DC = D // 128
E = 8

BFNP = ml_dtypes.bfloat16

ENGINES = ("sync", "gpsimd", "tensor", "scalar", "vector")



class Step:
    __slots__ = ("idx", "eng", "fn", "reads", "writes", "dma", "deps",
                 "need_inc", "seq", "fam", "memberidx")

    def __init__(self, idx, eng, fn, reads, writes, dma):
        self.idx = idx
        self.eng = eng
        self.fn = fn
        self.reads = reads
        self.writes = writes
        self.dma = dma
        self.deps = []
        self.need_inc = False
        self.seq = 0
        self.fam = None
        self.memberidx = 0


class Sched:

    def __init__(self, nc, ctx, serialize=False):
        self.nc = nc
        self.ctx = ctx
        self.steps = []
        self.last_w = {}
        self.readers = {}
        self.serialize = serialize
        self.sems = {e: ctx.enter_context(nc.semaphore(f"sched_{e}")) for e in ENGINES}
        self.fams = {}
        self.prefix = ""
        self.nbar = 0

    def _fam(self, key, eng, bulk):
        f = self.fams.get(key)
        if f is None:
            f = {"sem": self.ctx.enter_context(self.nc.semaphore(f"fam_{len(self.fams)}")),
                 "eng": eng, "count": 0, "bulk": bulk, "key": key}
            self.fams[key] = f
        if f["eng"] != eng:
            raise RuntimeError(f"fam {key} used from {eng} and {f['eng']}")
        if bulk and not f["bulk"]:
            raise RuntimeError(f"fam {key} bulk mismatch")
        return f

    def step(self, eng, fn, reads=(), writes=(), dma=False, fam=None, bulk=False):
        i = len(self.steps)
        p = self.prefix
        reads = tuple((p, t) for t in reads)
        writes = tuple((p, t) for t in writes)
        st = Step(i, eng, fn, reads, writes, dma)
        if dma:
            if fam is None:
                w0 = writes[0][1] if writes else ("anon", i)
                fam = w0[0] if isinstance(w0, tuple) else w0
                fam = f"{eng}:{fam}" + ("" if not isinstance(w0, tuple) or len(w0) < 2
                                        else f":{w0[1]}")
            fam = f"{p}/{fam}"
            f = self._fam(fam, eng, bulk)
            f["count"] += 1
            f["last_idx"] = i
            st.fam = fam
            st.memberidx = f["count"]
        deps = set()
        for t in st.reads:
            if t in self.last_w:
                deps.add(self.last_w[t])
        for t in st.writes:
            if t in self.last_w:
                deps.add(self.last_w[t])
            for r in self.readers.get(t, ()):
                deps.add(r)
        if self.serialize and i > 0:
            deps.add(i - 1)
        st.deps = sorted(deps)
        self.steps.append(st)
        for t in st.reads:
            self.readers.setdefault(t, []).append(i)
        for t in st.writes:
            self.last_w[t] = i
            self.readers[t] = []
        return i

    def barrier(self):
        deps = set()
        last_eng = {}
        for st in self.steps:
            if st.fn is not None:
                last_eng[st.eng] = st.idx
        for f in self.fams.values():
            if f.get("last_idx") is not None:
                deps.add(f["last_idx"])
        deps.update(last_eng.values())
        self.nbar += 1
        for e in ENGINES:
            i = len(self.steps)
            st = Step(i, e, None, (), ((f"__bar{self.nbar}", e),), False)
            st.deps = sorted(d for d in deps if d != i)
            self.steps.append(st)

    def finalize(self):
        for st in self.steps:
            for d in st.deps:
                p = self.steps[d]
                if p.fn is None:
                    raise RuntimeError("dependency on a barrier step")
                if not (p.eng == st.eng == "tensor" and not p.dma):
                    p.need_inc = True
        counts = {e: 0 for e in ENGINES}
        for st in self.steps:
            if st.dma:
                st.need_inc = True
                continue
            if st.need_inc:
                counts[st.eng] += 1
                st.seq = counts[st.eng]
        self.final_counts = counts

    def _wait_of(self, p):
        if p.dma:
            f = self.fams[p.fam]
            v = 16 * (f["count"] if f["bulk"] else p.memberidx)
            return (p.fam, f["sem"], v)
        return (p.eng, self.sems[p.eng], p.seq)

    def emit_engine(self, name, eng):
        watermark = {}
        for st in self.steps:
            if st.eng != name:
                continue
            for d in st.deps:
                p = self.steps[d]
                if p.eng == name == "tensor" and not p.dma:
                    continue
                if (st.dma and p.dma and st.fam == p.fam
                        and self.fams[st.fam]["bulk"]):
                    continue
                if not p.need_inc:
                    raise RuntimeError("dep without inc")
                key, sem, val = self._wait_of(p)
                if val > watermark.get(key, 0):
                    eng.wait_ge(sem, val)
                    watermark[key] = val
            if st.fn is None:
                continue
            inst = st.fn(eng)
            if st.need_inc:
                if inst is None:
                    raise RuntimeError(f"step {st.idx} on {name} returned no inst")
                if st.dma:
                    inst.then_inc(self.fams[st.fam]["sem"], 16)
                else:
                    inst.then_inc(self.sems[name], 1)
        for f in self.fams.values():
            if f["eng"] == name and f["count"]:
                eng.wait_ge(f["sem"], 16 * f["count"])

    def build_block(self, ctx):
        self.finalize()
        block = ctx.enter_context(self.nc.Block())

        @block.sync
        def _(e):
            self.emit_engine("sync", e)

        @block.gpsimd
        def _(e):
            self.emit_engine("gpsimd", e)

        @block.tensor
        def _(e):
            self.emit_engine("tensor", e)

        @block.scalar
        def _(e):
            self.emit_engine("scalar", e)

        @block.vector
        def _(e):
            self.emit_engine("vector", e)



def host_consts(core_id):
    ident = np.eye(128, dtype=np.float32)
    ltri = (np.arange(128)[:, None] < np.arange(128)[None, :]).astype(np.float32)
    bmask = np.zeros((128, 128), np.float32)
    for g in range(32):
        bmask[g * 4:(g + 1) * 4, g * 4:(g + 1) * 4] = 1.0
    corelt = np.zeros((8, 8), np.float32)
    corelt[:core_id, 0] = 1.0
    return {"c_ident": ident, "c_ltri": ltri, "c_bmask": bmask,
            "c_corelt": corelt}



def record_ln_tile(S, c, bufs, x_dram, extra_stage_readers=()):
    nb = len(bufs["xt"])
    xt = bufs["xt"][c % nb]
    xn = bufs["xn"][c % nb]
    stage = bufs["stage"][c % nb]
    stat = bufs["stat"]
    sqd = bufs["sqd"]
    ps_tr = bufs["ps_tr"]
    ident_bf = bufs["ident_bf"]
    tk_xt = ("xt", c % nb)
    tk_xn = ("xn", c % nb)
    tk_st = ("stage", c % nb)

    S.step("sync", lambda e: e.dma_start(xt[:], x_dram[c * 128:(c + 1) * 128, :]),
           reads=(("x_dram", c),), writes=(tk_xt,), dma=True)
    S.step("scalar", lambda e: e.activation(sqd[:], xt[:], AF.Square,
                                            accum_out=stat[:, 1:2]),
           reads=(tk_xt,), writes=("lnsqd", "lnssq"))
    S.step("vector", lambda e: e.reduce_sum(stat[:, 0:1], xt[:], axis=AX.X),
           reads=(tk_xt,), writes=("lnssum",))
    S.step("vector", lambda e: e.tensor_scalar_mul(stat[:, 2:3], stat[:, 0:1], 1.0 / D),
           reads=("lnssum",), writes=("lnmean",))
    S.step("vector", lambda e: e.tensor_mul(stat[:, 7:8], stat[:, 2:3], stat[:, 2:3]),
           reads=("lnmean",), writes=("lnmsq",))
    S.step("vector", lambda e: e.tensor_scalar(stat[:, 3:4], stat[:, 1:2], 1.0 / D,
                                               stat[:, 7:8], ALU.mult, ALU.subtract),
           reads=("lnssq", "lnmsq"), writes=("lnvar",))
    S.step("scalar", lambda e: e.activation(stat[:, 4:5], stat[:, 3:4], AF.Sqrt,
                                            bias=bufs["epsc"][:, 0:1]),
           reads=("lnvar", "lneps"), writes=("lnsd",))
    S.step("vector", lambda e: e.reciprocal(stat[:, 5:6], stat[:, 4:5]),
           reads=("lnsd",), writes=("lnrstd",))
    S.step("vector", lambda e: e.tensor_scalar(xn[:], xt[:], stat[:, 2:3],
                                               stat[:, 5:6], ALU.subtract, ALU.mult),
           reads=(tk_xt, "lnmean", "lnrstd"), writes=(tk_xn,))

    def tr(e):
        for g in range(DC):
            i = e.transpose(ps_tr[:, g * 128:(g + 1) * 128],
                            xn[:, g * 128:(g + 1) * 128], ident_bf[:])
        return i
    S.step("tensor", tr, reads=(tk_xn, "ident_bf"), writes=("ps_tr",))
    if bufs.get("evac_dsts") is not None:
        dsts, dtok = bufs["evac_dsts"](c)
        def ev(e, dsts=dsts):
            for g in range(DC):
                i = e.copy(dsts[g], ps_tr[:, g * 128:(g + 1) * 128])
            return i
        S.step("scalar", ev, reads=("ps_tr",) + tuple(bufs.get("evac_extra_reads", ())),
               writes=(dtok,))
        return dtok
    S.step("scalar", lambda e: e.copy(stage[:], ps_tr[:]),
           reads=("ps_tr",), writes=(tk_st,))
    return tk_st


def alloc_ln_bufs(sbf, psf, nbuf=2):
    return {
        "xt": [sbf(f"ln_xt{i}", [128, D], F32) for i in range(nbuf)],
        "xn": [sbf(f"ln_xn{i}", [128, D], BF16) for i in range(nbuf)],
        "stage": [sbf(f"ln_st{i}", [128, D], BF16) for i in range(nbuf)],
        "sqd": sbf("ln_sqd", [128, D], F32),
        "stat": sbf("ln_stat", [128, 8], F32),
        "epsc": sbf("ln_eps", [128, 1], F32),
        "ps_tr": psf("ps_tr", [128, 1024], BF16),
        "ident_bf": sbf("ident_bf", [128, 128], BF16),
    }




class BufAlloc:
    ALIAS = {
        "w1_sb": ("wpool", 0), "w2_sb": ("wpool", 16384),
        "wkT_sb": ("wpool", 0), "wvT_sb": ("wpool", 8192),
        "wrT_sb": ("wpool", 16384), "woT_sb": ("wpool", 24576),
        "wq_sb": ("wpool", 0), "wo_sb": ("wpool", 24576),
        "h_sb": ("apool", 0), "xnG": ("apool", 16384),
        "xnTa": ("apool", 0),
        "xk_a": ("apool", 4232), "xv_a": ("apool", 8328), "xr_a": ("apool", 12424),
        "r_a": ("apool", 16520), "y_a": ("apool", 20616), "dbuf": ("apool", 24712),
        "qT": ("apool", 4128), "kT": ("apool", 8256),
        "vA": ("apool", 12384), "vB": ("apool", 16480),
        "Pb": ("apool", 20576), "PTb": ("apool", 21600), "OT": ("apool", 22624),
        "acc": ("fpool", 0),
        "E1": ("fpool", 0), "K1": ("fpool", 512), "EV": ("fpool", 1024),
        "KV": ("fpool", 1536), "Abuf": ("fpool", 2048), "Pbuf": ("fpool", 2564),
        "dnb": ("fpool", 3080), "recb": ("fpool", 3592), "wkvb": ("fpool", 4104),
        "Pf": ("fpool", 0), "ybst": ("fpool", 1024),
        "yb3_0": ("fpool", 2048), "yb3_1": ("fpool", 3072),
        "ln_sqd": ("fpool", 9216),
        "xgT": ("apool", 0), "h_sg": ("apool", 12288),
        "rowb0": ("apool", 20480), "rowb1": ("apool", 21520),
        "grow0": ("apool", 22560), "grow1": ("apool", 23600),
        "tokhl": ("apool", 24640),
        "ysb0": ("fpool", 0), "ysb1": ("fpool", 1024),
        "rank_sb": ("fpool", 2048), "cum": ("fpool", 2056),
        "selA": ("fpool", 2064), "selB": ("fpool", 2072),
        "scr8": ("fpool", 2080), "ercol": ("fpool", 2088),
        "slotf": ("fpool", 2094), "ecap8": ("fpool", 2096),
        "iota8f": ("fpool", 2104), "mwc": ("fpool", 2368),
        "tokf": ("fpool", 2400), "onescol2": ("fpool", 2136),
        "gsb": ("fpool", 2140), "m8": ("fpool", 2148), "esb": ("fpool", 2156),
        "selsb": ("fpool", 2164), "srow": ("fpool", 2172),
        "rkst0": ("fpool", 2180), "rkst1": ("fpool", 2188),
        "cb_row": ("fpool", 2196), "negcap": ("fpool", 2204),
        "nthr_sb": ("fpool", 2212), "ag_sb": ("fpool", 2220),
        "corelt_sb": ("fpool", 2228), "cnt_sb": ("fpool", 2236),
        "pref_sb": ("fpool", 2268), "nstag0": ("fpool", 2301),
        "nstag1": ("fpool", 2309), "keep_sb": ("fpool", 2317),
        "maskw_bf": ("apool", 24704),
    }
    POOLS = {"wpool": (BF16, 32768), "apool": (BF16, 25344), "fpool": (F32, 10240)}
    PSMAP = {
        "ffn": {"ps_g": "f0", "ps_h0": "f1", "ps_h1": "f2", "ps_y0": "f3",
                "ps_y1": "f4", "ps_cnt": "f5", "ps_rk": "f5",
                "ps_tr": "b0", "ps_mwt": "b1"},
        "l1a": {"ps_k": "f0", "ps_v": "f1", "ps_r": "f2", "ps_o0": "f3",
                "ps_o1": "f4", "ps_tr": "b0"},
        "l0a": {"ps_s0": "f0", "ps_s1": "f1", "ps_av0": "f2", "ps_av1": "f3",
                "ps_y0": "f4", "ps_y1": "f5", "ps_tr": "b0", "ps_pt": "b1"},
    }

    def __init__(self, nc, ctx):
        self.nc = nc
        self.ctx = ctx
        self.pools = {k: ctx.enter_context(nc.sbuf_tensor(k, [128, n], dt))
                      for k, (dt, n) in self.POOLS.items()}
        self.banks = {}
        for i in range(6):
            self.banks[f"f{i}"] = ctx.enter_context(
                nc.psum_tensor(f"bank_f{i}", [128, 512], F32))
        for i in range(2):
            self.banks[f"b{i}"] = ctx.enter_context(
                nc.psum_tensor(f"bank_b{i}", [128, 1024], BF16))
        self.memo = {}
        self.kind = None
        self.gp_reg = ctx.enter_context(nc.gpsimd.register("sprs_r0"))

    def buf(self, name, shape, dt):
        if name in self.ALIAS:
            pool, start = self.ALIAS[name]
            pdt, pn = self.POOLS[pool]
            assert dt == pdt and shape[0] <= 128 and start + shape[1] <= pn, \
                (name, shape, dt)
            return self.pools[pool][0:shape[0], start:start + shape[1]]
        if name in self.memo:
            t, s, d = self.memo[name]
            assert s == tuple(shape) and d == dt, (name, shape, s)
            return t
        t = self.ctx.enter_context(self.nc.sbuf_tensor(name, shape, dt))
        self.memo[name] = (t, tuple(shape), dt)
        return t

    def psum(self, name, shape, dt):
        key = self.PSMAP[self.kind][name]
        b = self.banks[key]
        assert (dt == F32 and key.startswith("f") and shape == [128, 512]) or \
               (dt == BF16 and key.startswith("b") and shape == [128, 1024]), \
               (name, shape, dt)
        return b


def record_ffn(S, dr, B, layer, split="all", debug=False, sparse=False):
    assert not debug
    H = 2048 if layer == 0 else 4096
    NFB = H // 2048
    FCB = 16
    G = 1024
    NG = T // G
    GT = G // 128
    NFI = H // 128
    MWSCALE = 0.5 if layer == 0 else 0.0625
    ACTF = AF.Gelu if layer == 0 else AF.Relu
    CAPTHR = 10239.5
    do_p1 = split in ("all", "count", "cnt1")
    do_p2 = split in ("all", "main")
    cap = (layer == 0)
    do_counts = cap or split in ("count", "cnt1")

    do_p1 = split in ("all", "count", "cnt1")
    do_p2 = split in ("all", "main")
    cap = (layer == 0)
    do_counts = cap or split in ("count", "cnt1")
    nc = S.nc
    sbf = B.buf
    psf = B.psum
    c_ident = dr["c_ident"]
    x_in = dr.get("x_in")
    x_out = dr.get("x_out")
    wg = dr.get("wg")
    rb = dr.get("rb")
    w1 = dr.get("w1")
    b1T = dr.get("b1T")
    w2 = dr.get("w2")
    b2 = dr.get("b2")
    xnT_hbm = dr.get("xnT")
    mw_in = dr.get("mw_in"); sel_in = dr.get("sel_in")
    agcnt = dr.get("agcnt"); pref_in = dr.get("pref_in")
    c_ltri = dr.get("c_ltri"); c_corelt = dr.get("c_corelt")
    nthr_hbm = dr.get("nthr")
    mw_out = dr.get("mw_out"); sel_out = dr.get("sel_out")
    pref_out = dr.get("pref_out")
    if True:

        ident_bf = sbf("ident_bf2", [128, 128], BF16)
        S.step("gpsimd", lambda e: e.dma_start(ident_bf[:], c_ident[:, :]),
               writes=("ident_bf",), dma=True)
        maskw = sbf("maskw", [128, NT * E], F32)
        maskw_bf = sbf("maskw_bf", [128, 8], BF16)
        maskwT = sbf("maskwT", [8, T], BF16)
        ps_g = psf("ps_g", [128, 512], F32)
        ps_mwt = psf("ps_mwt", [128, 1024], BF16)

        if do_p1:
            lnb = alloc_ln_bufs(sbf, psf)
            lnb["ident_bf"] = ident_bf
            wg_sb = sbf("wg_sb", [128, DC * E], BF16)
            gsb = sbf("gsb", [128, 8], F32)
            m8 = sbf("m8", [128, 8], F32)
            srow = sbf("srow", [128, 4], F32)
            esb = sbf("esb", [128, 8], F32)
            selsb = sbf("selsb", [128, 8], F32)
            S.step("vector", lambda e: e.memset(lnb["epsc"][:], 1e-5),
                   writes=("lneps",))
            S.step("gpsimd", lambda e: e.dma_start(
                wg_sb[:].rearrange("p (g e) -> p g e", e=E),
                wg[:, :].rearrange("(g p) e -> p g e", p=128)),
                writes=("wg_sb",), dma=True)
            if do_counts:
                ones_r = sbf("ones_r", [1, 128], F32)
                sel_a = sbf("sel_a", [128, NT * E], F32)
                onescol = sbf("onescol", [128, 1], F32)
                cnt_sb = sbf("cnt_sb", [8, NT], F32)
                pref_sb = sbf("pref_sb", [8, NT + 1], F32)
                ps_cnt = psf("ps_cnt", [128, 512], F32)
                S.step("vector", lambda e: e.memset(ones_r[:], 1.0), writes=("ones_r",))
                S.step("vector", lambda e: e.memset(onescol[:], 1.0), writes=("onescol",))
            if cap:
                rb_sb = sbf("rb_sb", [1, E], F32)
                S.step("sync", lambda e: e.dma_start(rb_sb[:], rb[:, :]),
                       writes=("rb_sb",), dma=True)

            for c in range(NT):
                tk_st = record_ln_tile(S, c, lnb, x_in)
                stage = lnb["stage"][c % 2]
                dst = xnT_hbm[:, c * 128:(c + 1) * 128].rearrange("(g p) k -> p g k", p=128)
                S.step("sync", lambda e, dst=dst, stage=stage: e.dma_start(
                    dst, stage[:].rearrange("p (g k) -> p g k", k=128)),
                    reads=(tk_st,), writes=(("xnT", c),), dma=True, fam=f"xnT{c % 2}")

                def gmm(e, stage=stage):
                    for g in range(DC):
                        i = e.matmul(ps_g[:, 0:8], stage[:, g * 128:(g + 1) * 128],
                                     wg_sb[:, g * E:(g + 1) * E],
                                     start=(g == 0), stop=(g == DC - 1 and not cap))
                    if cap:
                        i = e.matmul(ps_g[:, 0:8], ones_r[:], rb_sb[:],
                                     start=False, stop=True)
                    return i
                rds = (tk_st, "wg_sb") + (("ones_r", "rb_sb") if cap else ())
                S.step("tensor", gmm, reads=rds, writes=("ps_g",))
                S.step("vector", lambda e: e.tensor_copy(gsb[:], ps_g[:, 0:8]),
                       reads=("ps_g",), writes=("gsb",))
                S.step("vector", lambda e: e.max(out=m8[:], in_=gsb[:]),
                       reads=("gsb",), writes=("m8",))
                S.step("vector", lambda e: e.tensor_scalar_mul(srow[:, 0:1], m8[:, 0:1], -1.0),
                       reads=("m8",), writes=("negm1",))
                S.step("scalar", lambda e: e.activation(esb[:], gsb[:], AF.Exp,
                                                        bias=srow[:, 0:1]),
                       reads=("gsb", "negm1"), writes=("esb",))
                S.step("vector", lambda e: e.tensor_scalar(selsb[:], gsb[:], m8[:, 1:2],
                                                           None, ALU.subtract),
                       reads=("gsb", "m8"), writes=("selsb",))
                S.step("scalar", lambda e: e.activation(selsb[:], selsb[:], AF.Sign),
                       reads=("selsb",), writes=("selsb",))
                S.step("vector", lambda e: e.tensor_scalar(selsb[:], selsb[:], 1.0, 1.0,
                                                           ALU.add, ALU.min),
                       reads=("selsb",), writes=("selsb",))
                S.step("vector", lambda e: e.tensor_mul(esb[:], esb[:], selsb[:]),
                       reads=("esb", "selsb"), writes=("esb",))
                S.step("vector", lambda e: e.reduce_sum(srow[:, 1:2], esb[:], axis=AX.X),
                       reads=("esb",), writes=("essum",))
                S.step("vector", lambda e: e.reciprocal(srow[:, 2:3], srow[:, 1:2]),
                       reads=("essum",), writes=("esrs",))
                S.step("vector", lambda e, c=c: e.tensor_scalar(
                    maskw[:, c * E:(c + 1) * E], esb[:], srow[:, 2:3], MWSCALE,
                    ALU.mult, ALU.mult),
                    reads=("esb", "esrs"), writes=(("maskw", c),))
                if dr.get("xntok") is not None and split in ("count", "cnt1"):
                    xnt = lnb["xn"][c % 2]
                    S.step("sync", lambda e, c=c, xnt=xnt: e.dma_start(
                        dr["xntok"][c * 128:(c + 1) * 128, :], xnt[:]),
                        reads=(("xn", c % 2),), writes=(("xntok", c),), dma=True,
                        fam=f"xntok{c % 2}")
                if do_counts:
                    S.step("vector", lambda e, c=c: e.tensor_copy(
                        sel_a[:, c * E:(c + 1) * E], selsb[:]),
                        reads=("selsb",), writes=(("sel", c),))
                    S.step("tensor", lambda e, c=c: e.matmul(
                        ps_cnt[0:8, c:c + 1], sel_a[:, c * E:(c + 1) * E], onescol[:],
                        start=(c == 0), stop=(c == NT - 1), skip_group_check=True),
                        reads=(("sel", c), "onescol"), writes=(("ps_cnt", c),))
                if not do_counts:
                    S.step("scalar", lambda e, c=c: e.copy(
                        maskw_bf[:], maskw[:, c * E:(c + 1) * E]),
                        reads=(("maskw", c),), writes=("maskw_bf",))
                    S.step("tensor", lambda e: e.transpose(ps_mwt[0:8, 0:128], maskw_bf[:],
                                                           ident_bf[:]),
                           reads=("maskw_bf", "ident_bf"), writes=("ps_mwt",))
                    S.step("scalar", lambda e, c=c: e.copy(maskwT[0:8, c * 128:(c + 1) * 128],
                                                           ps_mwt[0:8, 0:128]),
                           reads=("ps_mwt",), writes=(("maskwT", c),))
            if do_counts:
                cnt_all = tuple(("ps_cnt", c) for c in range(NT))
                S.step("scalar", lambda e: e.copy(cnt_sb[:], ps_cnt[0:8, 0:NT]),
                       reads=cnt_all, writes=("cnt_sb",))
                S.step("vector", lambda e: e.memset(pref_sb[0:8, 0:1], 0.0),
                       writes=("pref0",))
                S.step("vector", lambda e: e.tensor_tensor_scan(
                    pref_sb[0:8, 1:NT + 1], onescol[0:8, 0:1].to_broadcast([8, NT]),
                    cnt_sb[0:8, :], 0.0, ALU.mult, ALU.add),
                    reads=("cnt_sb", "onescol"), writes=("pref_sb",))

        if split == "count":
            mw_all = tuple(("maskw", c) for c in range(NT))
            S.step("sync", lambda e: e.dma_start(mw_out[:, :], maskw[:]),
                   reads=mw_all, dma=True)
            S.step("sync", lambda e: e.dma_start(sel_out[:, :], sel_a[:]),
                   reads=tuple(("sel", c) for c in range(NT)), dma=True)
            S.step("sync", lambda e: e.dma_start(pref_out[:, :], pref_sb[:]),
                   reads=("pref_sb", "pref0"), dma=True)
            return
        if split == "cnt1":
            S.step("sync", lambda e: e.dma_start(pref_out[:, :], pref_sb[:]),
                   reads=("pref_sb", "pref0"), dma=True)
            ident_f32 = sbf("ident_f32", [128, 128], F32)
            S.step("sync", lambda e: e.dma_start(ident_f32[:], c_ident[:, :]),
                   writes=("ident_f32",), dma=True)
            ps_rk = psf("ps_rk", [128, 512], F32)
            nthr_sb = sbf("nthr_sb", [33, 8], F32)
            ltri_sb = sbf("ltri_sb", [128, 128], F32)
            S.step("sync", lambda e: e.dma_start(ltri_sb[:], dr["c_ltri"][:, :]),
                   writes=("ltri_sb",), dma=True)
            S.step("tensor", lambda e: e.transpose(
                ps_rk[0:33, 0:8], pref_sb[0:8, 0:33], ident_f32[0:8, 0:8]),
                reads=("pref_sb", "pref0", "ident_f32"), writes=("ps_rk",))
            S.step("scalar", lambda e: e.copy(nthr_sb[:], ps_rk[0:33, 0:8]),
                   reads=("ps_rk",), writes=("nthr_sb",))
            S.step("sync", lambda e: e.dma_start(dr["prefT"][0:33, :], nthr_sb[:]),
                   reads=("nthr_sb",), writes=("prefh",), dma=True)
            rank_src = {
                "sel_a": sel_a, "ltri_sb": ltri_sb, "ones_r": ones_r,
                "ps_rk": ps_rk,
                "xn_ap_of": lambda c: (dr["xntok"], ("xntok", c), True),
            }
            record_slot_scatter(S, dr, B, layer,
                                lambda c: (maskw[:, c * E:(c + 1) * E], ("maskw", c)),
                                rank_src, dr["gbuf"], dr["prefT"])
            return

        if split == "main":
            sel_a = sbf("sel_a", [128, NT * E], F32)
            ones_r = sbf("ones_r", [1, 128], F32)
            pref_sb = sbf("pref_sb", [8, NT + 1], F32)
            ag_sb = sbf("ag_sb", [8, 8], F32)
            corelt_sb = sbf("corelt_sb", [8, 8], F32)
            cb_row = sbf("cb_row", [1, 8], F32)
            negcap = sbf("negcap", [1, 8], F32)
            nthr_sb = sbf("nthr_sb", [33, 8], F32)
            nstag = [sbf(f"nstag{i}", [1, 8], F32) for i in range(2)]
            keep_sb = sbf("keep_sb", [128, 8], F32)
            ltri_sb = sbf("ltri_sb", [128, 128], F32)
            ident_f32 = sbf("ident_f32", [128, 128], F32)
            ps_rk = psf("ps_rk", [128, 512], F32)
            S.step("vector", lambda e: e.memset(ones_r[:], 1.0), writes=("ones_r",))
            S.step("sync", lambda e: e.dma_start(maskw[:], mw_in[:, :]),
                   writes=tuple(("maskw", c) for c in range(NT)), dma=True)
            S.step("sync", lambda e: e.dma_start(sel_a[:], sel_in[:, :]),
                   writes=tuple(("sel", c) for c in range(NT)), dma=True)
            S.step("sync", lambda e: e.dma_start(pref_sb[:], pref_in[:, :]),
                   writes=("pref_sb", "pref0"), dma=True)
            S.step("sync", lambda e: e.dma_start(ag_sb[:], agcnt[:, :]),
                   writes=("ag_sb",), dma=True)
            S.step("sync", lambda e: e.dma_start(corelt_sb[:], c_corelt[:, :]),
                   writes=("corelt_sb",), dma=True)
            S.step("sync", lambda e: e.dma_start(ltri_sb[:], c_ltri[:, :]),
                   writes=("ltri_sb",), dma=True)
            S.step("sync", lambda e: e.dma_start(ident_f32[:], c_ident[:, :]),
                   writes=("ident_f32",), dma=True)
            S.step("tensor", lambda e: e.matmul(ps_g[0:1, 0:8], corelt_sb[0:8, 0:1],
                                                ag_sb[0:8, :], start=True, stop=True),
                   reads=("corelt_sb", "ag_sb"), writes=("ps_g",))
            S.step("scalar", lambda e: e.copy(cb_row[:], ps_g[0:1, 0:8]),
                   reads=("ps_g",), writes=("cb_row",))
            S.step("vector", lambda e: e.tensor_scalar(negcap[:], cb_row[:], 1.0,
                                                       -CAPTHR, ALU.mult, ALU.add),
                   reads=("cb_row",), writes=("negcap",))

            S.step("tensor", lambda e: e.transpose(
                ps_rk[0:33, 0:8], pref_sb[0:8, 0:33], ident_f32[0:8, 0:8]),
                reads=("pref_sb", "pref0", "ident_f32"), writes=("ps_rk",))
            S.step("scalar", lambda e: e.copy(nthr_sb[:], ps_rk[0:33, 0:8]),
                   reads=("ps_rk",), writes=("nthr_sb",))
            S.step("tensor", lambda e: e.matmul(
                ps_g[0:33, 0:8], ones_r[0:1, 0:33], negcap[:], start=True, stop=True),
                reads=("negcap", "ones_r"), writes=("ps_g",))
            S.step("vector", lambda e: e.tensor_add(
                nthr_sb[:], nthr_sb[:], ps_g[0:33, 0:8]),
                reads=("nthr_sb", "ps_g"), writes=("nthr_sb",))
            S.step("sync", lambda e: e.dma_start(nthr_hbm[0:33, :], nthr_sb[:]),
                   reads=("nthr_sb",), writes=("nthr_hbm",), dma=True)
            for c in range(NT):
                S.step("gpsimd", lambda e, c=c: e.dma_start(
                    nstag[c % 2][:], nthr_hbm[c:c + 1, :]),
                    reads=("nthr_hbm",), writes=(("nstag", c % 2),), dma=True)

                def rkmm(e, c=c):
                    e.matmul(ps_rk[:, 0:8], ltri_sb[:], sel_a[:, c * E:(c + 1) * E],
                             start=True, stop=False)
                    return e.matmul(ps_rk[:, 0:8], ones_r[:], nstag[c % 2][:],
                                    start=False, stop=True)
                S.step("tensor", rkmm, reads=(("sel", c), ("nstag", c % 2),
                                              "ltri_sb", "ones_r"), writes=("ps_rk",))
                S.step("scalar", lambda e: e.activation(keep_sb[:], ps_rk[:, 0:8],
                                                        AF.Sign),
                       reads=("ps_rk",), writes=("keep_sb",))
                S.step("vector", lambda e: e.tensor_scalar(keep_sb[:], keep_sb[:],
                                                           -1.0, 0.0, ALU.mult, ALU.max),
                       reads=("keep_sb",), writes=("keep_sb",))
                S.step("vector", lambda e, c=c: e.tensor_mul(
                    maskw[:, c * E:(c + 1) * E], maskw[:, c * E:(c + 1) * E],
                    keep_sb[:]),
                    reads=(("maskw", c), "keep_sb"), writes=(("maskw", c),))
                if not sparse:
                    S.step("scalar", lambda e, c=c: e.copy(
                        maskw_bf[:], maskw[:, c * E:(c + 1) * E]),
                        reads=(("maskw", c),), writes=("maskw_bf",))
                    S.step("tensor", lambda e: e.transpose(ps_mwt[0:8, 0:128], maskw_bf[:],
                                                           ident_bf[:]),
                           reads=("maskw_bf", "ident_bf"), writes=("ps_mwt",))
                    S.step("scalar", lambda e, c=c: e.copy(
                        maskwT[0:8, c * 128:(c + 1) * 128], ps_mwt[0:8, 0:128]),
                        reads=("ps_mwt",), writes=(("maskwT", c),))
            if sparse:
                rank_src = {
                    "sel_a": sel_a, "ltri_sb": ltri_sb, "ones_r": ones_r,
                    "ps_rk": ps_rk,
                    "xn_ap_of": lambda c: (dr["xntok"], ("xntok_ext",), True),
                }
                record_slot_scatter(
                    S, dr, B, layer,
                    lambda c: (maskw[:, c * E:(c + 1) * E], ("maskw", c)),
                    rank_src, dr["gbuf"], dr["prefT"])
                record_ffn_sparse_main(S, dr, B, layer)
                return

        b1T_sb = sbf(f"b1T_sb{layer}", [128, NFI * E], F32)
        b2_bf = sbf("b2_bf", [8, D], BF16)
        xnG = sbf("xnG", [128, DC * G], BF16)
        h_sb = sbf("h_sb", [128, FCB * G], BF16)
        w1_sb = sbf("w1_sb", [128, DC * FCB * 128], BF16)
        w2_sb = sbf("w2_sb", [128, FCB * D], BF16)
        acc = sbf("acc", [128, GT * D], F32)
        xres = [sbf(f"xres{i}", [128, D], F32) for i in range(2)]
        xo = [sbf(f"xo{i}", [128, D], F32) for i in range(2)]
        ps_h = [psf(f"ps_h{i}", [128, 512], F32) for i in range(2)]
        ps_y = [psf(f"ps_y{i}", [128, 512], F32) for i in range(2)]
        S.step("gpsimd", lambda e: e.dma_start(b2_bf[:], b2[:, :]),
               writes=("b2_bf",), dma=True)
        S.step("sync", lambda e: e.dma_start(b1T_sb[:], b1T[:, :]),
               writes=("b1T_sb",), dma=True)

        mwT_all = tuple(("maskwT", c) for c in range(NT))
        if debug:
            S.step("sync", lambda e: e.dma_start(dbg_mw[:, :], maskw[:]),
                   reads=tuple(("maskw", c) for c in range(NT)), dma=True)
        xnT_rd = (lambda c: ("xnT", c)) if split != "main" else (lambda c: "xnT_ext")
        if split == "main":
            S.step("vector", lambda e: e.engine_nop(), reads=(), writes=("xnT_ext",))

        for g4 in range(NG):
            src = xnT_hbm[:, g4 * G:(g4 + 1) * G].rearrange("(g p) t -> p g t", p=128)
            S.step("sync", lambda e, src=src: e.dma_start(
                xnG[:].rearrange("p (g t) -> p g t", t=G), src),
                reads=tuple(xnT_rd(c) for c in range(g4 * GT, (g4 + 1) * GT)),
                writes=("xnG",), dma=True)
            for g8 in range(GT):
                c = g4 * GT + g8
                for dt in range(2):
                    S.step("tensor", lambda e, c=c, dt=dt: e.matmul(
                        ps_y[dt][:], maskwT[0:8, c * 128:(c + 1) * 128],
                        b2_bf[0:8, dt * 512:(dt + 1) * 512], start=True, stop=True),
                        reads=mwT_all + ("b2_bf",), writes=(f"ps_y{dt}",))
                    S.step("scalar", lambda e, g8=g8, dt=dt: e.copy(
                        acc[:, g8 * D + dt * 512: g8 * D + (dt + 1) * 512], ps_y[dt][:]),
                        reads=(f"ps_y{dt}",), writes=(("acc", g8, dt),))
            for e_i in range(E):
                for fb in range(NFB):
                    src1 = w1[e_i, :, fb * 2048:(fb + 1) * 2048].rearrange(
                        "(dc p) (fc k) -> p dc fc k", p=128, k=128)
                    S.step("gpsimd", lambda e, src1=src1: e.dma_start(
                        w1_sb[:].rearrange("p (dc fc k) -> p dc fc k", fc=FCB, k=128),
                        src1),
                        writes=("w1_sb",), dma=True)
                    src2 = w2[e_i, fb * 2048:(fb + 1) * 2048, :].rearrange(
                        "(fc p) d -> p fc d", p=128)
                    S.step("gpsimd", lambda e, src2=src2: e.dma_start(
                        w2_sb[:].rearrange("p (fc d) -> p fc d", d=D), src2),
                        writes=("w2_sb",), dma=True)
                    for fc in range(FCB):
                        for tb in range(2):
                            def hmm(e, fc=fc, tb=tb):
                                for dc in range(DC):
                                    i = e.matmul(
                                        ps_h[tb][:],
                                        w1_sb[:, (dc * FCB + fc) * 128:
                                              (dc * FCB + fc + 1) * 128],
                                        xnG[:, dc * G + tb * 512: dc * G + (tb + 1) * 512],
                                        start=(dc == 0), stop=(dc == DC - 1))
                                return i
                            S.step("tensor", hmm, reads=("w1_sb", "xnG"),
                                   writes=(f"ps_h{tb}",))
                            bcol = (fb * FCB + fc) * E + e_i
                            S.step("scalar", lambda e, fc=fc, tb=tb, bcol=bcol:
                                   e.activation(
                                       h_sb[:, fc * G + tb * 512: fc * G + (tb + 1) * 512],
                                       ps_h[tb][:], ACTF,
                                       bias=b1T_sb[:, bcol:bcol + 1]),
                                   reads=(f"ps_h{tb}", "b1T_sb"),
                                   writes=(("h", fc, tb),))
                    h_all = tuple(("h", fc, tb) for fc in range(FCB) for tb in range(2))
                    for g8 in range(GT):
                        for dt in range(2):
                            def ymm(e, g8=g8, dt=dt):
                                for fc in range(FCB):
                                    i = e.matmul(
                                        ps_y[dt][:],
                                        h_sb[:, fc * G + g8 * 128: fc * G + g8 * 128 + 128],
                                        w2_sb[:, fc * D + dt * 512: fc * D + (dt + 1) * 512],
                                        start=(fc == 0), stop=(fc == FCB - 1))
                                return i
                            S.step("tensor", ymm, reads=h_all + ("w2_sb",),
                                   writes=(f"ps_y{dt}",))
                            mcol = (g4 * GT + g8) * E + e_i
                            S.step("vector", lambda e, g8=g8, dt=dt, mcol=mcol:
                                   e.scalar_tensor_tensor(
                                       acc[:, g8 * D + dt * 512: g8 * D + (dt + 1) * 512],
                                       ps_y[dt][:], maskw[:, mcol:mcol + 1],
                                       acc[:, g8 * D + dt * 512: g8 * D + (dt + 1) * 512],
                                       ALU.mult, ALU.add),
                                   reads=(f"ps_y{dt}", ("maskw", g4 * GT + g8),
                                          ("acc", g8, dt)),
                                   writes=(("acc", g8, dt),))
            if debug and g4 == 0:
                S.step("sync", lambda e: e.dma_start(dbg_acc[:, :], acc[:]),
                       reads=tuple(("acc", g8, dt) for g8 in range(GT) for dt in range(2)),
                       dma=True)
            for g8 in range(GT):
                c = g4 * GT + g8
                S.step("sync", lambda e, c=c: e.dma_start(
                    xres[c % 2][:], x_in[c * 128:(c + 1) * 128, :]),
                    reads=(("x_dram", c),), writes=(("xres", c % 2),), dma=True)
                S.step("vector", lambda e, c=c, g8=g8: e.tensor_add(
                    xo[c % 2][:], xres[c % 2][:], acc[:, g8 * D:(g8 + 1) * D]),
                    reads=(("xres", c % 2), ("acc", g8, 0), ("acc", g8, 1)),
                    writes=(("xo", c % 2),))
                S.step("sync", lambda e, c=c: e.dma_start(
                    x_out[c * 128:(c + 1) * 128, :], xo[c % 2][:]),
                    reads=(("xo", c % 2),), writes=(("xout", c),), dma=True, fam=f"xout{c % 2}")







def record_l1a(S, dr, B, debug=False):
    assert not debug
    TC = 512
    NCH = T // TC
    W5 = TC + 1

    nc = S.nc
    sbf = B.buf
    psf = B.psum
    c_ident = dr["c_ident"]
    x_in = dr["x_in"]
    x_out = dr["x_out"]
    wkT = dr["wkT"]; wvT = dr["wvT"]; wrT = dr["wrT"]; woT = dr["woT"]
    tmk5 = dr["tmk5"]; tmv5 = dr["tmv5"]; tmr5 = dr["tmr5"]
    dec5 = dr["dec5"]; fst5 = dr["fst5"]
    if True:

        lnb = alloc_ln_bufs(sbf, psf)
        S.step("gpsimd", lambda e: e.dma_start(lnb["ident_bf"][:], c_ident[:, :]),
               writes=("ident_bf",), dma=True)
        xnTa = sbf("xnTa", [128, DC * W5], BF16)

        def evac_dsts(c):
            ti = c % 4
            dsts = [xnTa[:, g * W5 + 1 + ti * 128: g * W5 + 1 + (ti + 1) * 128]
                    for g in range(DC)]
            return dsts, ("xnTa", c // 4)
        lnb["evac_dsts"] = evac_dsts
        lnb["evac_extra_reads"] = ("halo",)

        wT_sb = {}
        for nm, src in (("k", wkT), ("v", wvT), ("r", wrT), ("o", woT)):
            wT_sb[nm] = sbf(f"w{nm}T_sb", [128, DC * D], BF16)
            S.step("gpsimd", lambda e, nm=nm, src=src: e.dma_start(
                wT_sb[nm][:].rearrange("p (g e) -> p g e", e=D),
                src[:, :].rearrange("(g p) e -> p g e", p=128)),
                writes=(f"w{nm}T",), dma=True)
        tm_sb = {}
        for nm, src in (("k", tmk5), ("v", tmv5), ("r", tmr5), ("d", dec5), ("u", fst5)):
            tm_sb[nm] = sbf(f"tm{nm}_sb", [128, DC], F32)
            S.step("sync", lambda e, nm=nm, src=src: e.dma_start(tm_sb[nm][:], src[:, :]),
                   writes=(f"tm{nm}",), dma=True)
        ew = sbf("ew", [128, DC], F32)
        S.step("scalar", lambda e: e.activation(ew[:], tm_sb["d"][:], AF.Exp),
               reads=("tmd",), writes=("ew",))
        S.step("vector", lambda e: e.memset(lnb["epsc"][:], 1e-5), writes=("lneps",))

        Ast = sbf("Ast", [128, DC], F32)
        Pst = sbf("Pst", [128, DC], F32)
        S.step("vector", lambda e: e.memset(Ast[:], 0.0), writes=("Ast",))
        S.step("vector", lambda e: e.memset(Pst[:], 0.0), writes=("Pst",))
        S.step("vector", lambda e: e.memset(
            xnTa[:].rearrange("p (g c) -> p g c", c=W5)[:, :, 0:1], 0.0),
            writes=("halo",))

        dbuf = sbf("dbuf", [128, TC], BF16)
        xk_a = sbf("xk_a", [128, DC * TC], BF16)
        xv_a = sbf("xv_a", [128, DC * TC], BF16)
        xr_a = sbf("xr_a", [128, DC * TC], BF16)
        E1 = sbf("E1", [128, TC], F32)
        K1 = sbf("K1", [128, TC], F32)
        EV = sbf("EV", [128, TC], F32)
        KV = sbf("KV", [128, TC], F32)
        Abuf = sbf("Abuf", [128, W5], F32)
        Pbuf = sbf("Pbuf", [128, W5], F32)
        dnb = sbf("dnb", [128, TC], F32)
        recb = sbf("recb", [128, TC], F32)
        wkvb = sbf("wkvb", [128, TC], F32)
        r_a = sbf("r_a", [128, DC * TC], BF16)
        y_a = sbf("y_a", [128, DC * TC], BF16)
        xres = [sbf(f"xres{i}", [128, D], F32) for i in range(2)]
        xo = [sbf(f"xo{i}", [128, D], F32) for i in range(2)]

        ps_k = psf("ps_k", [128, 512], F32)
        ps_v = psf("ps_v", [128, 512], F32)
        ps_r = psf("ps_r", [128, 512], F32)
        ps_o = [psf(f"ps_o{i}", [128, 512], F32) for i in range(2)]

        for q in range(NCH):
            for ti in range(4):
                record_ln_tile(S, q * 4 + ti, lnb, x_in)
            tkx = ("xnTa", q)
            for dc in range(DC):
                sl = xnTa[:, dc * W5: (dc + 1) * W5]
                S.step("vector", lambda e, sl=sl: e.tensor_sub(
                    dbuf[:], sl[:, 1:W5], sl[:, 0:TC]),
                    reads=(tkx, "halo"), writes=("dbuf",))
                for nm, arr in (("k", xk_a), ("v", xv_a), ("r", xr_a)):
                    S.step("vector", lambda e, sl=sl, nm=nm, arr=arr, dc=dc:
                           e.scalar_tensor_tensor(
                               arr[:, dc * TC:(dc + 1) * TC], dbuf[:],
                               tm_sb[nm][:, dc:dc + 1], sl[:, 0:TC],
                               ALU.mult, ALU.add),
                           reads=("dbuf", f"tm{nm}", tkx, "halo"),
                           writes=((f"x{nm}", dc),))
            if q < NCH - 1:
                def haloc(e):
                    for dc in range(DC):
                        i = e.tensor_copy(xnTa[:, dc * W5: dc * W5 + 1],
                                          xnTa[:, dc * W5 + TC: dc * W5 + TC + 1])
                    return i
                S.step("vector", haloc, reads=(tkx,), writes=("halo",))
            xk_all = tuple(("xk", dc) for dc in range(DC))
            xv_all = tuple(("xv", dc) for dc in range(DC))
            xr_all = tuple(("xr", dc) for dc in range(DC))
            for ec in range(DC):
                def kmm(e, ec=ec, arr=xk_a, w="k", ps=ps_k):
                    for dc in range(DC):
                        i = e.matmul(ps[:], wT_sb[w][:, dc * D + ec * 128:
                                                     dc * D + (ec + 1) * 128],
                                     arr[:, dc * TC:(dc + 1) * TC],
                                     start=(dc == 0), stop=(dc == DC - 1))
                    return i
                S.step("tensor", lambda e, ec=ec: kmm(e, ec, xk_a, "k", ps_k),
                       reads=xk_all + ("wkT",), writes=("ps_k",))
                S.step("tensor", lambda e, ec=ec: kmm(e, ec, xv_a, "v", ps_v),
                       reads=xv_all + ("wvT",), writes=("ps_v",))
                S.step("tensor", lambda e, ec=ec: kmm(e, ec, xr_a, "r", ps_r),
                       reads=xr_all + ("wrT",), writes=("ps_r",))
                S.step("scalar", lambda e, ec=ec: e.activation(
                    E1[:], ps_k[:], AF.Exp, bias=tm_sb["u"][:, ec:ec + 1]),
                    reads=("ps_k", "tmu"), writes=("E1",))
                S.step("scalar", lambda e: e.activation(K1[:], ps_k[:], AF.Exp),
                       reads=("ps_k",), writes=("K1",))
                S.step("vector", lambda e: e.tensor_mul(EV[:], E1[:], ps_v[:]),
                       reads=("E1", "ps_v"), writes=("EV",))
                S.step("vector", lambda e: e.tensor_mul(KV[:], K1[:], ps_v[:]),
                       reads=("K1", "ps_v"), writes=("KV",))
                S.step("scalar", lambda e, ec=ec: e.activation(
                    r_a[:, ec * TC:(ec + 1) * TC], ps_r[:], AF.Sigmoid),
                    reads=("ps_r",), writes=(("r", ec),))
                S.step("vector", lambda e, ec=ec: e.tensor_copy(
                    Abuf[:, 0:1], Ast[:, ec:ec + 1]), reads=("Ast",), writes=("Abuf0",))
                S.step("vector", lambda e, ec=ec: e.tensor_tensor_scan(
                    Abuf[:, 1:W5], ew[:, ec:ec + 1].to_broadcast([128, TC]), KV[:],
                    Ast[:, ec:ec + 1], ALU.mult, ALU.add),
                    reads=("KV", "ew", "Ast"), writes=("Abuf",))
                S.step("vector", lambda e, ec=ec: e.tensor_copy(
                    Ast[:, ec:ec + 1], Abuf[:, TC:W5]),
                    reads=("Abuf",), writes=("Ast",))
                S.step("vector", lambda e, ec=ec: e.tensor_copy(
                    Pbuf[:, 0:1], Pst[:, ec:ec + 1]), reads=("Pst",), writes=("Pbuf0",))
                S.step("vector", lambda e, ec=ec: e.tensor_tensor_scan(
                    Pbuf[:, 1:W5], ew[:, ec:ec + 1].to_broadcast([128, TC]), K1[:],
                    Pst[:, ec:ec + 1], ALU.mult, ALU.max),
                    reads=("K1", "ew", "Pst"), writes=("Pbuf",))
                S.step("vector", lambda e, ec=ec: e.tensor_copy(
                    Pst[:, ec:ec + 1], Pbuf[:, TC:W5]),
                    reads=("Pbuf",), writes=("Pst",))
                S.step("vector", lambda e: e.tensor_add(dnb[:], Pbuf[:, 0:TC], E1[:]),
                       reads=("Pbuf", "Pbuf0", "E1"), writes=("dnb",))
                S.step("vector", lambda e: e.reciprocal(recb[:], dnb[:]),
                       reads=("dnb",), writes=("recb",))
                S.step("vector", lambda e: e.tensor_add(EV[:], Abuf[:, 0:TC], EV[:]),
                       reads=("Abuf", "Abuf0", "EV"), writes=("EV",))
                S.step("vector", lambda e: e.tensor_mul(wkvb[:], EV[:], recb[:]),
                       reads=("EV", "recb"), writes=("wkvb",))
                S.step("vector", lambda e, ec=ec: e.tensor_mul(
                    y_a[:, ec * TC:(ec + 1) * TC], wkvb[:],
                    r_a[:, ec * TC:(ec + 1) * TC]),
                    reads=("wkvb", ("r", ec)), writes=(("y", ec),))
                if debug and q == 0 and ec == 0:
                    S.step("gpsimd", lambda e: e.dma_start(
                        dbg_ry[:, 0:512], r_a[:, 0:512]), reads=(("r", 0),), dma=True)
                    S.step("gpsimd", lambda e: e.dma_start(
                        dbg_ry[:, 512:1024], y_a[:, 0:512]), reads=(("y", 0),), dma=True)
                    for j, (nm, t) in enumerate(
                            (("E1", E1), ("K1", K1), ("EV", EV), ("KV", KV),
                             ("wkv", wkvb), ("dn", dnb))):
                        S.step("sync", lambda e, j=j, t=t: e.dma_start(
                            dbg_kv[:, j * 512:(j + 1) * 512], t[:]),
                            reads=(nm if nm in ("E1", "K1", "KV", "dn") else
                                   ("EV" if nm == "EV" else "wkvb"),), dma=True)
            y_all = tuple(("y", ec) for ec in range(DC))
            if debug and q == 0:
                S.step("gpsimd", lambda e: e.dma_start(dbg_ya[:, :], y_a[:]),
                       reads=y_all, dma=True)
            for ti in range(4):
                c = q * 4 + ti
                S.step("sync", lambda e, c=c, ti=ti: e.dma_start(
                    xres[ti % 2][:], x_in[c * 128:(c + 1) * 128, :]),
                    reads=(("x_dram", c),), writes=(("xres", ti % 2),), dma=True)
                for eh in range(2):
                    def omm(e, ti=ti, eh=eh):
                        for dc in range(DC):
                            i = e.matmul(ps_o[eh][:],
                                         y_a[:, dc * TC + ti * 128: dc * TC + (ti + 1) * 128],
                                         wT_sb["o"][:, dc * D + eh * 512:
                                                    dc * D + (eh + 1) * 512],
                                         start=(dc == 0), stop=(dc == DC - 1))
                        return i
                    S.step("tensor", omm, reads=y_all + ("woT",),
                           writes=(f"ps_o{eh}",))
                    S.step("vector", lambda e, ti=ti, eh=eh: e.tensor_add(
                        xo[ti % 2][:, eh * 512:(eh + 1) * 512],
                        xres[ti % 2][:, eh * 512:(eh + 1) * 512], ps_o[eh][:]),
                        reads=(f"ps_o{eh}", ("xres", ti % 2)),
                        writes=(("xo", ti % 2, eh),))
                S.step("sync", lambda e, c=c, ti=ti: e.dma_start(
                    x_out[c * 128:(c + 1) * 128, :], xo[ti % 2][:]),
                    reads=(("xo", ti % 2, 0), ("xo", ti % 2, 1)),
                    writes=(("xout", c),), dma=True, fam=f"xout{ti % 2}")
                if debug and q == 0 and ti == 0:
                    S.step("sync", lambda e: e.dma_start(dbg_xr[:, 0:D], xres[0][:]),
                           reads=(("xres", 0),), dma=True)
                    S.step("sync", lambda e: e.dma_start(dbg_xr[:, D:2*D], xo[0][:]),
                           reads=(("xo", 0, 0), ("xo", 0, 1)), dma=True)



def _l1a_inputs(x_shard, inputs, core_id):
    cons = host_consts(core_id)
    def five(v):
        return np.ascontiguousarray(
            np.asarray(v, np.float32).reshape(DC, 128).T)
    return {
        "x_in": x_shard,
        "wkT": np.ascontiguousarray(np.asarray(inputs["L1_attn_W_k"], np.float32).T),
        "wvT": np.ascontiguousarray(np.asarray(inputs["L1_attn_W_v"], np.float32).T),
        "wrT": np.ascontiguousarray(np.asarray(inputs["L1_attn_W_r"], np.float32).T),
        "woT": np.ascontiguousarray(np.asarray(inputs["L1_attn_W_o"], np.float32).T),
        "tmk5": five(inputs["L1_attn_time_mix_k"]),
        "tmv5": five(inputs["L1_attn_time_mix_v"]),
        "tmr5": five(inputs["L1_attn_time_mix_r"]),
        "dec5": five(inputs["L1_attn_time_decay"]),
        "fst5": five(inputs["L1_attn_time_first"]),
        "c_ident": cons["c_ident"],
    }




def record_l0a(S, dr, B, debug=False):
    assert not debug
    TC = 512
    NCH = T // TC
    WA = TC + 4
    NH = 8
    SCALE = D ** -0.5

    nc = S.nc
    sbf = B.buf
    psf = B.psum
    c_ident = dr["c_ident"]
    c_bmask = dr["c_bmask"]
    x_in = dr["x_in"]
    x_out = dr["x_out"]
    wqkv = dr["wqkv"]; wout = dr["wout"]; bout = dr["bout"]
    xnT_hbm = dr["xnT"]
    yB_hbm = dr["yB"]
    if True:

        lnb = alloc_ln_bufs(sbf, psf)
        S.step("gpsimd", lambda e: e.dma_start(lnb["ident_bf"][:], c_ident[:, :]),
               writes=("ident_bf",), dma=True)
        S.step("vector", lambda e: e.memset(lnb["epsc"][:], 1e-5), writes=("lneps",))

        wq_sb = sbf("wq_sb", [128, DC * 3 * D], BF16)
        wo_sb = sbf("wo_sb", [128, DC * D], BF16)
        bout_sb = sbf("bout_sb", [1, D], F32)
        ones_r = sbf("ones_r", [1, 128], F32)
        bm_sb = sbf("bm_sb", [128, 128], F32)
        S.step("gpsimd", lambda e: e.dma_start(
            wq_sb[:].rearrange("p (g f) -> p g f", f=3 * D),
            wqkv[:, :].rearrange("(g p) f -> p g f", p=128)),
            writes=("wq_sb",), dma=True)
        S.step("gpsimd", lambda e: e.dma_start(
            wo_sb[:].rearrange("p (g f) -> p g f", f=D),
            wout[:, :].rearrange("(g p) f -> p g f", p=128)),
            writes=("wo_sb",), dma=True)
        S.step("sync", lambda e: e.dma_start(bout_sb[:], bout[:, :]),
               writes=("bout_sb",), dma=True)
        S.step("sync", lambda e: e.dma_start(bm_sb[:], c_bmask[:, :]),
               writes=("bm_sb",), dma=True)
        S.step("vector", lambda e: e.memset(ones_r[:], 1.0), writes=("ones_r",))

        xnTa = sbf("xnTa", [128, DC * WA], BF16)
        qT = sbf("qT", [128, NH * WA], BF16)
        kT = sbf("kT", [128, NH * WA], BF16)
        vA = sbf("vA", [128, 4 * D], BF16)
        vB = sbf("vB", [128, 4 * D], BF16)
        Pf = sbf("Pf", [128, NH * 128], F32)
        nmax = sbf("nmax", [128, NH], F32)
        dsum = sbf("dsum", [128, NH], F32)
        rden = sbf("rden", [128, NH], F32)
        Pb = sbf("Pb", [128, NH * 128], BF16)
        PTb = sbf("PTb", [128, NH * 128], BF16)
        OT = sbf("OT", [128, NH * 128], BF16)
        ybst = sbf("ybst", [128, D], F32)
        xres = [sbf(f"xres{i}", [128, D], F32) for i in range(2)]
        xo = [sbf(f"xo{i}", [128, D], F32) for i in range(2)]
        yb3 = [sbf(f"yb3_{i}", [128, D], F32) for i in range(2)]

        ps_s = [psf(f"ps_s{i}", [128, 512], F32) for i in range(2)]
        ps_pt = psf("ps_pt", [128, 1024], BF16)
        ps_av = [psf(f"ps_av{i}", [128, 512], F32) for i in range(2)]
        ps_y = [psf(f"ps_y{i}", [128, 512], F32) for i in range(2)]

        for c in range(NT):
            tk_st = record_ln_tile(S, c, lnb, x_in)
            stage = lnb["stage"][c % 2]
            dst = xnT_hbm[:, c * 128:(c + 1) * 128].rearrange("(g p) k -> p g k", p=128)
            S.step("sync", lambda e, dst=dst, stage=stage: e.dma_start(
                dst, stage[:].rearrange("p (g k) -> p g k", k=128)),
                reads=(tk_st,), writes=(("xnT", c),), dma=True, fam=f"xnT{c % 2}")

        S.step("vector", lambda e: e.memset(ybst[:], 0.0), writes=("ybst",))
        S.step("sync", lambda e: e.dma_start(yB_hbm[0:2, :], ybst[0:2, :]),
               reads=("ybst",), writes=("yB",), dma=True)
        S.step("sync", lambda e: e.dma_start(yB_hbm[T - 2:T, :], ybst[0:2, :]),
               reads=("ybst",), writes=("yB",), dma=True)

        for q in range(NCH):
            t0 = q * TC
            w_load = TC + 4 if q < NCH - 1 else TC
            src = xnT_hbm[:, t0:t0 + w_load].rearrange("(g p) t -> p g t", p=128)
            dstv = xnTa[:].rearrange("p (g j) -> p g j", j=WA)[:, :, 0:w_load]
            S.step("sync", lambda e, src=src, dstv=dstv: e.dma_start(dstv, src),
                   reads=tuple(("xnT", c) for c in range(q * 4, min(NT, q * 4 + 5))),
                   writes=("xnTa",), dma=True)
            if q == NCH - 1:
                S.step("vector", lambda e: e.memset(
                    xnTa[:].rearrange("p (g j) -> p g j", j=WA)[:, :, TC:WA], 0.0),
                    reads=(), writes=("xnTa_pad",))
            else:
                S.step("vector", lambda e: e.engine_nop(), writes=("xnTa_pad",))

            for h in range(NH):
                for nm, arr, fof in (("q", qT, 0), ("k", kT, D)):
                    def qkmm(e, h=h, arr=arr, fof=fof, main=True):
                        lo, hi = (0, TC) if main else (TC, WA)
                        for dc in range(DC):
                            i = e.matmul(
                                ps_s[0][:, 0:hi - lo],
                                xnTa[:, dc * WA + lo: dc * WA + hi],
                                wq_sb[:, dc * 3 * D + fof + h * 128:
                                      dc * 3 * D + fof + (h + 1) * 128],
                                start=(dc == 0), stop=(dc == DC - 1))
                        return i
                    def qkmmT(e, h=h, fof=fof, lo=0, hi=TC):
                        for dc in range(DC):
                            i = e.matmul(
                                ps_s[0][:, 0:hi - lo],
                                wq_sb[:, dc * 3 * D + fof + h * 128:
                                      dc * 3 * D + fof + (h + 1) * 128],
                                xnTa[:, dc * WA + lo: dc * WA + hi],
                                start=(dc == 0), stop=(dc == DC - 1))
                        return i
                    S.step("tensor", lambda e, f=qkmmT: f(e),
                           reads=("xnTa", "xnTa_pad", "wq_sb"), writes=("ps_s0",))
                    S.step("scalar", lambda e, h=h, arr=arr: e.copy(
                        arr[:, h * WA: h * WA + TC], ps_s[0][:, 0:TC]),
                        reads=("ps_s0",), writes=((nm + "T", h),))
                    S.step("tensor", lambda e, f=qkmmT: f(e, lo=TC, hi=WA),
                           reads=("xnTa", "xnTa_pad", "wq_sb"), writes=("ps_s0",))
                    S.step("scalar", lambda e, h=h, arr=arr: e.copy(
                        arr[:, h * WA + TC: h * WA + WA], ps_s[0][:, 0:WA - TC]),
                        reads=("ps_s0",), writes=((nm + "T", h),))
            qk_all = tuple(("qT", h) for h in range(NH)) + tuple(("kT", h) for h in range(NH))

            for br, varr, off in (("A", vA, 0), ("B", vB, 2)):
                for ti in range(4):
                    for vh in range(2):
                        def vmm(e, ti=ti, vh=vh, off=off):
                            for dc in range(DC):
                                i = e.matmul(
                                    ps_av[vh][:],
                                    xnTa[:, dc * WA + off + ti * 128:
                                          dc * WA + off + (ti + 1) * 128],
                                    wq_sb[:, dc * 3 * D + 2 * D + vh * 512:
                                          dc * 3 * D + 2 * D + (vh + 1) * 512],
                                    start=(dc == 0), stop=(dc == DC - 1))
                            return i
                        S.step("tensor", vmm, reads=("xnTa", "xnTa_pad", "wq_sb"),
                               writes=(f"ps_av{vh}",))
                        S.step("scalar", lambda e, ti=ti, vh=vh, varr=varr: e.copy(
                            varr[:, ti * D + vh * 512: ti * D + (vh + 1) * 512],
                            ps_av[vh][:]),
                            reads=(f"ps_av{vh}",), writes=((f"v{br}", ti),))

            for br, varr, off in (("A", vA, 0), ("B", vB, 2)):
                for ti in range(4):
                    if br == "B" and q == NCH - 1 and ti == 3:
                        nrows = 124
                    else:
                        nrows = 128
                    for hb in range(2):
                        def smm(e, hb=hb, ti=ti, off=off):
                            for hh in range(4):
                                h = hb * 4 + hh
                                i = e.matmul(
                                    ps_s[hb][:, hh * 128:(hh + 1) * 128],
                                    qT[:, h * WA + off + ti * 128:
                                       h * WA + off + (ti + 1) * 128],
                                    kT[:, h * WA + off + ti * 128:
                                       h * WA + off + (ti + 1) * 128],
                                    start=(hh == 0), stop=(hh == 3))
                            return i
                        S.step("tensor", smm, reads=qk_all, writes=(f"ps_s{hb}",))
                        S.step("vector", lambda e, hb=hb: e.reduce_max(
                            nmax[:, hb * 4:(hb + 1) * 4],
                            ps_s[hb][:].rearrange("p (hh t) -> p hh t", t=128),
                            axis=AX.X, negate=True),
                            reads=(f"ps_s{hb}",), writes=(("nmax", hb),))
                        S.step("vector", lambda e, hb=hb: e.tensor_scalar_mul(
                            nmax[:, hb * 4:(hb + 1) * 4],
                            nmax[:, hb * 4:(hb + 1) * 4], SCALE),
                            reads=(("nmax", hb),), writes=(("nmax", hb),))
                        for hh in range(4):
                            h = hb * 4 + hh
                            S.step("scalar", lambda e, hb=hb, hh=hh, h=h: e.activation(
                                Pf[:, h * 128:(h + 1) * 128],
                                ps_s[hb][:, hh * 128:(hh + 1) * 128], AF.Exp,
                                bias=nmax[:, h:h + 1], scale=SCALE),
                                reads=(f"ps_s{hb}", ("nmax", hb)), writes=(("Pf", h),))
                    for h in range(NH):
                        S.step("vector", lambda e, h=h: e.tensor_mul(
                            Pf[:, h * 128:(h + 1) * 128], Pf[:, h * 128:(h + 1) * 128],
                            bm_sb[:]),
                            reads=(("Pf", h), "bm_sb"), writes=(("Pf", h),))
                    S.step("vector", lambda e: e.reduce_sum(
                        dsum[:], Pf[:].rearrange("p (h t) -> p h t", t=128), axis=AX.X),
                        reads=tuple(("Pf", h) for h in range(NH)), writes=("dsum",))
                    S.step("vector", lambda e: e.reciprocal(rden[:], dsum[:]),
                           reads=("dsum",), writes=("rden",))
                    for h in range(NH):
                        S.step("vector", lambda e, h=h: e.tensor_scalar_mul(
                            Pb[:, h * 128:(h + 1) * 128], Pf[:, h * 128:(h + 1) * 128],
                            rden[:, h:h + 1]),
                            reads=(("Pf", h), "rden"), writes=(("Pb", h),))
                        S.step("tensor", lambda e, h=h: e.transpose(
                            ps_pt[:, h * 128:(h + 1) * 128],
                            Pb[:, h * 128:(h + 1) * 128], lnb["ident_bf"][:]),
                            reads=(("Pb", h), "ident_bf"), writes=("ps_pt",))
                        S.step("scalar", lambda e, h=h: e.copy(
                            PTb[:, h * 128:(h + 1) * 128],
                            ps_pt[:, h * 128:(h + 1) * 128]),
                            reads=("ps_pt",), writes=(("PTb", h),))
                    for hb in range(2):
                        def avmm(e, hb=hb, ti=ti, varr=varr):
                            for hh in range(4):
                                h = hb * 4 + hh
                                i = e.matmul(
                                    ps_av[hb][:, hh * 128:(hh + 1) * 128],
                                    varr[:, ti * D + h * 128: ti * D + (h + 1) * 128],
                                    PTb[:, h * 128:(h + 1) * 128],
                                    start=(hh == 0), stop=(hh == 3))
                            return i
                        S.step("tensor", avmm,
                               reads=tuple(("PTb", hb * 4 + hh) for hh in range(4))
                               + ((f"v{br}", ti),),
                               writes=(f"ps_av{hb}",))
                        S.step("scalar", lambda e, hb=hb: e.copy(
                            OT[:, hb * 512:(hb + 1) * 512], ps_av[hb][:]),
                            reads=(f"ps_av{hb}",), writes=(("OT", hb),))
                    for eh in range(2):
                        def ymm(e, eh=eh):
                            for h in range(NH):
                                e.matmul(
                                    ps_y[eh][:], OT[:, h * 128:(h + 1) * 128],
                                    wo_sb[:, h * D + eh * 512: h * D + (eh + 1) * 512],
                                    start=(h == 0), stop=False)
                            return e.matmul(ps_y[eh][:], ones_r[:],
                                            bout_sb[:, eh * 512:(eh + 1) * 512],
                                            start=False, stop=True)
                        S.step("tensor", ymm,
                               reads=(("OT", 0), ("OT", 1), "wo_sb", "ones_r",
                                      "bout_sb"),
                               writes=(f"ps_y{eh}",))
                    c = q * 4 + ti
                    if br == "A":
                        S.step("sync", lambda e, c=c, ti=ti: e.dma_start(
                            xres[ti % 2][:], x_in[c * 128:(c + 1) * 128, :]),
                            reads=(("x_dram", c),), writes=(("xres", ti % 2),),
                            dma=True)
                        for eh in range(2):
                            S.step("vector", lambda e, ti=ti, eh=eh: e.tensor_add(
                                xo[ti % 2][:, eh * 512:(eh + 1) * 512],
                                xres[ti % 2][:, eh * 512:(eh + 1) * 512],
                                ps_y[eh][:]),
                                reads=(f"ps_y{eh}", ("xres", ti % 2)),
                                writes=(("xo", ti % 2, eh),))
                        S.step("sync", lambda e, c=c, ti=ti: e.dma_start(
                            x_out[c * 128:(c + 1) * 128, :], xo[ti % 2][:]),
                            reads=(("xo", ti % 2, 0), ("xo", ti % 2, 1)),
                            writes=(("xout1", c),), dma=True, fam=f"xout{ti % 2}")
                    else:
                        for eh in range(2):
                            S.step("scalar", lambda e, eh=eh: e.copy(
                                ybst[:, eh * 512:(eh + 1) * 512], ps_y[eh][:]),
                                reads=(f"ps_y{eh}",), writes=("ybst",))
                        r0 = t0 + 2 + ti * 128
                        S.step("sync", lambda e, r0=r0, nrows=nrows: e.dma_start(
                            yB_hbm[r0:r0 + nrows, :], ybst[0:nrows, :]),
                            reads=("ybst",), writes=("yB",), dma=True)

        for c in range(NT):
            S.step("sync", lambda e, c=c: e.dma_start(
                xres[c % 2][:], x_out[c * 128:(c + 1) * 128, :]),
                reads=(("xout1", c),), writes=(("xres", c % 2),), dma=True)
            S.step("sync", lambda e, c=c: e.dma_start(
                yb3[c % 2][:], yB_hbm[c * 128:(c + 1) * 128, :]),
                reads=("yB",), writes=(("yb3", c % 2),), dma=True)
            S.step("vector", lambda e, c=c: e.tensor_add(
                xo[c % 2][:], xres[c % 2][:], yb3[c % 2][:]),
                reads=(("xres", c % 2), ("yb3", c % 2)), writes=(("xo", c % 2, 0),))
            S.step("sync", lambda e, c=c: e.dma_start(
                x_out[c * 128:(c + 1) * 128, :], xo[c % 2][:]),
                reads=(("xo", c % 2, 0),), writes=(("xout2", c),), dma=True, fam=f"xoutP3{c % 2}")



def _l0a_inputs(x_shard, inputs, core_id):
    cons = host_consts(core_id)
    return {
        "x_in": x_shard,
        "wqkv": np.ascontiguousarray(inputs["L0_attn_w_qkv"], np.float32),
        "wout": np.ascontiguousarray(inputs["L0_attn_w_out"], np.float32),
        "bout": np.ascontiguousarray(
            np.asarray(inputs["L0_attn_b_out"], np.float32)).reshape(1, D),
        "c_ident": cons["c_ident"],
        "c_bmask": cons["c_bmask"],
    }



def _b1T(b1):
    Hh = b1.shape[1]
    return np.ascontiguousarray(
        b1.reshape(E, Hh // 128, 128).transpose(2, 1, 0).reshape(128, (Hh // 128) * E))



CAP = 1536
BUD = (4096, 4096, 2816, 2048, 1664, 1408, 1280, 1152)
OFF = tuple(int(np.cumsum((0,) + BUD)[i]) for i in range(8))
GROWS = int(np.sum(BUD))
GW = 1032


def record_slot_scatter(S, dr, B, layer, mw_col_of, rank_src, gbuf,
                        pref_rows_hbm):
    nc = S.nc
    sbf = B.buf
    sel_a = rank_src["sel_a"]
    ltri_sb = rank_src["ltri_sb"]
    ones_r = rank_src["ones_r"]
    ps_rk = rank_src["ps_rk"]
    xn_ap_of = rank_src["xn_ap_of"]
    ecap8 = sbf("ecap8", [128, 8], F32)
    S.step("sync", lambda e: e.dma_start(ecap8[:], dr["c_offe"][:, :]),
           writes=("ecap8",), dma=True)
    tokhl = sbf("tokhl", [128, 2 * NT], BF16)
    S.step("gpsimd", lambda e: e.dma_start(tokhl[:], dr["c_tokhl"][:, :]),
           writes=("tokhl",), dma=True)
    rkst = [sbf(f"rkst{i}", [1, 8], F32) for i in range(2)]
    rank_sb = sbf("rank_sb", [128, 8], F32)
    cum = sbf("cum", [128, 8], F32)
    selA = sbf("selA", [128, 8], F32)
    selB = sbf("selB", [128, 8], F32)
    scr8 = sbf("scr8", [128, 8], F32)
    ercol = sbf("ercol", [128, 6], F32)
    slotf = sbf("slotf", [128, 2], F32)
    sloti = sbf("sloti", [128, 2], I32)
    onescol2 = sbf("onescol2", [128, 1], F32)
    S.step("vector", lambda e: e.memset(onescol2[:], 1.0), writes=("onescol2",))
    rowb = [sbf(f"rowb{i}", [128, GW], BF16) for i in range(2)]
    zrow = sbf("grow0", [128, GW], BF16)
    S.step("vector", lambda e: e.memset(zrow[:], 0.0), writes=("growz",))
    for i in range(GROWS // 128):
        S.step("sync", lambda e, i=i: e.dma_start(
            gbuf[i * 128:(i + 1) * 128, :], zrow[:]),
            reads=("growz",), writes=(("gbuf", i % 2),), dma=True,
            fam="gz", bulk=True)

    for c in range(NT):
        xn_src, tk_xn, xn_hbm = xn_ap_of(c)
        mw_ap, tk_mw = mw_col_of(c)
        rb = rowb[c % 2]
        tkrb = ("rowb", c % 2)
        S.step("gpsimd", lambda e, c=c: e.dma_start(
            rkst[c % 2][:], pref_rows_hbm[c:c + 1, :]),
            reads=("prefh",), writes=(("rkst", c % 2),), dma=True)

        def rkmm(e, c=c):
            e.matmul(ps_rk[:, 0:8], ltri_sb[:], sel_a[:, c * E:(c + 1) * E],
                     start=True, stop=False)
            return e.matmul(ps_rk[:, 0:8], ones_r[:], rkst[c % 2][:],
                            start=False, stop=True)
        S.step("tensor", rkmm, reads=(("sel", c), ("rkst", c % 2),
                                      "ltri_sb", "ones_r"), writes=("ps_rk",))
        S.step("scalar", lambda e: e.copy(rank_sb[:], ps_rk[:, 0:8]),
               reads=("ps_rk",), writes=("rank_sb",))
        S.step("vector", lambda e, c=c: e.tensor_tensor_scan(
            cum[:], onescol2[:, 0:1].to_broadcast([128, 8]),
            sel_a[:, c * E:(c + 1) * E], 0.0, ALU.mult, ALU.add),
            reads=(("sel", c), "onescol2"), writes=("cum",))
        S.step("vector", lambda e: e.tensor_scalar(scr8[:], cum[:], -1.0, 2.0,
                                                   ALU.mult, ALU.add),
               reads=("cum",), writes=("scr8",))
        S.step("vector", lambda e: e.tensor_mul(scr8[:], scr8[:], cum[:]),
               reads=("scr8", "cum"), writes=("scr8",))
        S.step("vector", lambda e, c=c: e.tensor_mul(
            selA[:], scr8[:], sel_a[:, c * E:(c + 1) * E]),
            reads=("scr8", ("sel", c)), writes=("selA",))
        S.step("vector", lambda e, c=c: e.tensor_sub(
            selB[:], sel_a[:, c * E:(c + 1) * E], selA[:]),
            reads=(("sel", c), "selA"), writes=("selB",))
        for k, selk, kt in ((0, selA, "selA"), (1, selB, "selB")):
            S.step("vector", lambda e, selk=selk: e.tensor_mul(
                scr8[:], ecap8[:], selk[:]),
                reads=("ecap8", kt), writes=("scr8",))
            S.step("vector", lambda e, k=k: e.reduce_sum(
                ercol[:, 3 * k:3 * k + 1], scr8[:], axis=AX.X),
                reads=("scr8",), writes=(("er", k),))
            S.step("vector", lambda e, selk=selk: e.tensor_mul(
                scr8[:], rank_sb[:], selk[:]),
                reads=("rank_sb", kt), writes=("scr8",))
            S.step("vector", lambda e, k=k: e.reduce_sum(
                ercol[:, 3 * k + 1:3 * k + 2], scr8[:], axis=AX.X),
                reads=("scr8",), writes=(("er", k),))
            S.step("vector", lambda e, selk=selk, mw_ap=mw_ap: e.tensor_mul(
                scr8[:], mw_ap, selk[:]),
                reads=(tk_mw, kt), writes=("scr8",))
            S.step("vector", lambda e, k=k: e.reduce_sum(
                ercol[:, 3 * k + 2:3 * k + 3], scr8[:], axis=AX.X),
                reads=("scr8",), writes=(("er", k),))
            S.step("vector", lambda e, k=k: e.tensor_add(
                slotf[:, k:k + 1], ercol[:, 3 * k:3 * k + 1],
                ercol[:, 3 * k + 1:3 * k + 2]),
                reads=(("er", k),), writes=("slotf",))
        S.step("vector", lambda e: e.tensor_copy(sloti[:], slotf[:]),
               reads=("slotf",), writes=("sloti",))
        if xn_hbm:
            S.step("sync", lambda e, rb=rb, xn_src=xn_src, c=c: e.dma_start(
                rb[:, 0:D], xn_src[c * 128:(c + 1) * 128, :]),
                reads=(tk_xn,), writes=(tkrb,), dma=True, fam=f"rowbl{c % 2}")
        else:
            S.step("scalar", lambda e, rb=rb, xn_src=xn_src: e.copy(
                rb[:, 0:D], xn_src),
                reads=(tk_xn,), writes=(tkrb,))
        S.step("scalar", lambda e, rb=rb, c=c: e.copy(
            rb[:, D + 1:D + 3], tokhl[:, 2 * c:2 * c + 2]),
            reads=("tokhl",), writes=(tkrb,))
        for k in range(2):
            S.step("scalar", lambda e, rb=rb, k=k: e.copy(
                rb[:, D:D + 1], ercol[:, 3 * k + 2:3 * k + 3]),
                reads=(("er", k),), writes=(tkrb,))
            S.step("gpsimd", lambda e, rb=rb, k=k: e.indirect_dma_start(
                gbuf[:, :], IndirectOffsetOnAxis(sloti[:, k:k + 1], 0),
                rb[:], None),
                reads=(tkrb, "sloti"), writes=(("gbuf", c % 2),), dma=True,
                fam=f"gsc{c % 2}")


def record_ffn_sparse_l1main(S, dr, B):
    nc = S.nc
    sbf = B.buf
    psf = B.psum
    maskw = sbf("maskw", [128, NT * E], F32)
    sel_a = sbf("sel_a", [128, NT * E], F32)
    ones_r = sbf("ones_r", [1, 128], F32)
    ltri_sb = sbf("ltri_sb", [128, 128], F32)
    ps_rk = psf("ps_rk", [128, 512], F32)
    S.step("sync", lambda e: e.dma_start(maskw[:], dr["mw_in"][:, :]),
           writes=tuple(("maskw", c) for c in range(NT)), dma=True)
    S.step("sync", lambda e: e.dma_start(sel_a[:], dr["sel_in"][:, :]),
           writes=tuple(("sel", c) for c in range(NT)), dma=True)
    S.step("sync", lambda e: e.dma_start(ltri_sb[:], dr["c_ltri"][:, :]),
           writes=("ltri_sb",), dma=True)
    S.step("vector", lambda e: e.memset(ones_r[:], 1.0), writes=("ones_r",))
    rank_src = {
        "sel_a": sel_a, "ltri_sb": ltri_sb, "ones_r": ones_r, "ps_rk": ps_rk,
        "xn_ap_of": lambda c: (dr["xntok"], ("xntok_ext",), True),
    }
    record_slot_scatter(S, dr, B, 1,
                        lambda c: (maskw[:, c * E:(c + 1) * E], ("maskw", c)),
                        rank_src, dr["gbuf"], dr["prefT"])
    record_ffn_sparse_main(S, dr, B, 1)


def record_ffn_sparse_main(S, dr, B, layer):
    H = 2048 if layer == 0 else 4096
    NFB = H // 2048
    FCB = 16
    NFI = H // 128
    ACTF = AF.Gelu if layer == 0 else AF.Relu
    nc = S.nc
    sbf = B.buf
    psf = B.psum
    gbuf = dr["gbuf"]
    x_res = dr["x_res"]
    w1 = dr["w1"]; w2 = dr["w2"]; b1Te = dr["b1Te"]; b2 = dr["b2"]
    c_ident = dr["c_ident"]

    ident_bf = sbf("ident_bf2", [128, 128], BF16)
    S.step("gpsimd", lambda e: e.dma_start(ident_bf[:], c_ident[:, :]),
           writes=("ident_bf",), dma=True)
    b1st = sbf("b1st", [128, NFI], F32)
    b2row = sbf("b2row", [1, D], BF16)
    ones_r2 = sbf("ones_r2", [1, 128], BF16)
    S.step("vector", lambda e: e.memset(ones_r2[:], 1.0), writes=("ones_r2",))
    xgT = sbf("xgT", [128, DC * 512], BF16)
    h_sg = sbf("h_sg", [128, FCB * 512], BF16)
    w1_sb = sbf("w1_sb", [128, DC * FCB * 128], BF16)
    w2_sb = sbf("w2_sb", [128, FCB * D], BF16)
    grow = [sbf(f"grow{i}", [128, GW], BF16) for i in range(2)]
    mwc = sbf("mwc", [128, 32], F32)
    tokf = sbf("tokf", [128, 32], F32)
    toki = sbf("toki", [128, 32], I32)
    ysb = [sbf(f"ysb{i}", [128, D], F32) for i in range(2)]
    ps_tr = psf("ps_tr", [128, 1024], BF16)
    ps_h = [psf(f"ps_h{i}", [128, 512], F32) for i in range(2)]
    ps_y = [psf(f"ps_y{i}", [128, 512], F32) for i in range(2)]

    for v in range(E):
        nsub = BUD[v] // 128
        base = OFF[v]
        S.step("sync", lambda e, v=v: e.dma_start(
            b1st[:], b1Te[:, v * NFI:(v + 1) * NFI]),
            writes=("b1st",), dma=True, fam="b1st")
        S.step("gpsimd", lambda e, v=v: e.dma_start(b2row[:], b2[v:v + 1, :]),
               writes=("b2row",), dma=True, fam="b2row")
        for st in range(nsub):
            gsl = grow[st % 2]
            S.step("sync", lambda e, st=st, gsl=gsl, base=base: e.dma_start(
                gsl[:], gbuf[base + st * 128: base + (st + 1) * 128, :]),
                reads=(("gbuf", 0), ("gbuf", 1)), writes=(("grow", st % 2),),
                dma=True, fam=f"grow{st % 2}")
            S.step("vector", lambda e, st=st, gsl=gsl: e.tensor_copy(
                mwc[:, st:st + 1], gsl[:, D:D + 1]),
                reads=(("grow", st % 2),), writes=("mwc",))
            S.step("vector", lambda e, st=st, gsl=gsl: e.tensor_copy(
                tokf[:, st:st + 1], gsl[:, D + 1:D + 2]),
                reads=(("grow", st % 2),), writes=(("tokf", st),))
            S.step("vector", lambda e, st=st, gsl=gsl: e.scalar_tensor_tensor(
                tokf[:, st:st + 1], tokf[:, st:st + 1], 64.0,
                gsl[:, D + 2:D + 3], ALU.mult, ALU.add),
                reads=(("tokf", st), ("grow", st % 2)), writes=(("tokf", st),))
        S.step("vector", lambda e, nsub=nsub: e.tensor_copy(
            toki[:, 0:nsub], tokf[:, 0:nsub]),
            reads=tuple(("tokf", st) for st in range(nsub)), writes=("toki",))

        for fb in range(NFB):
            src1 = w1[v, :, fb * 2048:(fb + 1) * 2048].rearrange(
                "(dc p) (fc k) -> p dc fc k", p=128, k=128)
            S.step("gpsimd", lambda e, src1=src1: e.dma_start(
                w1_sb[:].rearrange("p (dc fc k) -> p dc fc k", fc=FCB, k=128),
                src1),
                writes=("w1_sb",), dma=True)
            src2 = w2[v, fb * 2048:(fb + 1) * 2048, :].rearrange(
                "(fc p) d -> p fc d", p=128)
            S.step("gpsimd", lambda e, src2=src2: e.dma_start(
                w2_sb[:].rearrange("p (fc d) -> p fc d", d=D), src2),
                writes=("w2_sb",), dma=True)
            nsg = (nsub + 3) // 4
            for sg in range(nsg):
                sgw = min(512, (nsub - sg * 4) * 128)
                for su in range(sgw // 128):
                    st = sg * 4 + su
                    gsl = grow[st % 2]
                    if fb == 0 and False:
                        pass
                    def trx(e, su=su, st=st, gsl=gsl):
                        for g in range(DC):
                            i = e.transpose(ps_tr[:, g * 128:(g + 1) * 128],
                                            gsl[:, g * 128:(g + 1) * 128], ident_bf[:])
                        return i
                    S.step("sync", lambda e, st=st, gsl=gsl, base=base: e.dma_start(
                        gsl[:], gbuf[base + st * 128: base + (st + 1) * 128, :]),
                        reads=(("gbuf", 0), ("gbuf", 1)),
                        writes=(("grow", st % 2),), dma=True, fam=f"grow{st % 2}")
                    S.step("tensor", trx, reads=(("grow", st % 2), "ident_bf"),
                           writes=("ps_tr",))
                    def trev(e, su=su):
                        for g in range(DC):
                            i = e.copy(xgT[:, g * 512 + su * 128: g * 512 + (su + 1) * 128],
                                       ps_tr[:, g * 128:(g + 1) * 128])
                        return i
                    S.step("scalar", trev, reads=("ps_tr",), writes=(("xgT", su),))
                xg_all = tuple(("xgT", su) for su in range(sgw // 128))
                for fc in range(FCB):
                    def hmm(e, fc=fc, sgw=sgw):
                        for dc in range(DC):
                            i = e.matmul(
                                ps_h[fc % 2][:, 0:sgw],
                                w1_sb[:, (dc * FCB + fc) * 128:(dc * FCB + fc + 1) * 128],
                                xgT[:, dc * 512: dc * 512 + sgw],
                                start=(dc == 0), stop=(dc == DC - 1))
                        return i
                    S.step("tensor", hmm, reads=xg_all + ("w1_sb",),
                           writes=(f"ps_h{fc % 2}",))
                    fi = fb * FCB + fc
                    S.step("scalar", lambda e, fc=fc, fi=fi, sgw=sgw: e.activation(
                        h_sg[:, fc * 512: fc * 512 + sgw], ps_h[fc % 2][:, 0:sgw],
                        ACTF, bias=b1st[:, fi:fi + 1]),
                        reads=(f"ps_h{fc % 2}", "b1st"), writes=(("h", fc),))
                h_all = tuple(("h", fc) for fc in range(FCB))
                for su in range(sgw // 128):
                    st = sg * 4 + su
                    for dt in range(2):
                        def ymm(e, su=su, dt=dt, fb=fb):
                            for fc in range(FCB):
                                last = (fc == FCB - 1) and (fb < NFB - 1)
                                i = e.matmul(
                                    ps_y[dt][:],
                                    h_sg[:, fc * 512 + su * 128: fc * 512 + su * 128 + 128],
                                    w2_sb[:, fc * D + dt * 512: fc * D + (dt + 1) * 512],
                                    start=(fc == 0), stop=last)
                            if fb == NFB - 1:
                                i = e.matmul(
                                    ps_y[dt][:], ones_r2[:],
                                    b2row[0:1, dt * 512:(dt + 1) * 512],
                                    start=False, stop=True)
                            return i
                        S.step("tensor", ymm,
                               reads=h_all + ("w2_sb", "ones_r2", "b2row"),
                               writes=(f"ps_y{dt}",))
                        S.step("vector", lambda e, st=st, dt=dt: e.tensor_scalar_mul(
                            ysb[st % 2][:, dt * 512:(dt + 1) * 512], ps_y[dt][:],
                            mwc[:, st:st + 1]),
                            reads=(f"ps_y{dt}", "mwc"), writes=(("ysb", st % 2, dt),))
                    S.step("gpsimd", lambda e, st=st: e.indirect_dma_start(
                        x_res[:, :], IndirectOffsetOnAxis(toki[:, st:st + 1], 0),
                        ysb[st % 2][:], None, compute_op=ALU.add),
                        reads=(("ysb", st % 2, 0), ("ysb", st % 2, 1), "toki", "xsc"),
                        writes=("xsc",), dma=True, fam=f"gadd{st % 2}")

def _mk_composite(build_fn, serialize=False):
    nc = bass.Bass("TRN2", target_bir_lowering=False, debug=False)
    with ExitStack() as ctx:
        S = Sched(nc, ctx, serialize=serialize)
        B = BufAlloc(nc, ctx)
        cache = {}

        def decl(name, shape, dt, kind=None):
            if name not in cache:
                if kind:
                    cache[name] = nc.dram_tensor(name, shape, dt, kind=kind)
                else:
                    cache[name] = nc.dram_tensor(name, shape, dt)
            return cache[name]

        build_fn(S, B, decl)
        S.build_block(ctx)
    return nc


def _ffn_dr(decl, layer, split, x_src, x_dst):
    H = 2048 if layer == 0 else 4096
    dr = {"c_ident": decl("c_ident", [128, 128], F32, "ExternalInput"),
          "x_in": x_src, "x_out": x_dst}
    if split in ("all", "count", "cnt1"):
        dr["wg"] = decl(f"wg_{layer}", [D, E], F32, "ExternalInput")
        if layer == 0:
            dr["rb"] = decl("rb_0", [1, E], F32, "ExternalInput")
    if split in ("all", "main"):
        dr["w1"] = decl(f"w1_{layer}", [E, D, H], F32, "ExternalInput")
        dr["b1T"] = decl(f"b1T_{layer}", [128, (H // 128) * E], F32, "ExternalInput")
        dr["w2"] = decl(f"w2_{layer}", [E, H, D], F32, "ExternalInput")
        dr["b2"] = decl(f"b2_{layer}", [E, D], F32, "ExternalInput")
    if split == "count":
        dr["xnT"] = decl("xnT_out", [D, T], BF16, "ExternalOutput")
        dr["mw_out"] = decl("mw_out", [128, NT * E], F32, "ExternalOutput")
        dr["sel_out"] = decl("sel_out", [128, NT * E], F32, "ExternalOutput")
        dr["pref_out"] = decl("pref_out", [8, NT + 1], F32, "ExternalOutput")
    elif split == "main":
        dr["xnT"] = decl("xnT_in", [D, T], BF16, "ExternalInput")
        dr["mw_in"] = decl("mw_in", [128, NT * E], F32, "ExternalInput")
        dr["sel_in"] = decl("sel_in", [128, NT * E], F32, "ExternalInput")
        dr["pref_in"] = decl("pref_in", [8, NT + 1], F32, "ExternalInput")
        dr["agcnt"] = decl("agcnt", [8, E], F32, "ExternalInput")
        dr["c_ltri"] = decl("c_ltri", [128, 128], F32, "ExternalInput")
        dr["c_corelt"] = decl("c_corelt", [8, 8], F32, "ExternalInput")
        dr["nthr"] = decl("scr_nthr", [NT + 1, E], F32)
    else:
        dr["xnT"] = decl("scr_xnT_ffn", [D, T], BF16)
    return dr


def _l0a_dr(decl, x_src, x_dst):
    return {
        "c_ident": decl("c_ident", [128, 128], F32, "ExternalInput"),
        "c_bmask": decl("c_bmask", [128, 128], F32, "ExternalInput"),
        "x_in": x_src, "x_out": x_dst,
        "wqkv": decl("wqkv", [D, 3 * D], F32, "ExternalInput"),
        "wout": decl("wout", [D, D], F32, "ExternalInput"),
        "bout": decl("bout", [1, D], F32, "ExternalInput"),
        "xnT": decl("scr_xnT_l0a", [D, T], BF16),
        "yB": decl("scr_yB", [T, D], F32),
    }


def _l1a_dr(decl, x_src, x_dst):
    dr = {"c_ident": decl("c_ident", [128, 128], F32, "ExternalInput"),
          "x_in": x_src, "x_out": x_dst}
    for nm in ("wkT", "wvT", "wrT", "woT"):
        dr[nm] = decl(nm, [D, D], F32, "ExternalInput")
    for nm in ("tmk5", "tmv5", "tmr5", "dec5", "fst5"):
        dr[nm] = decl(nm, [128, DC], F32, "ExternalInput")
    return dr


def build_pa(serialize=False):
    def b(S, B, decl):
        x_in = decl("x_in", [T, D], F32, "ExternalInput")
        B.kind = "ffn"
        S.prefix = "a0"
        record_ffn(S, _ffn_dr(decl, 0, "count", x_in, None), B, 0, split="count")
    return _mk_composite(b, serialize)


def build_pb(serialize=False):
    def b(S, B, decl):
        x_in = decl("x_in", [T, D], F32, "ExternalInput")
        x_out = decl("x_out", [T, D], F32, "ExternalOutput")
        x_scr = decl("x_scr", [T, D], F32)
        B.kind = "ffn"
        S.prefix = "b0"
        record_ffn(S, _ffn_dr(decl, 0, "main", x_in, x_scr), B, 0, split="main")
        S.barrier()
        B.kind = "l0a"
        S.prefix = "b1"
        record_l0a(S, _l0a_dr(decl, x_scr, x_out), B)
        S.barrier()
        B.kind = "ffn"
        S.prefix = "b2"
        record_ffn(S, _ffn_dr(decl, 0, "count", x_out, None), B, 0, split="count")
    return _mk_composite(b, serialize)


def build_pc(serialize=False):
    def b(S, B, decl):
        x_in = decl("x_in", [T, D], F32, "ExternalInput")
        x_out = decl("x_out", [T, D], F32, "ExternalOutput")
        x_scr = decl("x_scr", [T, D], F32)
        B.kind = "ffn"
        S.prefix = "c0"
        record_ffn(S, _ffn_dr(decl, 0, "main", x_in, x_scr), B, 0, split="main")
        S.barrier()
        S.prefix = "c1"
        record_ffn(S, _ffn_dr(decl, 1, "all", x_scr, x_scr), B, 1, split="all")
        S.barrier()
        B.kind = "l1a"
        S.prefix = "c2"
        record_l1a(S, _l1a_dr(decl, x_scr, x_scr), B)
        S.barrier()
        B.kind = "ffn"
        S.prefix = "c3"
        record_ffn(S, _ffn_dr(decl, 1, "all", x_scr, x_out), B, 1, split="all")
    return _mk_composite(b, serialize)



def host_consts_sparse():
    tok = np.arange(T, dtype=np.int64)
    hi = (tok // 64).astype(np.float32)
    lo = (tok % 64).astype(np.float32)
    tokhl = np.zeros((128, 2 * NT), np.float32)
    for c in range(NT):
        tokhl[:, 2 * c] = hi[c * 128:(c + 1) * 128]
        tokhl[:, 2 * c + 1] = lo[c * 128:(c + 1) * 128]
    return {"c_tokhl": tokhl.astype(BFNP)}


def _perm_inputs(pref):
    counts = np.asarray(pref)[:, NT]
    perm = np.argsort(-counts, kind="stable")
    inv = np.empty(8, np.int64)
    inv[perm] = np.arange(8)
    offe = np.broadcast_to(
        np.asarray([OFF[inv[e]] for e in range(8)], np.float32), (128, 8)).copy()
    return {"c_offe": offe,
            "prefT_in": np.ascontiguousarray(np.asarray(pref).T)}, perm


def _sparse_main_dr(decl, layer, x_res):
    H = 2048 if layer == 0 else 4096
    return {
        "c_offe": decl("c_offe", [128, 8], F32, "ExternalInput"),
        "c_tokhl": decl("c_tokhl", [128, 2 * NT], BF16, "ExternalInput"),
        "b1Te": decl(f"b1Te_{layer}", [128, E * (H // 128)], F32, "ExternalInput"),
        "gbuf": decl("gbuf_scr", [GROWS, GW], BF16, "ExternalOutput"),
        "x_res": x_res,
        "xntok": decl("xntok_in", [T, D], BF16, "ExternalInput"),
        "prefT": decl("prefT_in", [NT + 1, E], F32, "ExternalInput"),
        "mw_in": decl("mw_in", [128, NT * E], F32, "ExternalInput"),
        "sel_in": decl("sel_in", [128, NT * E], F32, "ExternalInput"),
        "c_ltri": decl("c_ltri", [128, 128], F32, "ExternalInput"),
        "c_ident": decl("c_ident", [128, 128], F32, "ExternalInput"),
        "w1": decl(f"w1_{layer}", [E, D, H], F32, "ExternalInput"),
        "w2": decl(f"w2_{layer}", [E, H, D], F32, "ExternalInput"),
        "b2": decl(f"b2_{layer}", [E, D], F32, "ExternalInput"),
    }


def _xcopy(S, src_t, dst_t):
    for i in range(4):
        S.step("sync", lambda e, i=i: e.dma_start(
            dst_t[i * 1024:(i + 1) * 1024, :], src_t[i * 1024:(i + 1) * 1024, :]),
            reads=(), writes=("xsc", (f"xcp{i}",)), dma=True, fam="xcopy")


def build_sa(serialize=False):
    def b(S, B, decl):
        x_in = decl("x_in", [T, D], F32, "ExternalInput")
        B.kind = "ffn"
        S.prefix = "a0"
        dr = _ffn_dr(decl, 0, "count", x_in, None)
        dr["xntok"] = decl("xntok_out", [T, D], BF16, "ExternalOutput")
        record_ffn(S, dr, B, 0, split="count")
    return _mk_composite(b, serialize)


def build_sb(serialize=False):
    def b(S, B, decl):
        x_in = decl("x_in", [T, D], F32, "ExternalInput")
        x_out = decl("x_out", [T, D], F32, "ExternalOutput")
        x_scr = decl("x_scr", [T, D], F32)
        B.kind = "ffn"
        S.prefix = "b0"
        _xcopy(S, x_in, x_scr)
        dr = _ffn_dr(decl, 0, "main", x_in, None)
        dr.update(_sparse_main_dr(decl, 0, x_scr))
        record_ffn(S, dr, B, 0, split="main", sparse=True)
        S.barrier()
        B.kind = "l0a"
        S.prefix = "b1"
        record_l0a(S, _l0a_dr(decl, x_scr, x_out), B)
        S.barrier()
        B.kind = "ffn"
        S.prefix = "b2"
        dr2 = _ffn_dr(decl, 0, "count", x_out, None)
        dr2["xntok"] = decl("xntok_out", [T, D], BF16, "ExternalOutput")
        record_ffn(S, dr2, B, 0, split="count")
    return _mk_composite(b, serialize)


def build_sc(serialize=False):
    def b(S, B, decl):
        x_in = decl("x_in", [T, D], F32, "ExternalInput")
        x_out = decl("x_out", [T, D], F32, "ExternalOutput")
        B.kind = "ffn"
        S.prefix = "c0"
        _xcopy(S, x_in, x_out)
        dr = _ffn_dr(decl, 0, "main", x_in, None)
        dr.update(_sparse_main_dr(decl, 0, x_out))
        record_ffn(S, dr, B, 0, split="main", sparse=True)
        S.barrier()
        S.prefix = "c1"
        dr2 = _ffn_dr(decl, 1, "count", x_out, None)
        dr2["xntok"] = decl("xntok_out", [T, D], BF16, "ExternalOutput")
        record_ffn(S, dr2, B, 1, split="count")
    return _mk_composite(b, serialize)


def build_sd(serialize=False):
    def b(S, B, decl):
        x_in = decl("x_in", [T, D], F32, "ExternalInput")
        x_out = decl("x_out", [T, D], F32, "ExternalOutput")
        x_scr = decl("x_scr", [T, D], F32)
        B.kind = "ffn"
        S.prefix = "d0"
        _xcopy(S, x_in, x_scr)
        dr = _sparse_main_dr(decl, 1, x_scr)
        record_ffn_sparse_l1main(S, dr, B)
        S.barrier()
        B.kind = "l1a"
        S.prefix = "d1"
        record_l1a(S, _l1a_dr(decl, x_scr, x_out), B)
        S.barrier()
        B.kind = "ffn"
        S.prefix = "d2"
        dr2 = _ffn_dr(decl, 1, "count", x_out, None)
        dr2["xntok"] = decl("xntok_out", [T, D], BF16, "ExternalOutput")
        record_ffn(S, dr2, B, 1, split="count")
    return _mk_composite(b, serialize)


def build_se(serialize=False):
    def b(S, B, decl):
        x_in = decl("x_in", [T, D], F32, "ExternalInput")
        x_out = decl("x_out", [T, D], F32, "ExternalOutput")
        B.kind = "ffn"
        S.prefix = "e0"
        _xcopy(S, x_in, x_out)
        dr = _sparse_main_dr(decl, 1, x_out)
        record_ffn_sparse_l1main(S, dr, B)
    return _mk_composite(b, serialize)



def _ffn_w(inputs, layer):
    if layer == 0:
        return {
            "w1_0": inputs["L0_ffn_w1"],
            "b1T_0": _b1T(np.asarray(inputs["L0_ffn_b1"], np.float32)),
            "w2_0": inputs["L0_ffn_w2"],
            "b2_0": np.ascontiguousarray(np.asarray(inputs["L0_ffn_b2"], np.float32)),
        }
    return {
        "wg_1": np.ascontiguousarray(np.asarray(inputs["L1_ffn_W_gate"], np.float32)),
        "w1_1": inputs["L1_ffn_W1"],
        "b1T_1": _b1T(np.asarray(inputs["L1_ffn_b1"], np.float32)),
        "w2_1": inputs["L1_ffn_W2"],
        "b2_1": np.ascontiguousarray(np.asarray(inputs["L1_ffn_b2"], np.float32)),
    }


def _cnt_w(inputs):
    return {
        "wg_0": np.ascontiguousarray(np.asarray(inputs["L0_ffn_router_w"], np.float32)),
        "rb_0": np.ascontiguousarray(
            np.asarray(inputs["L0_ffn_router_b"], np.float32)).reshape(1, E),
    }


def _l0a_w(inputs, core_id):
    cons = host_consts(core_id)
    return {
        "wqkv": np.ascontiguousarray(np.asarray(inputs["L0_attn_w_qkv"], np.float32)),
        "wout": np.ascontiguousarray(np.asarray(inputs["L0_attn_w_out"], np.float32)),
        "bout": np.ascontiguousarray(
            np.asarray(inputs["L0_attn_b_out"], np.float32)).reshape(1, D),
        "c_bmask": cons["c_bmask"],
    }


def _l1a_w(inputs):
    def five(v):
        return np.ascontiguousarray(np.asarray(v, np.float32).reshape(DC, 128).T)
    return {
        "wkT": np.ascontiguousarray(np.asarray(inputs["L1_attn_W_k"], np.float32).T),
        "wvT": np.ascontiguousarray(np.asarray(inputs["L1_attn_W_v"], np.float32).T),
        "wrT": np.ascontiguousarray(np.asarray(inputs["L1_attn_W_r"], np.float32).T),
        "woT": np.ascontiguousarray(np.asarray(inputs["L1_attn_W_o"], np.float32).T),
        "tmk5": five(inputs["L1_attn_time_mix_k"]),
        "tmv5": five(inputs["L1_attn_time_mix_v"]),
        "tmr5": five(inputs["L1_attn_time_mix_r"]),
        "dec5": five(inputs["L1_attn_time_decay"]),
        "fst5": five(inputs["L1_attn_time_first"]),
    }


def _count_glue(cres, b):
    return {
        "xnT_in": cres[b]["xnT_out"],
        "mw_in": cres[b]["mw_out"],
        "sel_in": cres[b]["sel_out"],
        "pref_in": cres[b]["pref_out"],
    }


_PROGS = {}


def _get(name):
    builders = {"pa": build_pa, "pb": build_pb, "pc": build_pc,
                "sa": build_sa, "sb": build_sb, "sc": build_sc,
                "sd": build_sd, "se": build_se}
    if name not in _PROGS:
        _PROGS[name] = builders[name]()
    return _PROGS[name]


def _kernel_dense(inputs, shards):
    cores = list(range(NCORES))
    maps = [{"x_in": shards[b], "c_ident": host_consts(b)["c_ident"],
             **_cnt_w(inputs)} for b in cores]
    cres = run_bass_kernel_spmd(_get("pa"), maps, core_ids=cores).results
    ag1 = np.ascontiguousarray(
        np.stack([np.asarray(cres[b]["pref_out"])[:, NT] for b in cores]))
    maps = []
    for b in cores:
        cons = host_consts(b)
        maps.append({"x_in": shards[b], "c_ident": cons["c_ident"],
                     "c_ltri": cons["c_ltri"], "c_corelt": cons["c_corelt"],
                     "agcnt": ag1, **_count_glue(cres, b), **_ffn_w(inputs, 0),
                     **_cnt_w(inputs), **_l0a_w(inputs, b)})
    bres = run_bass_kernel_spmd(_get("pb"), maps, core_ids=cores).results
    ag2 = np.ascontiguousarray(
        np.stack([np.asarray(bres[b]["pref_out"])[:, NT] for b in cores]))
    maps = []
    for b in cores:
        cons = host_consts(b)
        maps.append({"x_in": np.ascontiguousarray(np.asarray(bres[b]["x_out"], np.float32)),
                     "c_ident": cons["c_ident"], "c_ltri": cons["c_ltri"],
                     "c_corelt": cons["c_corelt"], "agcnt": ag2,
                     **_count_glue(bres, b), **_ffn_w(inputs, 0),
                     **_ffn_w(inputs, 1), **_l1a_w(inputs)})
    fres = run_bass_kernel_spmd(_get("pc"), maps, core_ids=cores).results
    return np.stack([np.asarray(fres[b]["x_out"], np.float32)
                     for b in cores]).astype(np.float32)


def _b1Te(b1, H):
    b1 = np.asarray(b1, np.float32).reshape(E, H // 128, 128)
    return np.ascontiguousarray(b1.transpose(2, 0, 1).reshape(128, -1))


def _sp_w(inputs, layer, perm):
    if layer == 0:
        w1 = np.asarray(inputs["L0_ffn_w1"], np.float32)[perm]
        w2 = np.asarray(inputs["L0_ffn_w2"], np.float32)[perm]
        b1 = np.asarray(inputs["L0_ffn_b1"], np.float32)[perm]
        b2 = np.asarray(inputs["L0_ffn_b2"], np.float32)[perm]
        return {"w1_0": np.ascontiguousarray(w1),
                "w2_0": np.ascontiguousarray(w2),
                "b2_0": np.ascontiguousarray(b2),
                "b1T_0": _b1T(b1),
                "b1Te_0": _b1Te(b1, 2048)}
    w1 = np.asarray(inputs["L1_ffn_W1"], np.float32)[perm]
    w2 = np.asarray(inputs["L1_ffn_W2"], np.float32)[perm]
    b1 = np.asarray(inputs["L1_ffn_b1"], np.float32)[perm]
    b2 = np.asarray(inputs["L1_ffn_b2"], np.float32)[perm]
    return {"w1_1": np.ascontiguousarray(w1),
            "w2_1": np.ascontiguousarray(w2),
            "b2_1": np.ascontiguousarray(b2),
            "b1Te_1": _b1Te(b1, 4096)}


def _budget_ok(prefs):
    bud = np.asarray(BUD, np.float32)
    for p in prefs:
        c = np.sort(np.asarray(p)[:, NT])[::-1]
        if (c > bud).any():
            return False
    return True


def kernel(**inputs):
    x = np.ascontiguousarray(np.asarray(inputs["L0_attn_x"], np.float32))
    shards = [np.ascontiguousarray(x[b]) for b in range(NCORES)]
    cores = list(range(NCORES))
    spc = host_consts_sparse()

    maps = [{"x_in": shards[b], "c_ident": host_consts(b)["c_ident"],
             **_cnt_w(inputs)} for b in cores]
    ares = run_bass_kernel_spmd(_get("sa"), maps, core_ids=cores).results
    prefs = [np.asarray(ares[b]["pref_out"]) for b in cores]
    if not _budget_ok(prefs):
        return _kernel_dense(inputs, shards)
    ag1 = np.ascontiguousarray(np.stack([p[:, NT] for p in prefs]))

    def glue0(res, b):
        p = np.asarray(res[b]["pref_out"])
        dev, perm = _perm_inputs(p)
        return {"xnT_in": res[b]["xnT_out"], "xntok_in": res[b]["xntok_out"],
                "mw_in": res[b]["mw_out"], "sel_in": res[b]["sel_out"],
                "pref_in": p, **dev, **_sp_w(inputs, 0, perm)}

    def glue1(res, b):
        p = np.asarray(res[b]["pref_out"])
        dev, perm = _perm_inputs(p)
        return {"xntok_in": res[b]["xntok_out"], "mw_in": res[b]["mw_out"],
                "sel_in": res[b]["sel_out"], **dev, **_sp_w(inputs, 1, perm)}

    wg1 = np.ascontiguousarray(np.asarray(inputs["L1_ffn_W_gate"], np.float32))

    maps = []
    for b in cores:
        cons = host_consts(b)
        maps.append({"x_in": shards[b], "c_ident": cons["c_ident"],
                     "c_ltri": cons["c_ltri"], "c_corelt": cons["c_corelt"],
                     "agcnt": ag1, **glue0(ares, b), **_cnt_w(inputs),
                     **_l0a_w(inputs, b), **spc})
    bres = run_bass_kernel_spmd(_get("sb"), maps, core_ids=cores).results
    prefs = [np.asarray(bres[b]["pref_out"]) for b in cores]
    if not _budget_ok(prefs):
        return _kernel_dense(inputs, shards)
    ag2 = np.ascontiguousarray(np.stack([p[:, NT] for p in prefs]))

    maps = []
    for b in cores:
        cons = host_consts(b)
        maps.append({"x_in": np.ascontiguousarray(
                         np.asarray(bres[b]["x_out"], np.float32)),
                     "c_ident": cons["c_ident"], "c_ltri": cons["c_ltri"],
                     "c_corelt": cons["c_corelt"], "agcnt": ag2,
                     **glue0(bres, b), "wg_1": wg1, **spc})
    crez = run_bass_kernel_spmd(_get("sc"), maps, core_ids=cores).results
    prefs = [np.asarray(crez[b]["pref_out"]) for b in cores]
    if not _budget_ok(prefs):
        return _kernel_dense(inputs, shards)

    maps = []
    for b in cores:
        cons = host_consts(b)
        maps.append({"x_in": np.ascontiguousarray(
                         np.asarray(crez[b]["x_out"], np.float32)),
                     "c_ident": cons["c_ident"], "c_ltri": cons["c_ltri"],
                     **glue1(crez, b), "wg_1": wg1,
                     **_l1a_w(inputs), **spc})
    dres = run_bass_kernel_spmd(_get("sd"), maps, core_ids=cores).results
    prefs = [np.asarray(dres[b]["pref_out"]) for b in cores]
    if not _budget_ok(prefs):
        return _kernel_dense(inputs, shards)

    maps = []
    for b in cores:
        cons = host_consts(b)
        maps.append({"x_in": np.ascontiguousarray(
                         np.asarray(dres[b]["x_out"], np.float32)),
                     "c_ident": cons["c_ident"], "c_ltri": cons["c_ltri"],
                     **glue1(dres, b), **spc})
    eres = run_bass_kernel_spmd(_get("se"), maps, core_ids=cores).results
    return np.stack([np.asarray(eres[b]["x_out"], np.float32)
                     for b in cores]).astype(np.float32)


# version 08
# speedup vs baseline: 1.9382x; 1.7162x over previous
import numpy as np
import ml_dtypes
from contextlib import ExitStack

import concourse.bass as bass
import concourse.mybir as mybir
from concourse.bass_utils import run_bass_kernel_spmd
from concourse.bass_core_rust import IndirectOffsetOnAxis

F32 = mybir.dt.float32
BF16 = mybir.dt.bfloat16
AF = mybir.ActivationFunctionType
ALU = mybir.AluOpType
AX = mybir.AxisListType
I32 = mybir.dt.int32

NCORES = 8
T = 4096
D = 1024
NT = T // 128
DC = D // 128
E = 8

BFNP = ml_dtypes.bfloat16

ENGINES = ("sync", "gpsimd", "tensor", "scalar", "vector")



class Step:
    __slots__ = ("idx", "eng", "fn", "reads", "writes", "dma", "deps",
                 "need_inc", "seq", "fam", "memberidx")

    def __init__(self, idx, eng, fn, reads, writes, dma):
        self.idx = idx
        self.eng = eng
        self.fn = fn
        self.reads = reads
        self.writes = writes
        self.dma = dma
        self.deps = []
        self.need_inc = False
        self.seq = 0
        self.fam = None
        self.memberidx = 0


class Sched:

    def __init__(self, nc, ctx, serialize=False):
        self.nc = nc
        self.ctx = ctx
        self.steps = []
        self.last_w = {}
        self.readers = {}
        self.serialize = serialize
        self.sems = {e: ctx.enter_context(nc.semaphore(f"sched_{e}")) for e in ENGINES}
        self.fams = {}
        self.prefix = ""
        self.nbar = 0

    def _fam(self, key, eng, bulk):
        f = self.fams.get(key)
        if f is None:
            f = {"sem": self.ctx.enter_context(self.nc.semaphore(f"fam_{len(self.fams)}")),
                 "eng": eng, "count": 0, "bulk": bulk, "key": key}
            self.fams[key] = f
        if f["eng"] != eng:
            raise RuntimeError(f"fam {key} used from {eng} and {f['eng']}")
        if bulk and not f["bulk"]:
            raise RuntimeError(f"fam {key} bulk mismatch")
        return f

    def step(self, eng, fn, reads=(), writes=(), dma=False, fam=None, bulk=False):
        i = len(self.steps)
        p = self.prefix
        reads = tuple((p, t) for t in reads)
        writes = tuple((p, t) for t in writes)
        st = Step(i, eng, fn, reads, writes, dma)
        if dma:
            if fam is None:
                w0 = writes[0][1] if writes else ("anon", i)
                fam = w0[0] if isinstance(w0, tuple) else w0
                fam = f"{eng}:{fam}" + ("" if not isinstance(w0, tuple) or len(w0) < 2
                                        else f":{w0[1]}")
            fam = f"{p}/{fam}"
            f = self._fam(fam, eng, bulk)
            f["count"] += 1
            f["last_idx"] = i
            st.fam = fam
            st.memberidx = f["count"]
        deps = set()
        for t in st.reads:
            if t in self.last_w:
                deps.add(self.last_w[t])
        for t in st.writes:
            if t in self.last_w:
                deps.add(self.last_w[t])
            for r in self.readers.get(t, ()):
                deps.add(r)
        if self.serialize and i > 0:
            deps.add(i - 1)
        st.deps = sorted(deps)
        self.steps.append(st)
        for t in st.reads:
            self.readers.setdefault(t, []).append(i)
        for t in st.writes:
            self.last_w[t] = i
            self.readers[t] = []
        return i

    def barrier(self):
        deps = set()
        last_eng = {}
        for st in self.steps:
            if st.fn is not None:
                last_eng[st.eng] = st.idx
        for f in self.fams.values():
            if f.get("last_idx") is not None:
                deps.add(f["last_idx"])
        deps.update(last_eng.values())
        self.nbar += 1
        for e in ENGINES:
            i = len(self.steps)
            st = Step(i, e, None, (), ((f"__bar{self.nbar}", e),), False)
            st.deps = sorted(d for d in deps if d != i)
            self.steps.append(st)

    def finalize(self):
        for st in self.steps:
            for d in st.deps:
                p = self.steps[d]
                if p.fn is None:
                    raise RuntimeError("dependency on a barrier step")
                if not (p.eng == st.eng == "tensor" and not p.dma):
                    p.need_inc = True
        counts = {e: 0 for e in ENGINES}
        for st in self.steps:
            if st.dma:
                st.need_inc = True
                continue
            if st.need_inc:
                counts[st.eng] += 1
                st.seq = counts[st.eng]
        self.final_counts = counts

    def _wait_of(self, p):
        if p.dma:
            f = self.fams[p.fam]
            v = 16 * (f["count"] if f["bulk"] else p.memberidx)
            return (p.fam, f["sem"], v)
        return (p.eng, self.sems[p.eng], p.seq)

    def emit_engine(self, name, eng):
        watermark = {}
        for st in self.steps:
            if st.eng != name:
                continue
            for d in st.deps:
                p = self.steps[d]
                if p.eng == name == "tensor" and not p.dma:
                    continue
                if (st.dma and p.dma and st.fam == p.fam
                        and self.fams[st.fam]["bulk"]):
                    continue
                if not p.need_inc:
                    raise RuntimeError("dep without inc")
                key, sem, val = self._wait_of(p)
                if val > watermark.get(key, 0):
                    eng.wait_ge(sem, val)
                    watermark[key] = val
            if st.fn is None:
                continue
            inst = st.fn(eng)
            if st.need_inc:
                if inst is None:
                    raise RuntimeError(f"step {st.idx} on {name} returned no inst")
                if st.dma:
                    inst.then_inc(self.fams[st.fam]["sem"], 16)
                else:
                    inst.then_inc(self.sems[name], 1)
        for f in self.fams.values():
            if f["eng"] == name and f["count"]:
                eng.wait_ge(f["sem"], 16 * f["count"])

    def build_block(self, ctx):
        self.finalize()
        block = ctx.enter_context(self.nc.Block())

        @block.sync
        def _(e):
            self.emit_engine("sync", e)

        @block.gpsimd
        def _(e):
            self.emit_engine("gpsimd", e)

        @block.tensor
        def _(e):
            self.emit_engine("tensor", e)

        @block.scalar
        def _(e):
            self.emit_engine("scalar", e)

        @block.vector
        def _(e):
            self.emit_engine("vector", e)



def host_consts(core_id):
    ident = np.eye(128, dtype=np.float32)
    ltri = (np.arange(128)[:, None] < np.arange(128)[None, :]).astype(np.float32)
    bmask = np.zeros((128, 128), np.float32)
    for g in range(32):
        bmask[g * 4:(g + 1) * 4, g * 4:(g + 1) * 4] = 1.0
    corelt = np.zeros((8, 8), np.float32)
    corelt[:core_id, 0] = 1.0
    return {"c_ident": ident, "c_ltri": ltri, "c_bmask": bmask,
            "c_corelt": corelt}



def record_ln_tile(S, c, bufs, x_dram, extra_stage_readers=()):
    nb = len(bufs["xt"])
    xt = bufs["xt"][c % nb]
    xn = bufs["xn"][c % nb]
    stage = bufs["stage"][c % nb]
    stat = bufs["stat"]
    sqd = bufs["sqd"]
    ps_tr = bufs["ps_tr"]
    ident_bf = bufs["ident_bf"]
    tk_xt = ("xt", c % nb)
    tk_xn = ("xn", c % nb)
    tk_st = ("stage", c % nb)

    S.step("sync", lambda e: e.dma_start(xt[:], x_dram[c * 128:(c + 1) * 128, :]),
           reads=(("x_dram", c),), writes=(tk_xt,), dma=True)
    S.step("scalar", lambda e: e.activation(sqd[:], xt[:], AF.Square,
                                            accum_out=stat[:, 1:2]),
           reads=(tk_xt,), writes=("lnsqd", "lnssq"))
    S.step("vector", lambda e: e.reduce_sum(stat[:, 0:1], xt[:], axis=AX.X),
           reads=(tk_xt,), writes=("lnssum",))
    S.step("vector", lambda e: e.tensor_scalar_mul(stat[:, 2:3], stat[:, 0:1], 1.0 / D),
           reads=("lnssum",), writes=("lnmean",))
    S.step("vector", lambda e: e.tensor_mul(stat[:, 7:8], stat[:, 2:3], stat[:, 2:3]),
           reads=("lnmean",), writes=("lnmsq",))
    S.step("vector", lambda e: e.tensor_scalar(stat[:, 3:4], stat[:, 1:2], 1.0 / D,
                                               stat[:, 7:8], ALU.mult, ALU.subtract),
           reads=("lnssq", "lnmsq"), writes=("lnvar",))
    S.step("scalar", lambda e: e.activation(stat[:, 4:5], stat[:, 3:4], AF.Sqrt,
                                            bias=bufs["epsc"][:, 0:1]),
           reads=("lnvar", "lneps"), writes=("lnsd",))
    S.step("vector", lambda e: e.reciprocal(stat[:, 5:6], stat[:, 4:5]),
           reads=("lnsd",), writes=("lnrstd",))
    S.step("vector", lambda e: e.tensor_scalar(xn[:], xt[:], stat[:, 2:3],
                                               stat[:, 5:6], ALU.subtract, ALU.mult),
           reads=(tk_xt, "lnmean", "lnrstd"), writes=(tk_xn,))

    def tr(e):
        for g in range(DC):
            i = e.transpose(ps_tr[:, g * 128:(g + 1) * 128],
                            xn[:, g * 128:(g + 1) * 128], ident_bf[:])
        return i
    S.step("tensor", tr, reads=(tk_xn, "ident_bf"), writes=("ps_tr",))
    if bufs.get("evac_dsts") is not None:
        dsts, dtok = bufs["evac_dsts"](c)
        def ev(e, dsts=dsts):
            for g in range(DC):
                i = e.copy(dsts[g], ps_tr[:, g * 128:(g + 1) * 128])
            return i
        S.step("scalar", ev, reads=("ps_tr",) + tuple(bufs.get("evac_extra_reads", ())),
               writes=(dtok,))
        return dtok
    S.step("scalar", lambda e: e.copy(stage[:], ps_tr[:]),
           reads=("ps_tr",), writes=(tk_st,))
    return tk_st


def alloc_ln_bufs(sbf, psf, nbuf=2):
    return {
        "xt": [sbf(f"ln_xt{i}", [128, D], F32) for i in range(nbuf)],
        "xn": [sbf(f"ln_xn{i}", [128, D], BF16) for i in range(nbuf)],
        "stage": [sbf(f"ln_st{i}", [128, D], BF16) for i in range(nbuf)],
        "sqd": sbf("ln_sqd", [128, D], F32),
        "stat": sbf("ln_stat", [128, 8], F32),
        "epsc": sbf("ln_eps", [128, 1], F32),
        "ps_tr": psf("ps_tr", [128, 1024], BF16),
        "ident_bf": sbf("ident_bf", [128, 128], BF16),
    }




class BufAlloc:
    ALIAS = {
        "w1_sb": ("wpool", 0), "w2_sb": ("wpool", 16384),
        "wkT_sb": ("wpool", 0), "wvT_sb": ("wpool", 8192),
        "wrT_sb": ("wpool", 16384), "woT_sb": ("wpool", 24576),
        "wq_sb": ("wpool", 0), "wo_sb": ("wpool", 24576),
        "h_sb": ("apool", 0), "xnG": ("apool", 16384),
        "xnTa": ("apool", 0),
        "xk_a": ("apool", 4232), "xv_a": ("apool", 8328), "xr_a": ("apool", 12424),
        "r_a": ("apool", 16520), "y_a": ("apool", 20616), "dbuf": ("apool", 24712),
        "qT": ("apool", 4128), "kT": ("apool", 8256),
        "vA": ("apool", 12384), "vB": ("apool", 16480),
        "Pb": ("apool", 20576), "PTb": ("apool", 21600), "OT": ("apool", 22624),
        "acc": ("fpool", 0),
        "E1": ("fpool", 0), "K1": ("fpool", 512), "EV": ("fpool", 1024),
        "KV": ("fpool", 1536), "Abuf": ("fpool", 2048), "Pbuf": ("fpool", 2564),
        "dnb": ("fpool", 3080), "recb": ("fpool", 3592), "wkvb": ("fpool", 4104),
        "Pf": ("fpool", 0), "ybst": ("fpool", 1024),
        "yb3_0": ("fpool", 2048), "yb3_1": ("fpool", 3072),
        "ln_sqd": ("fpool", 9216),
        "xgT": ("apool", 0), "h_sg": ("apool", 12288),
        "rowb0": ("apool", 20480), "rowb1": ("apool", 21520),
        "grow0": ("apool", 22560), "grow1": ("apool", 23600),
        "tokhl": ("apool", 24640),
        "ysb0": ("fpool", 0), "ysb1": ("fpool", 1024),
        "rank_sb": ("fpool", 2048), "cum": ("fpool", 2056),
        "selA": ("fpool", 2064), "selB": ("fpool", 2072),
        "scr8": ("fpool", 2080), "ercol": ("fpool", 2088),
        "slotf": ("fpool", 2094), "ecap8": ("fpool", 2096),
        "iota8f": ("fpool", 2104), "mwc": ("fpool", 2368),
        "tokf": ("fpool", 2400), "onescol2": ("fpool", 2136),
        "gsb": ("fpool", 2140), "m8": ("fpool", 2148), "esb": ("fpool", 2156),
        "selsb": ("fpool", 2164), "srow": ("fpool", 2172),
        "rkst0": ("fpool", 2180), "rkst1": ("fpool", 2188),
        "cb_row": ("fpool", 2196), "negcap": ("fpool", 2204),
        "nthr_sb": ("fpool", 2212), "ag_sb": ("fpool", 2220),
        "corelt_sb": ("fpool", 2228), "cnt_sb": ("fpool", 2236),
        "pref_sb": ("fpool", 2268), "nstag0": ("fpool", 2301),
        "nstag1": ("fpool", 2309), "keep_sb": ("fpool", 2317),
        "maskw_bf": ("apool", 24704),
    }
    POOLS = {"wpool": (BF16, 32768), "apool": (BF16, 25344), "fpool": (F32, 10240)}
    PSMAP = {
        "ffn": {"ps_g": "f0", "ps_h0": "f1", "ps_h1": "f2", "ps_y0": "f3",
                "ps_y1": "f4", "ps_cnt": "f5", "ps_rk": "f5",
                "ps_tr": "b0", "ps_mwt": "b1"},
        "l1a": {"ps_k": "f0", "ps_v": "f1", "ps_r": "f2", "ps_o0": "f3",
                "ps_o1": "f4", "ps_tr": "b0"},
        "l0a": {"ps_s0": "f0", "ps_s1": "f1", "ps_av0": "f2", "ps_av1": "f3",
                "ps_y0": "f4", "ps_y1": "f5", "ps_tr": "b0", "ps_pt": "b1"},
    }

    def __init__(self, nc, ctx):
        self.nc = nc
        self.ctx = ctx
        self.pools = {k: ctx.enter_context(nc.sbuf_tensor(k, [128, n], dt))
                      for k, (dt, n) in self.POOLS.items()}
        self.banks = {}
        for i in range(6):
            self.banks[f"f{i}"] = ctx.enter_context(
                nc.psum_tensor(f"bank_f{i}", [128, 512], F32))
        for i in range(2):
            self.banks[f"b{i}"] = ctx.enter_context(
                nc.psum_tensor(f"bank_b{i}", [128, 1024], BF16))
        self.memo = {}
        self.kind = None
        self.gp_reg = ctx.enter_context(nc.gpsimd.register("sprs_r0"))

    def buf(self, name, shape, dt):
        if name in self.ALIAS:
            pool, start = self.ALIAS[name]
            pdt, pn = self.POOLS[pool]
            assert dt == pdt and shape[0] <= 128 and start + shape[1] <= pn, \
                (name, shape, dt)
            return self.pools[pool][0:shape[0], start:start + shape[1]]
        if name in self.memo:
            t, s, d = self.memo[name]
            assert s == tuple(shape) and d == dt, (name, shape, s)
            return t
        t = self.ctx.enter_context(self.nc.sbuf_tensor(name, shape, dt))
        self.memo[name] = (t, tuple(shape), dt)
        return t

    def psum(self, name, shape, dt):
        key = self.PSMAP[self.kind][name]
        b = self.banks[key]
        assert (dt == F32 and key.startswith("f") and shape == [128, 512]) or \
               (dt == BF16 and key.startswith("b") and shape == [128, 1024]), \
               (name, shape, dt)
        return b


def record_ffn(S, dr, B, layer, split="all", debug=False, sparse=False):
    assert not debug
    H = 2048 if layer == 0 else 4096
    NFB = H // 2048
    FCB = 16
    G = 1024
    NG = T // G
    GT = G // 128
    NFI = H // 128
    MWSCALE = 0.5 if layer == 0 else 0.0625
    ACTF = AF.Gelu if layer == 0 else AF.Relu
    CAPTHR = 10239.5
    do_p1 = split in ("all", "count", "cnt1")
    do_p2 = split in ("all", "main")
    cap = (layer == 0)
    do_counts = cap or split in ("count", "cnt1")

    do_p1 = split in ("all", "count", "cnt1")
    do_p2 = split in ("all", "main")
    cap = (layer == 0)
    do_counts = cap or split in ("count", "cnt1")
    nc = S.nc
    sbf = B.buf
    psf = B.psum
    c_ident = dr["c_ident"]
    x_in = dr.get("x_in")
    x_out = dr.get("x_out")
    wg = dr.get("wg")
    rb = dr.get("rb")
    w1 = dr.get("w1")
    b1T = dr.get("b1T")
    w2 = dr.get("w2")
    b2 = dr.get("b2")
    xnT_hbm = dr.get("xnT")
    mw_in = dr.get("mw_in"); sel_in = dr.get("sel_in")
    agcnt = dr.get("agcnt"); pref_in = dr.get("pref_in")
    c_ltri = dr.get("c_ltri"); c_corelt = dr.get("c_corelt")
    nthr_hbm = dr.get("nthr")
    mw_out = dr.get("mw_out"); sel_out = dr.get("sel_out")
    pref_out = dr.get("pref_out")
    if True:

        ident_bf = sbf("ident_bf2", [128, 128], BF16)
        S.step("gpsimd", lambda e: e.dma_start(ident_bf[:], c_ident[:, :]),
               writes=("ident_bf",), dma=True)
        maskw = sbf("maskw", [128, NT * E], F32)
        maskw_bf = sbf("maskw_bf", [128, 8], BF16)
        maskwT = sbf("maskwT", [8, T], BF16)
        ps_g = psf("ps_g", [128, 512], F32)
        ps_mwt = psf("ps_mwt", [128, 1024], BF16)

        if do_p1:
            lnb = alloc_ln_bufs(sbf, psf)
            lnb["ident_bf"] = ident_bf
            wg_sb = sbf("wg_sb", [128, DC * E], BF16)
            gsb = sbf("gsb", [128, 8], F32)
            m8 = sbf("m8", [128, 8], F32)
            srow = sbf("srow", [128, 4], F32)
            esb = sbf("esb", [128, 8], F32)
            selsb = sbf("selsb", [128, 8], F32)
            S.step("vector", lambda e: e.memset(lnb["epsc"][:], 1e-5),
                   writes=("lneps",))
            S.step("gpsimd", lambda e: e.dma_start(
                wg_sb[:].rearrange("p (g e) -> p g e", e=E),
                wg[:, :].rearrange("(g p) e -> p g e", p=128)),
                writes=("wg_sb",), dma=True)
            if do_counts:
                ones_r = sbf("ones_r", [1, 128], F32)
                sel_a = sbf("sel_a", [128, NT * E], F32)
                onescol = sbf("onescol", [128, 1], F32)
                cnt_sb = sbf("cnt_sb", [8, NT], F32)
                pref_sb = sbf("pref_sb", [8, NT + 1], F32)
                ps_cnt = psf("ps_cnt", [128, 512], F32)
                S.step("vector", lambda e: e.memset(ones_r[:], 1.0), writes=("ones_r",))
                S.step("vector", lambda e: e.memset(onescol[:], 1.0), writes=("onescol",))
            if cap:
                rb_sb = sbf("rb_sb", [1, E], F32)
                S.step("sync", lambda e: e.dma_start(rb_sb[:], rb[:, :]),
                       writes=("rb_sb",), dma=True)

            for c in range(NT):
                tk_st = record_ln_tile(S, c, lnb, x_in)
                stage = lnb["stage"][c % 2]
                dst = xnT_hbm[:, c * 128:(c + 1) * 128].rearrange("(g p) k -> p g k", p=128)
                S.step("sync", lambda e, dst=dst, stage=stage: e.dma_start(
                    dst, stage[:].rearrange("p (g k) -> p g k", k=128)),
                    reads=(tk_st,), writes=(("xnT", c),), dma=True, fam=f"xnT{c % 2}")

                def gmm(e, stage=stage):
                    for g in range(DC):
                        i = e.matmul(ps_g[:, 0:8], stage[:, g * 128:(g + 1) * 128],
                                     wg_sb[:, g * E:(g + 1) * E],
                                     start=(g == 0), stop=(g == DC - 1 and not cap))
                    if cap:
                        i = e.matmul(ps_g[:, 0:8], ones_r[:], rb_sb[:],
                                     start=False, stop=True)
                    return i
                rds = (tk_st, "wg_sb") + (("ones_r", "rb_sb") if cap else ())
                S.step("tensor", gmm, reads=rds, writes=("ps_g",))
                S.step("vector", lambda e: e.tensor_copy(gsb[:], ps_g[:, 0:8]),
                       reads=("ps_g",), writes=("gsb",))
                S.step("vector", lambda e: e.max(out=m8[:], in_=gsb[:]),
                       reads=("gsb",), writes=("m8",))
                S.step("vector", lambda e: e.tensor_scalar_mul(srow[:, 0:1], m8[:, 0:1], -1.0),
                       reads=("m8",), writes=("negm1",))
                S.step("scalar", lambda e: e.activation(esb[:], gsb[:], AF.Exp,
                                                        bias=srow[:, 0:1]),
                       reads=("gsb", "negm1"), writes=("esb",))
                S.step("vector", lambda e: e.tensor_scalar(selsb[:], gsb[:], m8[:, 1:2],
                                                           None, ALU.subtract),
                       reads=("gsb", "m8"), writes=("selsb",))
                S.step("scalar", lambda e: e.activation(selsb[:], selsb[:], AF.Sign),
                       reads=("selsb",), writes=("selsb",))
                S.step("vector", lambda e: e.tensor_scalar(selsb[:], selsb[:], 1.0, 1.0,
                                                           ALU.add, ALU.min),
                       reads=("selsb",), writes=("selsb",))
                S.step("vector", lambda e: e.tensor_mul(esb[:], esb[:], selsb[:]),
                       reads=("esb", "selsb"), writes=("esb",))
                S.step("vector", lambda e: e.reduce_sum(srow[:, 1:2], esb[:], axis=AX.X),
                       reads=("esb",), writes=("essum",))
                S.step("vector", lambda e: e.reciprocal(srow[:, 2:3], srow[:, 1:2]),
                       reads=("essum",), writes=("esrs",))
                S.step("vector", lambda e, c=c: e.tensor_scalar(
                    maskw[:, c * E:(c + 1) * E], esb[:], srow[:, 2:3], MWSCALE,
                    ALU.mult, ALU.mult),
                    reads=("esb", "esrs"), writes=(("maskw", c),))
                if dr.get("xntok") is not None and split in ("count", "cnt1"):
                    xnt = lnb["xn"][c % 2]
                    S.step("sync", lambda e, c=c, xnt=xnt: e.dma_start(
                        dr["xntok"][c * 128:(c + 1) * 128, :], xnt[:]),
                        reads=(("xn", c % 2),), writes=(("xntok", c),), dma=True,
                        fam=f"xntok{c % 2}")
                if do_counts:
                    S.step("vector", lambda e, c=c: e.tensor_copy(
                        sel_a[:, c * E:(c + 1) * E], selsb[:]),
                        reads=("selsb",), writes=(("sel", c),))
                    S.step("tensor", lambda e, c=c: e.matmul(
                        ps_cnt[0:8, c:c + 1], sel_a[:, c * E:(c + 1) * E], onescol[:],
                        start=(c == 0), stop=(c == NT - 1), skip_group_check=True),
                        reads=(("sel", c), "onescol"), writes=(("ps_cnt", c),))
                if not do_counts:
                    S.step("scalar", lambda e, c=c: e.copy(
                        maskw_bf[:], maskw[:, c * E:(c + 1) * E]),
                        reads=(("maskw", c),), writes=("maskw_bf",))
                    S.step("tensor", lambda e: e.transpose(ps_mwt[0:8, 0:128], maskw_bf[:],
                                                           ident_bf[:]),
                           reads=("maskw_bf", "ident_bf"), writes=("ps_mwt",))
                    S.step("scalar", lambda e, c=c: e.copy(maskwT[0:8, c * 128:(c + 1) * 128],
                                                           ps_mwt[0:8, 0:128]),
                           reads=("ps_mwt",), writes=(("maskwT", c),))
            if do_counts:
                cnt_all = tuple(("ps_cnt", c) for c in range(NT))
                S.step("scalar", lambda e: e.copy(cnt_sb[:], ps_cnt[0:8, 0:NT]),
                       reads=cnt_all, writes=("cnt_sb",))
                S.step("vector", lambda e: e.memset(pref_sb[0:8, 0:1], 0.0),
                       writes=("pref0",))
                S.step("vector", lambda e: e.tensor_tensor_scan(
                    pref_sb[0:8, 1:NT + 1], onescol[0:8, 0:1].to_broadcast([8, NT]),
                    cnt_sb[0:8, :], 0.0, ALU.mult, ALU.add),
                    reads=("cnt_sb", "onescol"), writes=("pref_sb",))

        if split == "count":
            mw_all = tuple(("maskw", c) for c in range(NT))
            S.step("sync", lambda e: e.dma_start(mw_out[:, :], maskw[:]),
                   reads=mw_all, dma=True)
            S.step("sync", lambda e: e.dma_start(sel_out[:, :], sel_a[:]),
                   reads=tuple(("sel", c) for c in range(NT)), dma=True)
            S.step("sync", lambda e: e.dma_start(pref_out[:, :], pref_sb[:]),
                   reads=("pref_sb", "pref0"), dma=True)
            return
        if split == "cnt1":
            S.step("sync", lambda e: e.dma_start(pref_out[:, :], pref_sb[:]),
                   reads=("pref_sb", "pref0"), dma=True)
            ident_f32 = sbf("ident_f32", [128, 128], F32)
            S.step("sync", lambda e: e.dma_start(ident_f32[:], c_ident[:, :]),
                   writes=("ident_f32",), dma=True)
            ps_rk = psf("ps_rk", [128, 512], F32)
            nthr_sb = sbf("nthr_sb", [33, 8], F32)
            ltri_sb = sbf("ltri_sb", [128, 128], F32)
            S.step("sync", lambda e: e.dma_start(ltri_sb[:], dr["c_ltri"][:, :]),
                   writes=("ltri_sb",), dma=True)
            S.step("tensor", lambda e: e.transpose(
                ps_rk[0:33, 0:8], pref_sb[0:8, 0:33], ident_f32[0:8, 0:8]),
                reads=("pref_sb", "pref0", "ident_f32"), writes=("ps_rk",))
            S.step("scalar", lambda e: e.copy(nthr_sb[:], ps_rk[0:33, 0:8]),
                   reads=("ps_rk",), writes=("nthr_sb",))
            S.step("sync", lambda e: e.dma_start(dr["prefT"][0:33, :], nthr_sb[:]),
                   reads=("nthr_sb",), writes=("prefh",), dma=True)
            rank_src = {
                "sel_a": sel_a, "ltri_sb": ltri_sb, "ones_r": ones_r,
                "ps_rk": ps_rk,
                "xn_ap_of": lambda c: (dr["xntok"], ("xntok", c), True),
            }
            record_slot_scatter(S, dr, B, layer,
                                lambda c: (maskw[:, c * E:(c + 1) * E], ("maskw", c)),
                                rank_src, dr["gbuf"], dr["prefT"])
            return

        if split == "main":
            sel_a = sbf("sel_a", [128, NT * E], F32)
            ones_r = sbf("ones_r", [1, 128], F32)
            pref_sb = sbf("pref_sb", [8, NT + 1], F32)
            ag_sb = sbf("ag_sb", [8, 8], F32)
            corelt_sb = sbf("corelt_sb", [8, 8], F32)
            cb_row = sbf("cb_row", [1, 8], F32)
            negcap = sbf("negcap", [1, 8], F32)
            nthr_sb = sbf("nthr_sb", [33, 8], F32)
            nstag = [sbf(f"nstag{i}", [1, 8], F32) for i in range(2)]
            keep_sb = sbf("keep_sb", [128, 8], F32)
            ltri_sb = sbf("ltri_sb", [128, 128], F32)
            ident_f32 = sbf("ident_f32", [128, 128], F32)
            ps_rk = psf("ps_rk", [128, 512], F32)
            S.step("vector", lambda e: e.memset(ones_r[:], 1.0), writes=("ones_r",))
            S.step("sync", lambda e: e.dma_start(maskw[:], mw_in[:, :]),
                   writes=tuple(("maskw", c) for c in range(NT)), dma=True)
            S.step("sync", lambda e: e.dma_start(sel_a[:], sel_in[:, :]),
                   writes=tuple(("sel", c) for c in range(NT)), dma=True)
            S.step("sync", lambda e: e.dma_start(pref_sb[:], pref_in[:, :]),
                   writes=("pref_sb", "pref0"), dma=True)
            S.step("sync", lambda e: e.dma_start(ag_sb[:], agcnt[:, :]),
                   writes=("ag_sb",), dma=True)
            S.step("sync", lambda e: e.dma_start(corelt_sb[:], c_corelt[:, :]),
                   writes=("corelt_sb",), dma=True)
            S.step("sync", lambda e: e.dma_start(ltri_sb[:], c_ltri[:, :]),
                   writes=("ltri_sb",), dma=True)
            S.step("sync", lambda e: e.dma_start(ident_f32[:], c_ident[:, :]),
                   writes=("ident_f32",), dma=True)
            S.step("tensor", lambda e: e.matmul(ps_g[0:1, 0:8], corelt_sb[0:8, 0:1],
                                                ag_sb[0:8, :], start=True, stop=True),
                   reads=("corelt_sb", "ag_sb"), writes=("ps_g",))
            S.step("scalar", lambda e: e.copy(cb_row[:], ps_g[0:1, 0:8]),
                   reads=("ps_g",), writes=("cb_row",))
            S.step("vector", lambda e: e.tensor_scalar(negcap[:], cb_row[:], 1.0,
                                                       -CAPTHR, ALU.mult, ALU.add),
                   reads=("cb_row",), writes=("negcap",))

            S.step("tensor", lambda e: e.transpose(
                ps_rk[0:33, 0:8], pref_sb[0:8, 0:33], ident_f32[0:8, 0:8]),
                reads=("pref_sb", "pref0", "ident_f32"), writes=("ps_rk",))
            S.step("scalar", lambda e: e.copy(nthr_sb[:], ps_rk[0:33, 0:8]),
                   reads=("ps_rk",), writes=("nthr_sb",))
            S.step("tensor", lambda e: e.matmul(
                ps_g[0:33, 0:8], ones_r[0:1, 0:33], negcap[:], start=True, stop=True),
                reads=("negcap", "ones_r"), writes=("ps_g",))
            S.step("vector", lambda e: e.tensor_add(
                nthr_sb[:], nthr_sb[:], ps_g[0:33, 0:8]),
                reads=("nthr_sb", "ps_g"), writes=("nthr_sb",))
            S.step("sync", lambda e: e.dma_start(nthr_hbm[0:33, :], nthr_sb[:]),
                   reads=("nthr_sb",), writes=("nthr_hbm",), dma=True)
            for c in range(NT):
                S.step("gpsimd", lambda e, c=c: e.dma_start(
                    nstag[c % 2][:], nthr_hbm[c:c + 1, :]),
                    reads=("nthr_hbm",), writes=(("nstag", c % 2),), dma=True)

                def rkmm(e, c=c):
                    e.matmul(ps_rk[:, 0:8], ltri_sb[:], sel_a[:, c * E:(c + 1) * E],
                             start=True, stop=False)
                    return e.matmul(ps_rk[:, 0:8], ones_r[:], nstag[c % 2][:],
                                    start=False, stop=True)
                S.step("tensor", rkmm, reads=(("sel", c), ("nstag", c % 2),
                                              "ltri_sb", "ones_r"), writes=("ps_rk",))
                S.step("scalar", lambda e: e.activation(keep_sb[:], ps_rk[:, 0:8],
                                                        AF.Sign),
                       reads=("ps_rk",), writes=("keep_sb",))
                S.step("vector", lambda e: e.tensor_scalar(keep_sb[:], keep_sb[:],
                                                           -1.0, 0.0, ALU.mult, ALU.max),
                       reads=("keep_sb",), writes=("keep_sb",))
                S.step("vector", lambda e, c=c: e.tensor_mul(
                    maskw[:, c * E:(c + 1) * E], maskw[:, c * E:(c + 1) * E],
                    keep_sb[:]),
                    reads=(("maskw", c), "keep_sb"), writes=(("maskw", c),))
                if not sparse:
                    S.step("scalar", lambda e, c=c: e.copy(
                        maskw_bf[:], maskw[:, c * E:(c + 1) * E]),
                        reads=(("maskw", c),), writes=("maskw_bf",))
                    S.step("tensor", lambda e: e.transpose(ps_mwt[0:8, 0:128], maskw_bf[:],
                                                           ident_bf[:]),
                           reads=("maskw_bf", "ident_bf"), writes=("ps_mwt",))
                    S.step("scalar", lambda e, c=c: e.copy(
                        maskwT[0:8, c * 128:(c + 1) * 128], ps_mwt[0:8, 0:128]),
                        reads=("ps_mwt",), writes=(("maskwT", c),))
            if sparse:
                rank_src = {
                    "sel_a": sel_a, "ltri_sb": ltri_sb, "ones_r": ones_r,
                    "ps_rk": ps_rk,
                    "xn_ap_of": lambda c: (dr["xntok"], ("xntok_ext",), True),
                }
                record_slot_scatter(
                    S, dr, B, layer,
                    lambda c: (maskw[:, c * E:(c + 1) * E], ("maskw", c)),
                    rank_src, dr["gbuf"], dr["prefT"])
                record_ffn_sparse_main(S, dr, B, layer)
                return

        b1T_sb = sbf(f"b1T_sb{layer}", [128, NFI * E], F32)
        b2_bf = sbf("b2_bf", [8, D], BF16)
        xnG = sbf("xnG", [128, DC * G], BF16)
        h_sb = sbf("h_sb", [128, FCB * G], BF16)
        w1_sb = sbf("w1_sb", [128, DC * FCB * 128], BF16)
        w2_sb = sbf("w2_sb", [128, FCB * D], BF16)
        acc = sbf("acc", [128, GT * D], F32)
        xres = [sbf(f"xres{i}", [128, D], F32) for i in range(2)]
        xo = [sbf(f"xo{i}", [128, D], F32) for i in range(2)]
        ps_h = [psf(f"ps_h{i}", [128, 512], F32) for i in range(2)]
        ps_y = [psf(f"ps_y{i}", [128, 512], F32) for i in range(2)]
        S.step("gpsimd", lambda e: e.dma_start(b2_bf[:], b2[:, :]),
               writes=("b2_bf",), dma=True)
        S.step("sync", lambda e: e.dma_start(b1T_sb[:], b1T[:, :]),
               writes=("b1T_sb",), dma=True)

        mwT_all = tuple(("maskwT", c) for c in range(NT))
        if debug:
            S.step("sync", lambda e: e.dma_start(dbg_mw[:, :], maskw[:]),
                   reads=tuple(("maskw", c) for c in range(NT)), dma=True)
        xnT_rd = (lambda c: ("xnT", c)) if split != "main" else (lambda c: "xnT_ext")
        if split == "main":
            S.step("vector", lambda e: e.engine_nop(), reads=(), writes=("xnT_ext",))

        for g4 in range(NG):
            src = xnT_hbm[:, g4 * G:(g4 + 1) * G].rearrange("(g p) t -> p g t", p=128)
            S.step("sync", lambda e, src=src: e.dma_start(
                xnG[:].rearrange("p (g t) -> p g t", t=G), src),
                reads=tuple(xnT_rd(c) for c in range(g4 * GT, (g4 + 1) * GT)),
                writes=("xnG",), dma=True)
            for g8 in range(GT):
                c = g4 * GT + g8
                for dt in range(2):
                    S.step("tensor", lambda e, c=c, dt=dt: e.matmul(
                        ps_y[dt][:], maskwT[0:8, c * 128:(c + 1) * 128],
                        b2_bf[0:8, dt * 512:(dt + 1) * 512], start=True, stop=True),
                        reads=mwT_all + ("b2_bf",), writes=(f"ps_y{dt}",))
                    S.step("scalar", lambda e, g8=g8, dt=dt: e.copy(
                        acc[:, g8 * D + dt * 512: g8 * D + (dt + 1) * 512], ps_y[dt][:]),
                        reads=(f"ps_y{dt}",), writes=(("acc", g8, dt),))
            for e_i in range(E):
                for fb in range(NFB):
                    src1 = w1[e_i, :, fb * 2048:(fb + 1) * 2048].rearrange(
                        "(dc p) (fc k) -> p dc fc k", p=128, k=128)
                    S.step("gpsimd", lambda e, src1=src1: e.dma_start(
                        w1_sb[:].rearrange("p (dc fc k) -> p dc fc k", fc=FCB, k=128),
                        src1),
                        writes=("w1_sb",), dma=True)
                    src2 = w2[e_i, fb * 2048:(fb + 1) * 2048, :].rearrange(
                        "(fc p) d -> p fc d", p=128)
                    S.step("gpsimd", lambda e, src2=src2: e.dma_start(
                        w2_sb[:].rearrange("p (fc d) -> p fc d", d=D), src2),
                        writes=("w2_sb",), dma=True)
                    for fc in range(FCB):
                        for tb in range(2):
                            def hmm(e, fc=fc, tb=tb):
                                for dc in range(DC):
                                    i = e.matmul(
                                        ps_h[tb][:],
                                        w1_sb[:, (dc * FCB + fc) * 128:
                                              (dc * FCB + fc + 1) * 128],
                                        xnG[:, dc * G + tb * 512: dc * G + (tb + 1) * 512],
                                        start=(dc == 0), stop=(dc == DC - 1))
                                return i
                            S.step("tensor", hmm, reads=("w1_sb", "xnG"),
                                   writes=(f"ps_h{tb}",))
                            bcol = (fb * FCB + fc) * E + e_i
                            S.step("scalar", lambda e, fc=fc, tb=tb, bcol=bcol:
                                   e.activation(
                                       h_sb[:, fc * G + tb * 512: fc * G + (tb + 1) * 512],
                                       ps_h[tb][:], ACTF,
                                       bias=b1T_sb[:, bcol:bcol + 1]),
                                   reads=(f"ps_h{tb}", "b1T_sb"),
                                   writes=(("h", fc, tb),))
                    h_all = tuple(("h", fc, tb) for fc in range(FCB) for tb in range(2))
                    for g8 in range(GT):
                        for dt in range(2):
                            def ymm(e, g8=g8, dt=dt):
                                for fc in range(FCB):
                                    i = e.matmul(
                                        ps_y[dt][:],
                                        h_sb[:, fc * G + g8 * 128: fc * G + g8 * 128 + 128],
                                        w2_sb[:, fc * D + dt * 512: fc * D + (dt + 1) * 512],
                                        start=(fc == 0), stop=(fc == FCB - 1))
                                return i
                            S.step("tensor", ymm, reads=h_all + ("w2_sb",),
                                   writes=(f"ps_y{dt}",))
                            mcol = (g4 * GT + g8) * E + e_i
                            S.step("vector", lambda e, g8=g8, dt=dt, mcol=mcol:
                                   e.scalar_tensor_tensor(
                                       acc[:, g8 * D + dt * 512: g8 * D + (dt + 1) * 512],
                                       ps_y[dt][:], maskw[:, mcol:mcol + 1],
                                       acc[:, g8 * D + dt * 512: g8 * D + (dt + 1) * 512],
                                       ALU.mult, ALU.add),
                                   reads=(f"ps_y{dt}", ("maskw", g4 * GT + g8),
                                          ("acc", g8, dt)),
                                   writes=(("acc", g8, dt),))
            if debug and g4 == 0:
                S.step("sync", lambda e: e.dma_start(dbg_acc[:, :], acc[:]),
                       reads=tuple(("acc", g8, dt) for g8 in range(GT) for dt in range(2)),
                       dma=True)
            for g8 in range(GT):
                c = g4 * GT + g8
                S.step("sync", lambda e, c=c: e.dma_start(
                    xres[c % 2][:], x_in[c * 128:(c + 1) * 128, :]),
                    reads=(("x_dram", c),), writes=(("xres", c % 2),), dma=True)
                S.step("vector", lambda e, c=c, g8=g8: e.tensor_add(
                    xo[c % 2][:], xres[c % 2][:], acc[:, g8 * D:(g8 + 1) * D]),
                    reads=(("xres", c % 2), ("acc", g8, 0), ("acc", g8, 1)),
                    writes=(("xo", c % 2),))
                S.step("sync", lambda e, c=c: e.dma_start(
                    x_out[c * 128:(c + 1) * 128, :], xo[c % 2][:]),
                    reads=(("xo", c % 2),), writes=(("xout", c),), dma=True, fam=f"xout{c % 2}")







def record_l1a(S, dr, B, debug=False):
    assert not debug
    TC = 512
    NCH = T // TC
    W5 = TC + 1

    nc = S.nc
    sbf = B.buf
    psf = B.psum
    c_ident = dr["c_ident"]
    x_in = dr["x_in"]
    x_out = dr["x_out"]
    wkT = dr["wkT"]; wvT = dr["wvT"]; wrT = dr["wrT"]; woT = dr["woT"]
    tmk5 = dr["tmk5"]; tmv5 = dr["tmv5"]; tmr5 = dr["tmr5"]
    dec5 = dr["dec5"]; fst5 = dr["fst5"]
    if True:

        lnb = alloc_ln_bufs(sbf, psf)
        S.step("gpsimd", lambda e: e.dma_start(lnb["ident_bf"][:], c_ident[:, :]),
               writes=("ident_bf",), dma=True)
        xnTa = sbf("xnTa", [128, DC * W5], BF16)

        def evac_dsts(c):
            ti = c % 4
            dsts = [xnTa[:, g * W5 + 1 + ti * 128: g * W5 + 1 + (ti + 1) * 128]
                    for g in range(DC)]
            return dsts, ("xnTa", c // 4)
        lnb["evac_dsts"] = evac_dsts
        lnb["evac_extra_reads"] = ("halo",)

        wT_sb = {}
        for nm, src in (("k", wkT), ("v", wvT), ("r", wrT), ("o", woT)):
            wT_sb[nm] = sbf(f"w{nm}T_sb", [128, DC * D], BF16)
            S.step("gpsimd", lambda e, nm=nm, src=src: e.dma_start(
                wT_sb[nm][:].rearrange("p (g e) -> p g e", e=D),
                src[:, :].rearrange("(g p) e -> p g e", p=128)),
                writes=(f"w{nm}T",), dma=True)
        tm_sb = {}
        for nm, src in (("k", tmk5), ("v", tmv5), ("r", tmr5), ("d", dec5), ("u", fst5)):
            tm_sb[nm] = sbf(f"tm{nm}_sb", [128, DC], F32)
            S.step("sync", lambda e, nm=nm, src=src: e.dma_start(tm_sb[nm][:], src[:, :]),
                   writes=(f"tm{nm}",), dma=True)
        ew = sbf("ew", [128, DC], F32)
        S.step("scalar", lambda e: e.activation(ew[:], tm_sb["d"][:], AF.Exp),
               reads=("tmd",), writes=("ew",))
        S.step("vector", lambda e: e.memset(lnb["epsc"][:], 1e-5), writes=("lneps",))

        Ast = sbf("Ast", [128, DC], F32)
        Pst = sbf("Pst", [128, DC], F32)
        S.step("vector", lambda e: e.memset(Ast[:], 0.0), writes=("Ast",))
        S.step("vector", lambda e: e.memset(Pst[:], 0.0), writes=("Pst",))
        S.step("vector", lambda e: e.memset(
            xnTa[:].rearrange("p (g c) -> p g c", c=W5)[:, :, 0:1], 0.0),
            writes=("halo",))

        dbuf = sbf("dbuf", [128, TC], BF16)
        xk_a = sbf("xk_a", [128, DC * TC], BF16)
        xv_a = sbf("xv_a", [128, DC * TC], BF16)
        xr_a = sbf("xr_a", [128, DC * TC], BF16)
        E1 = sbf("E1", [128, TC], F32)
        K1 = sbf("K1", [128, TC], F32)
        EV = sbf("EV", [128, TC], F32)
        KV = sbf("KV", [128, TC], F32)
        Abuf = sbf("Abuf", [128, W5], F32)
        Pbuf = sbf("Pbuf", [128, W5], F32)
        dnb = sbf("dnb", [128, TC], F32)
        recb = sbf("recb", [128, TC], F32)
        wkvb = sbf("wkvb", [128, TC], F32)
        r_a = sbf("r_a", [128, DC * TC], BF16)
        y_a = sbf("y_a", [128, DC * TC], BF16)
        xres = [sbf(f"xres{i}", [128, D], F32) for i in range(2)]
        xo = [sbf(f"xo{i}", [128, D], F32) for i in range(2)]

        ps_k = psf("ps_k", [128, 512], F32)
        ps_v = psf("ps_v", [128, 512], F32)
        ps_r = psf("ps_r", [128, 512], F32)
        ps_o = [psf(f"ps_o{i}", [128, 512], F32) for i in range(2)]

        for q in range(NCH):
            for ti in range(4):
                record_ln_tile(S, q * 4 + ti, lnb, x_in)
            tkx = ("xnTa", q)
            for dc in range(DC):
                sl = xnTa[:, dc * W5: (dc + 1) * W5]
                S.step("vector", lambda e, sl=sl: e.tensor_sub(
                    dbuf[:], sl[:, 1:W5], sl[:, 0:TC]),
                    reads=(tkx, "halo"), writes=("dbuf",))
                for nm, arr in (("k", xk_a), ("v", xv_a), ("r", xr_a)):
                    S.step("vector", lambda e, sl=sl, nm=nm, arr=arr, dc=dc:
                           e.scalar_tensor_tensor(
                               arr[:, dc * TC:(dc + 1) * TC], dbuf[:],
                               tm_sb[nm][:, dc:dc + 1], sl[:, 0:TC],
                               ALU.mult, ALU.add),
                           reads=("dbuf", f"tm{nm}", tkx, "halo"),
                           writes=((f"x{nm}", dc),))
            if q < NCH - 1:
                def haloc(e):
                    for dc in range(DC):
                        i = e.tensor_copy(xnTa[:, dc * W5: dc * W5 + 1],
                                          xnTa[:, dc * W5 + TC: dc * W5 + TC + 1])
                    return i
                S.step("vector", haloc, reads=(tkx,), writes=("halo",))
            xk_all = tuple(("xk", dc) for dc in range(DC))
            xv_all = tuple(("xv", dc) for dc in range(DC))
            xr_all = tuple(("xr", dc) for dc in range(DC))
            for ec in range(DC):
                def kmm(e, ec=ec, arr=xk_a, w="k", ps=ps_k):
                    for dc in range(DC):
                        i = e.matmul(ps[:], wT_sb[w][:, dc * D + ec * 128:
                                                     dc * D + (ec + 1) * 128],
                                     arr[:, dc * TC:(dc + 1) * TC],
                                     start=(dc == 0), stop=(dc == DC - 1))
                    return i
                S.step("tensor", lambda e, ec=ec: kmm(e, ec, xk_a, "k", ps_k),
                       reads=xk_all + ("wkT",), writes=("ps_k",))
                S.step("tensor", lambda e, ec=ec: kmm(e, ec, xv_a, "v", ps_v),
                       reads=xv_all + ("wvT",), writes=("ps_v",))
                S.step("tensor", lambda e, ec=ec: kmm(e, ec, xr_a, "r", ps_r),
                       reads=xr_all + ("wrT",), writes=("ps_r",))
                S.step("scalar", lambda e, ec=ec: e.activation(
                    E1[:], ps_k[:], AF.Exp, bias=tm_sb["u"][:, ec:ec + 1]),
                    reads=("ps_k", "tmu"), writes=("E1",))
                S.step("scalar", lambda e: e.activation(K1[:], ps_k[:], AF.Exp),
                       reads=("ps_k",), writes=("K1",))
                S.step("vector", lambda e: e.tensor_mul(EV[:], E1[:], ps_v[:]),
                       reads=("E1", "ps_v"), writes=("EV",))
                S.step("vector", lambda e: e.tensor_mul(KV[:], K1[:], ps_v[:]),
                       reads=("K1", "ps_v"), writes=("KV",))
                S.step("scalar", lambda e, ec=ec: e.activation(
                    r_a[:, ec * TC:(ec + 1) * TC], ps_r[:], AF.Sigmoid),
                    reads=("ps_r",), writes=(("r", ec),))
                S.step("vector", lambda e, ec=ec: e.tensor_copy(
                    Abuf[:, 0:1], Ast[:, ec:ec + 1]), reads=("Ast",), writes=("Abuf0",))
                S.step("vector", lambda e, ec=ec: e.tensor_tensor_scan(
                    Abuf[:, 1:W5], ew[:, ec:ec + 1].to_broadcast([128, TC]), KV[:],
                    Ast[:, ec:ec + 1], ALU.mult, ALU.add),
                    reads=("KV", "ew", "Ast"), writes=("Abuf",))
                S.step("vector", lambda e, ec=ec: e.tensor_copy(
                    Ast[:, ec:ec + 1], Abuf[:, TC:W5]),
                    reads=("Abuf",), writes=("Ast",))
                S.step("vector", lambda e, ec=ec: e.tensor_copy(
                    Pbuf[:, 0:1], Pst[:, ec:ec + 1]), reads=("Pst",), writes=("Pbuf0",))
                S.step("vector", lambda e, ec=ec: e.tensor_tensor_scan(
                    Pbuf[:, 1:W5], ew[:, ec:ec + 1].to_broadcast([128, TC]), K1[:],
                    Pst[:, ec:ec + 1], ALU.mult, ALU.max),
                    reads=("K1", "ew", "Pst"), writes=("Pbuf",))
                S.step("vector", lambda e, ec=ec: e.tensor_copy(
                    Pst[:, ec:ec + 1], Pbuf[:, TC:W5]),
                    reads=("Pbuf",), writes=("Pst",))
                S.step("vector", lambda e: e.tensor_add(dnb[:], Pbuf[:, 0:TC], E1[:]),
                       reads=("Pbuf", "Pbuf0", "E1"), writes=("dnb",))
                S.step("vector", lambda e: e.reciprocal(recb[:], dnb[:]),
                       reads=("dnb",), writes=("recb",))
                S.step("vector", lambda e: e.tensor_add(EV[:], Abuf[:, 0:TC], EV[:]),
                       reads=("Abuf", "Abuf0", "EV"), writes=("EV",))
                S.step("vector", lambda e: e.tensor_mul(wkvb[:], EV[:], recb[:]),
                       reads=("EV", "recb"), writes=("wkvb",))
                S.step("vector", lambda e, ec=ec: e.tensor_mul(
                    y_a[:, ec * TC:(ec + 1) * TC], wkvb[:],
                    r_a[:, ec * TC:(ec + 1) * TC]),
                    reads=("wkvb", ("r", ec)), writes=(("y", ec),))
                if debug and q == 0 and ec == 0:
                    S.step("gpsimd", lambda e: e.dma_start(
                        dbg_ry[:, 0:512], r_a[:, 0:512]), reads=(("r", 0),), dma=True)
                    S.step("gpsimd", lambda e: e.dma_start(
                        dbg_ry[:, 512:1024], y_a[:, 0:512]), reads=(("y", 0),), dma=True)
                    for j, (nm, t) in enumerate(
                            (("E1", E1), ("K1", K1), ("EV", EV), ("KV", KV),
                             ("wkv", wkvb), ("dn", dnb))):
                        S.step("sync", lambda e, j=j, t=t: e.dma_start(
                            dbg_kv[:, j * 512:(j + 1) * 512], t[:]),
                            reads=(nm if nm in ("E1", "K1", "KV", "dn") else
                                   ("EV" if nm == "EV" else "wkvb"),), dma=True)
            y_all = tuple(("y", ec) for ec in range(DC))
            if debug and q == 0:
                S.step("gpsimd", lambda e: e.dma_start(dbg_ya[:, :], y_a[:]),
                       reads=y_all, dma=True)
            for ti in range(4):
                c = q * 4 + ti
                S.step("sync", lambda e, c=c, ti=ti: e.dma_start(
                    xres[ti % 2][:], x_in[c * 128:(c + 1) * 128, :]),
                    reads=(("x_dram", c),), writes=(("xres", ti % 2),), dma=True)
                for eh in range(2):
                    def omm(e, ti=ti, eh=eh):
                        for dc in range(DC):
                            i = e.matmul(ps_o[eh][:],
                                         y_a[:, dc * TC + ti * 128: dc * TC + (ti + 1) * 128],
                                         wT_sb["o"][:, dc * D + eh * 512:
                                                    dc * D + (eh + 1) * 512],
                                         start=(dc == 0), stop=(dc == DC - 1))
                        return i
                    S.step("tensor", omm, reads=y_all + ("woT",),
                           writes=(f"ps_o{eh}",))
                    S.step("vector", lambda e, ti=ti, eh=eh: e.tensor_add(
                        xo[ti % 2][:, eh * 512:(eh + 1) * 512],
                        xres[ti % 2][:, eh * 512:(eh + 1) * 512], ps_o[eh][:]),
                        reads=(f"ps_o{eh}", ("xres", ti % 2)),
                        writes=(("xo", ti % 2, eh),))
                S.step("sync", lambda e, c=c, ti=ti: e.dma_start(
                    x_out[c * 128:(c + 1) * 128, :], xo[ti % 2][:]),
                    reads=(("xo", ti % 2, 0), ("xo", ti % 2, 1)),
                    writes=(("xout", c),), dma=True, fam=f"xout{ti % 2}")
                if debug and q == 0 and ti == 0:
                    S.step("sync", lambda e: e.dma_start(dbg_xr[:, 0:D], xres[0][:]),
                           reads=(("xres", 0),), dma=True)
                    S.step("sync", lambda e: e.dma_start(dbg_xr[:, D:2*D], xo[0][:]),
                           reads=(("xo", 0, 0), ("xo", 0, 1)), dma=True)



def _l1a_inputs(x_shard, inputs, core_id):
    cons = host_consts(core_id)
    def five(v):
        return np.ascontiguousarray(
            np.asarray(v, np.float32).reshape(DC, 128).T)
    return {
        "x_in": x_shard,
        "wkT": np.ascontiguousarray(np.asarray(inputs["L1_attn_W_k"], np.float32).T),
        "wvT": np.ascontiguousarray(np.asarray(inputs["L1_attn_W_v"], np.float32).T),
        "wrT": np.ascontiguousarray(np.asarray(inputs["L1_attn_W_r"], np.float32).T),
        "woT": np.ascontiguousarray(np.asarray(inputs["L1_attn_W_o"], np.float32).T),
        "tmk5": five(inputs["L1_attn_time_mix_k"]),
        "tmv5": five(inputs["L1_attn_time_mix_v"]),
        "tmr5": five(inputs["L1_attn_time_mix_r"]),
        "dec5": five(inputs["L1_attn_time_decay"]),
        "fst5": five(inputs["L1_attn_time_first"]),
        "c_ident": cons["c_ident"],
    }




def record_l0a(S, dr, B, debug=False):
    assert not debug
    TC = 512
    NCH = T // TC
    WA = TC + 4
    NH = 8
    SCALE = D ** -0.5

    nc = S.nc
    sbf = B.buf
    psf = B.psum
    c_ident = dr["c_ident"]
    c_bmask = dr["c_bmask"]
    x_in = dr["x_in"]
    x_out = dr["x_out"]
    wqkv = dr["wqkv"]; wout = dr["wout"]; bout = dr["bout"]
    xnT_hbm = dr["xnT"]
    yB_hbm = dr["yB"]
    if True:

        lnb = alloc_ln_bufs(sbf, psf)
        S.step("gpsimd", lambda e: e.dma_start(lnb["ident_bf"][:], c_ident[:, :]),
               writes=("ident_bf",), dma=True)
        S.step("vector", lambda e: e.memset(lnb["epsc"][:], 1e-5), writes=("lneps",))

        wq_sb = sbf("wq_sb", [128, DC * 3 * D], BF16)
        wo_sb = sbf("wo_sb", [128, DC * D], BF16)
        bout_sb = sbf("bout_sb", [1, D], F32)
        ones_r = sbf("ones_r", [1, 128], F32)
        bm_sb = sbf("bm_sb", [128, 128], F32)
        S.step("gpsimd", lambda e: e.dma_start(
            wq_sb[:].rearrange("p (g f) -> p g f", f=3 * D),
            wqkv[:, :].rearrange("(g p) f -> p g f", p=128)),
            writes=("wq_sb",), dma=True)
        S.step("gpsimd", lambda e: e.dma_start(
            wo_sb[:].rearrange("p (g f) -> p g f", f=D),
            wout[:, :].rearrange("(g p) f -> p g f", p=128)),
            writes=("wo_sb",), dma=True)
        S.step("sync", lambda e: e.dma_start(bout_sb[:], bout[:, :]),
               writes=("bout_sb",), dma=True)
        S.step("sync", lambda e: e.dma_start(bm_sb[:], c_bmask[:, :]),
               writes=("bm_sb",), dma=True)
        S.step("vector", lambda e: e.memset(ones_r[:], 1.0), writes=("ones_r",))

        xnTa = sbf("xnTa", [128, DC * WA], BF16)
        qT = sbf("qT", [128, NH * WA], BF16)
        kT = sbf("kT", [128, NH * WA], BF16)
        vA = sbf("vA", [128, 4 * D], BF16)
        vB = sbf("vB", [128, 4 * D], BF16)
        Pf = sbf("Pf", [128, NH * 128], F32)
        nmax = sbf("nmax", [128, NH], F32)
        dsum = sbf("dsum", [128, NH], F32)
        rden = sbf("rden", [128, NH], F32)
        Pb = sbf("Pb", [128, NH * 128], BF16)
        PTb = sbf("PTb", [128, NH * 128], BF16)
        OT = sbf("OT", [128, NH * 128], BF16)
        ybst = sbf("ybst", [128, D], F32)
        xres = [sbf(f"xres{i}", [128, D], F32) for i in range(2)]
        xo = [sbf(f"xo{i}", [128, D], F32) for i in range(2)]
        yb3 = [sbf(f"yb3_{i}", [128, D], F32) for i in range(2)]

        ps_s = [psf(f"ps_s{i}", [128, 512], F32) for i in range(2)]
        ps_pt = psf("ps_pt", [128, 1024], BF16)
        ps_av = [psf(f"ps_av{i}", [128, 512], F32) for i in range(2)]
        ps_y = [psf(f"ps_y{i}", [128, 512], F32) for i in range(2)]

        for c in range(NT):
            tk_st = record_ln_tile(S, c, lnb, x_in)
            stage = lnb["stage"][c % 2]
            dst = xnT_hbm[:, c * 128:(c + 1) * 128].rearrange("(g p) k -> p g k", p=128)
            S.step("sync", lambda e, dst=dst, stage=stage: e.dma_start(
                dst, stage[:].rearrange("p (g k) -> p g k", k=128)),
                reads=(tk_st,), writes=(("xnT", c),), dma=True, fam=f"xnT{c % 2}")

        S.step("vector", lambda e: e.memset(ybst[:], 0.0), writes=("ybst",))
        S.step("sync", lambda e: e.dma_start(yB_hbm[0:2, :], ybst[0:2, :]),
               reads=("ybst",), writes=("yB",), dma=True)
        S.step("sync", lambda e: e.dma_start(yB_hbm[T - 2:T, :], ybst[0:2, :]),
               reads=("ybst",), writes=("yB",), dma=True)

        for q in range(NCH):
            t0 = q * TC
            w_load = TC + 4 if q < NCH - 1 else TC
            src = xnT_hbm[:, t0:t0 + w_load].rearrange("(g p) t -> p g t", p=128)
            dstv = xnTa[:].rearrange("p (g j) -> p g j", j=WA)[:, :, 0:w_load]
            S.step("sync", lambda e, src=src, dstv=dstv: e.dma_start(dstv, src),
                   reads=tuple(("xnT", c) for c in range(q * 4, min(NT, q * 4 + 5))),
                   writes=("xnTa",), dma=True)
            if q == NCH - 1:
                S.step("vector", lambda e: e.memset(
                    xnTa[:].rearrange("p (g j) -> p g j", j=WA)[:, :, TC:WA], 0.0),
                    reads=(), writes=("xnTa_pad",))
            else:
                S.step("vector", lambda e: e.engine_nop(), writes=("xnTa_pad",))

            for h in range(NH):
                for nm, arr, fof in (("q", qT, 0), ("k", kT, D)):
                    def qkmm(e, h=h, arr=arr, fof=fof, main=True):
                        lo, hi = (0, TC) if main else (TC, WA)
                        for dc in range(DC):
                            i = e.matmul(
                                ps_s[0][:, 0:hi - lo],
                                xnTa[:, dc * WA + lo: dc * WA + hi],
                                wq_sb[:, dc * 3 * D + fof + h * 128:
                                      dc * 3 * D + fof + (h + 1) * 128],
                                start=(dc == 0), stop=(dc == DC - 1))
                        return i
                    def qkmmT(e, h=h, fof=fof, lo=0, hi=TC):
                        for dc in range(DC):
                            i = e.matmul(
                                ps_s[0][:, 0:hi - lo],
                                wq_sb[:, dc * 3 * D + fof + h * 128:
                                      dc * 3 * D + fof + (h + 1) * 128],
                                xnTa[:, dc * WA + lo: dc * WA + hi],
                                start=(dc == 0), stop=(dc == DC - 1))
                        return i
                    S.step("tensor", lambda e, f=qkmmT: f(e),
                           reads=("xnTa", "xnTa_pad", "wq_sb"), writes=("ps_s0",))
                    S.step("scalar", lambda e, h=h, arr=arr: e.copy(
                        arr[:, h * WA: h * WA + TC], ps_s[0][:, 0:TC]),
                        reads=("ps_s0",), writes=((nm + "T", h),))
                    S.step("tensor", lambda e, f=qkmmT: f(e, lo=TC, hi=WA),
                           reads=("xnTa", "xnTa_pad", "wq_sb"), writes=("ps_s0",))
                    S.step("scalar", lambda e, h=h, arr=arr: e.copy(
                        arr[:, h * WA + TC: h * WA + WA], ps_s[0][:, 0:WA - TC]),
                        reads=("ps_s0",), writes=((nm + "T", h),))
            qk_all = tuple(("qT", h) for h in range(NH)) + tuple(("kT", h) for h in range(NH))

            for br, varr, off in (("A", vA, 0), ("B", vB, 2)):
                for ti in range(4):
                    for vh in range(2):
                        def vmm(e, ti=ti, vh=vh, off=off):
                            for dc in range(DC):
                                i = e.matmul(
                                    ps_av[vh][:],
                                    xnTa[:, dc * WA + off + ti * 128:
                                          dc * WA + off + (ti + 1) * 128],
                                    wq_sb[:, dc * 3 * D + 2 * D + vh * 512:
                                          dc * 3 * D + 2 * D + (vh + 1) * 512],
                                    start=(dc == 0), stop=(dc == DC - 1))
                            return i
                        S.step("tensor", vmm, reads=("xnTa", "xnTa_pad", "wq_sb"),
                               writes=(f"ps_av{vh}",))
                        S.step("scalar", lambda e, ti=ti, vh=vh, varr=varr: e.copy(
                            varr[:, ti * D + vh * 512: ti * D + (vh + 1) * 512],
                            ps_av[vh][:]),
                            reads=(f"ps_av{vh}",), writes=((f"v{br}", ti),))

            for br, varr, off in (("A", vA, 0), ("B", vB, 2)):
                for ti in range(4):
                    if br == "B" and q == NCH - 1 and ti == 3:
                        nrows = 124
                    else:
                        nrows = 128
                    for hb in range(2):
                        def smm(e, hb=hb, ti=ti, off=off):
                            for hh in range(4):
                                h = hb * 4 + hh
                                i = e.matmul(
                                    ps_s[hb][:, hh * 128:(hh + 1) * 128],
                                    qT[:, h * WA + off + ti * 128:
                                       h * WA + off + (ti + 1) * 128],
                                    kT[:, h * WA + off + ti * 128:
                                       h * WA + off + (ti + 1) * 128],
                                    start=(hh == 0), stop=(hh == 3))
                            return i
                        S.step("tensor", smm, reads=qk_all, writes=(f"ps_s{hb}",))
                        S.step("vector", lambda e, hb=hb: e.reduce_max(
                            nmax[:, hb * 4:(hb + 1) * 4],
                            ps_s[hb][:].rearrange("p (hh t) -> p hh t", t=128),
                            axis=AX.X, negate=True),
                            reads=(f"ps_s{hb}",), writes=(("nmax", hb),))
                        S.step("vector", lambda e, hb=hb: e.tensor_scalar_mul(
                            nmax[:, hb * 4:(hb + 1) * 4],
                            nmax[:, hb * 4:(hb + 1) * 4], SCALE),
                            reads=(("nmax", hb),), writes=(("nmax", hb),))
                        for hh in range(4):
                            h = hb * 4 + hh
                            S.step("scalar", lambda e, hb=hb, hh=hh, h=h: e.activation(
                                Pf[:, h * 128:(h + 1) * 128],
                                ps_s[hb][:, hh * 128:(hh + 1) * 128], AF.Exp,
                                bias=nmax[:, h:h + 1], scale=SCALE),
                                reads=(f"ps_s{hb}", ("nmax", hb)), writes=(("Pf", h),))
                    for h in range(NH):
                        S.step("vector", lambda e, h=h: e.tensor_mul(
                            Pf[:, h * 128:(h + 1) * 128], Pf[:, h * 128:(h + 1) * 128],
                            bm_sb[:]),
                            reads=(("Pf", h), "bm_sb"), writes=(("Pf", h),))
                    S.step("vector", lambda e: e.reduce_sum(
                        dsum[:], Pf[:].rearrange("p (h t) -> p h t", t=128), axis=AX.X),
                        reads=tuple(("Pf", h) for h in range(NH)), writes=("dsum",))
                    S.step("vector", lambda e: e.reciprocal(rden[:], dsum[:]),
                           reads=("dsum",), writes=("rden",))
                    for h in range(NH):
                        S.step("vector", lambda e, h=h: e.tensor_scalar_mul(
                            Pb[:, h * 128:(h + 1) * 128], Pf[:, h * 128:(h + 1) * 128],
                            rden[:, h:h + 1]),
                            reads=(("Pf", h), "rden"), writes=(("Pb", h),))
                        S.step("tensor", lambda e, h=h: e.transpose(
                            ps_pt[:, h * 128:(h + 1) * 128],
                            Pb[:, h * 128:(h + 1) * 128], lnb["ident_bf"][:]),
                            reads=(("Pb", h), "ident_bf"), writes=("ps_pt",))
                        S.step("scalar", lambda e, h=h: e.copy(
                            PTb[:, h * 128:(h + 1) * 128],
                            ps_pt[:, h * 128:(h + 1) * 128]),
                            reads=("ps_pt",), writes=(("PTb", h),))
                    for hb in range(2):
                        def avmm(e, hb=hb, ti=ti, varr=varr):
                            for hh in range(4):
                                h = hb * 4 + hh
                                i = e.matmul(
                                    ps_av[hb][:, hh * 128:(hh + 1) * 128],
                                    varr[:, ti * D + h * 128: ti * D + (h + 1) * 128],
                                    PTb[:, h * 128:(h + 1) * 128],
                                    start=(hh == 0), stop=(hh == 3))
                            return i
                        S.step("tensor", avmm,
                               reads=tuple(("PTb", hb * 4 + hh) for hh in range(4))
                               + ((f"v{br}", ti),),
                               writes=(f"ps_av{hb}",))
                        S.step("scalar", lambda e, hb=hb: e.copy(
                            OT[:, hb * 512:(hb + 1) * 512], ps_av[hb][:]),
                            reads=(f"ps_av{hb}",), writes=(("OT", hb),))
                    for eh in range(2):
                        def ymm(e, eh=eh):
                            for h in range(NH):
                                e.matmul(
                                    ps_y[eh][:], OT[:, h * 128:(h + 1) * 128],
                                    wo_sb[:, h * D + eh * 512: h * D + (eh + 1) * 512],
                                    start=(h == 0), stop=False)
                            return e.matmul(ps_y[eh][:], ones_r[:],
                                            bout_sb[:, eh * 512:(eh + 1) * 512],
                                            start=False, stop=True)
                        S.step("tensor", ymm,
                               reads=(("OT", 0), ("OT", 1), "wo_sb", "ones_r",
                                      "bout_sb"),
                               writes=(f"ps_y{eh}",))
                    c = q * 4 + ti
                    if br == "A":
                        S.step("sync", lambda e, c=c, ti=ti: e.dma_start(
                            xres[ti % 2][:], x_in[c * 128:(c + 1) * 128, :]),
                            reads=(("x_dram", c),), writes=(("xres", ti % 2),),
                            dma=True)
                        for eh in range(2):
                            S.step("vector", lambda e, ti=ti, eh=eh: e.tensor_add(
                                xo[ti % 2][:, eh * 512:(eh + 1) * 512],
                                xres[ti % 2][:, eh * 512:(eh + 1) * 512],
                                ps_y[eh][:]),
                                reads=(f"ps_y{eh}", ("xres", ti % 2)),
                                writes=(("xo", ti % 2, eh),))
                        S.step("sync", lambda e, c=c, ti=ti: e.dma_start(
                            x_out[c * 128:(c + 1) * 128, :], xo[ti % 2][:]),
                            reads=(("xo", ti % 2, 0), ("xo", ti % 2, 1)),
                            writes=(("xout1", c),), dma=True, fam=f"xout{ti % 2}")
                    else:
                        for eh in range(2):
                            S.step("scalar", lambda e, eh=eh: e.copy(
                                ybst[:, eh * 512:(eh + 1) * 512], ps_y[eh][:]),
                                reads=(f"ps_y{eh}",), writes=("ybst",))
                        r0 = t0 + 2 + ti * 128
                        S.step("sync", lambda e, r0=r0, nrows=nrows: e.dma_start(
                            yB_hbm[r0:r0 + nrows, :], ybst[0:nrows, :]),
                            reads=("ybst",), writes=("yB",), dma=True)

        for c in range(NT):
            S.step("sync", lambda e, c=c: e.dma_start(
                xres[c % 2][:], x_out[c * 128:(c + 1) * 128, :]),
                reads=(("xout1", c),), writes=(("xres", c % 2),), dma=True)
            S.step("sync", lambda e, c=c: e.dma_start(
                yb3[c % 2][:], yB_hbm[c * 128:(c + 1) * 128, :]),
                reads=("yB",), writes=(("yb3", c % 2),), dma=True)
            S.step("vector", lambda e, c=c: e.tensor_add(
                xo[c % 2][:], xres[c % 2][:], yb3[c % 2][:]),
                reads=(("xres", c % 2), ("yb3", c % 2)), writes=(("xo", c % 2, 0),))
            S.step("sync", lambda e, c=c: e.dma_start(
                x_out[c * 128:(c + 1) * 128, :], xo[c % 2][:]),
                reads=(("xo", c % 2, 0),), writes=(("xout2", c),), dma=True, fam=f"xoutP3{c % 2}")



def _l0a_inputs(x_shard, inputs, core_id):
    cons = host_consts(core_id)
    return {
        "x_in": x_shard,
        "wqkv": np.ascontiguousarray(inputs["L0_attn_w_qkv"], np.float32),
        "wout": np.ascontiguousarray(inputs["L0_attn_w_out"], np.float32),
        "bout": np.ascontiguousarray(
            np.asarray(inputs["L0_attn_b_out"], np.float32)).reshape(1, D),
        "c_ident": cons["c_ident"],
        "c_bmask": cons["c_bmask"],
    }



def _b1T(b1):
    Hh = b1.shape[1]
    return np.ascontiguousarray(
        b1.reshape(E, Hh // 128, 128).transpose(2, 1, 0).reshape(128, (Hh // 128) * E))



CAP = 1536
BUD = (4096, 4096, 2816, 2048, 1664, 1408, 1280, 1152)
OFF = tuple(int(np.cumsum((0,) + BUD)[i]) for i in range(8))
GROWS = int(np.sum(BUD))


def _set_budget(bud):
    global BUD, OFF, GROWS
    BUD = tuple(int(v) for v in bud)
    OFF = tuple(int(np.cumsum((0,) + BUD)[i]) for i in range(8))
    GROWS = int(np.sum(BUD))


def _off_of(bud):
    return tuple(int(np.cumsum((0,) + tuple(bud))[i]) for i in range(8))


def _profile(prefs):
    cs = np.stack([np.sort(np.asarray(p)[:, NT])[::-1] for p in prefs])
    mx = cs.max(0)
    return tuple(int(v) for v in np.ceil(mx / 128.0).astype(np.int64) * 128)
GW = 1032


def record_slot_scatter(S, dr, B, layer, mw_col_of, rank_src, gbuf,
                        pref_rows_hbm):
    nc = S.nc
    sbf = B.buf
    sel_a = rank_src["sel_a"]
    ltri_sb = rank_src["ltri_sb"]
    ones_r = rank_src["ones_r"]
    ps_rk = rank_src["ps_rk"]
    xn_ap_of = rank_src["xn_ap_of"]
    ecap8 = sbf("ecap8", [128, 8], F32)
    S.step("sync", lambda e: e.dma_start(ecap8[:], dr["c_offe"][:, :]),
           writes=("ecap8",), dma=True)
    tokhl = sbf("tokhl", [128, 2 * NT], BF16)
    S.step("gpsimd", lambda e: e.dma_start(tokhl[:], dr["c_tokhl"][:, :]),
           writes=("tokhl",), dma=True)
    rkst = [sbf(f"rkst{i}", [1, 8], F32) for i in range(2)]
    rank_sb = sbf("rank_sb", [128, 8], F32)
    cum = sbf("cum", [128, 8], F32)
    selA = sbf("selA", [128, 8], F32)
    selB = sbf("selB", [128, 8], F32)
    scr8 = sbf("scr8", [128, 8], F32)
    ercol = sbf("ercol", [128, 6], F32)
    slotf = sbf("slotf", [128, 2], F32)
    sloti = sbf("sloti", [128, 2], I32)
    onescol2 = sbf("onescol2", [128, 1], F32)
    S.step("vector", lambda e: e.memset(onescol2[:], 1.0), writes=("onescol2",))
    rowb = [sbf(f"rowb{i}", [128, GW], BF16) for i in range(2)]
    zrow = sbf("grow0", [128, GW], BF16)
    S.step("vector", lambda e: e.memset(zrow[:], 0.0), writes=("growz",))
    for i in range(GROWS // 128):
        S.step("sync", lambda e, i=i: e.dma_start(
            gbuf[i * 128:(i + 1) * 128, :], zrow[:]),
            reads=("growz",), writes=(("gbuf", i % 2),), dma=True,
            fam="gz", bulk=True)

    for c in range(NT):
        xn_src, tk_xn, xn_hbm = xn_ap_of(c)
        mw_ap, tk_mw = mw_col_of(c)
        rb = rowb[c % 2]
        tkrb = ("rowb", c % 2)
        S.step("gpsimd", lambda e, c=c: e.dma_start(
            rkst[c % 2][:], pref_rows_hbm[c:c + 1, :]),
            reads=("prefh",), writes=(("rkst", c % 2),), dma=True)

        def rkmm(e, c=c):
            e.matmul(ps_rk[:, 0:8], ltri_sb[:], sel_a[:, c * E:(c + 1) * E],
                     start=True, stop=False)
            return e.matmul(ps_rk[:, 0:8], ones_r[:], rkst[c % 2][:],
                            start=False, stop=True)
        S.step("tensor", rkmm, reads=(("sel", c), ("rkst", c % 2),
                                      "ltri_sb", "ones_r"), writes=("ps_rk",))
        S.step("scalar", lambda e: e.copy(rank_sb[:], ps_rk[:, 0:8]),
               reads=("ps_rk",), writes=("rank_sb",))
        S.step("vector", lambda e, c=c: e.tensor_tensor_scan(
            cum[:], onescol2[:, 0:1].to_broadcast([128, 8]),
            sel_a[:, c * E:(c + 1) * E], 0.0, ALU.mult, ALU.add),
            reads=(("sel", c), "onescol2"), writes=("cum",))
        S.step("vector", lambda e: e.tensor_scalar(scr8[:], cum[:], -1.0, 2.0,
                                                   ALU.mult, ALU.add),
               reads=("cum",), writes=("scr8",))
        S.step("vector", lambda e: e.tensor_mul(scr8[:], scr8[:], cum[:]),
               reads=("scr8", "cum"), writes=("scr8",))
        S.step("vector", lambda e, c=c: e.tensor_mul(
            selA[:], scr8[:], sel_a[:, c * E:(c + 1) * E]),
            reads=("scr8", ("sel", c)), writes=("selA",))
        S.step("vector", lambda e, c=c: e.tensor_sub(
            selB[:], sel_a[:, c * E:(c + 1) * E], selA[:]),
            reads=(("sel", c), "selA"), writes=("selB",))
        for k, selk, kt in ((0, selA, "selA"), (1, selB, "selB")):
            S.step("vector", lambda e, selk=selk: e.tensor_mul(
                scr8[:], ecap8[:], selk[:]),
                reads=("ecap8", kt), writes=("scr8",))
            S.step("vector", lambda e, k=k: e.reduce_sum(
                ercol[:, 3 * k:3 * k + 1], scr8[:], axis=AX.X),
                reads=("scr8",), writes=(("er", k),))
            S.step("vector", lambda e, selk=selk: e.tensor_mul(
                scr8[:], rank_sb[:], selk[:]),
                reads=("rank_sb", kt), writes=("scr8",))
            S.step("vector", lambda e, k=k: e.reduce_sum(
                ercol[:, 3 * k + 1:3 * k + 2], scr8[:], axis=AX.X),
                reads=("scr8",), writes=(("er", k),))
            S.step("vector", lambda e, selk=selk, mw_ap=mw_ap: e.tensor_mul(
                scr8[:], mw_ap, selk[:]),
                reads=(tk_mw, kt), writes=("scr8",))
            S.step("vector", lambda e, k=k: e.reduce_sum(
                ercol[:, 3 * k + 2:3 * k + 3], scr8[:], axis=AX.X),
                reads=("scr8",), writes=(("er", k),))
            S.step("vector", lambda e, k=k: e.tensor_add(
                slotf[:, k:k + 1], ercol[:, 3 * k:3 * k + 1],
                ercol[:, 3 * k + 1:3 * k + 2]),
                reads=(("er", k),), writes=("slotf",))
        S.step("vector", lambda e: e.tensor_copy(sloti[:], slotf[:]),
               reads=("slotf",), writes=("sloti",))
        if xn_hbm:
            S.step("sync", lambda e, rb=rb, xn_src=xn_src, c=c: e.dma_start(
                rb[:, 0:D], xn_src[c * 128:(c + 1) * 128, :]),
                reads=(tk_xn,), writes=(tkrb,), dma=True, fam=f"rowbl{c % 2}")
        else:
            S.step("scalar", lambda e, rb=rb, xn_src=xn_src: e.copy(
                rb[:, 0:D], xn_src),
                reads=(tk_xn,), writes=(tkrb,))
        S.step("scalar", lambda e, rb=rb, c=c: e.copy(
            rb[:, D + 1:D + 3], tokhl[:, 2 * c:2 * c + 2]),
            reads=("tokhl",), writes=(tkrb,))
        for k in range(2):
            S.step("scalar", lambda e, rb=rb, k=k: e.copy(
                rb[:, D:D + 1], ercol[:, 3 * k + 2:3 * k + 3]),
                reads=(("er", k),), writes=(tkrb,))
            S.step("gpsimd", lambda e, rb=rb, k=k: e.indirect_dma_start(
                gbuf[:, :], IndirectOffsetOnAxis(sloti[:, k:k + 1], 0),
                rb[:], None),
                reads=(tkrb, "sloti"), writes=(("gbuf", c % 2),), dma=True,
                fam=f"gsc{c % 2}")


def record_ffn_sparse_l1main(S, dr, B):
    nc = S.nc
    sbf = B.buf
    psf = B.psum
    maskw = sbf("maskw", [128, NT * E], F32)
    sel_a = sbf("sel_a", [128, NT * E], F32)
    ones_r = sbf("ones_r", [1, 128], F32)
    ltri_sb = sbf("ltri_sb", [128, 128], F32)
    ps_rk = psf("ps_rk", [128, 512], F32)
    S.step("sync", lambda e: e.dma_start(maskw[:], dr["mw_in"][:, :]),
           writes=tuple(("maskw", c) for c in range(NT)), dma=True)
    S.step("sync", lambda e: e.dma_start(sel_a[:], dr["sel_in"][:, :]),
           writes=tuple(("sel", c) for c in range(NT)), dma=True)
    S.step("sync", lambda e: e.dma_start(ltri_sb[:], dr["c_ltri"][:, :]),
           writes=("ltri_sb",), dma=True)
    S.step("vector", lambda e: e.memset(ones_r[:], 1.0), writes=("ones_r",))
    rank_src = {
        "sel_a": sel_a, "ltri_sb": ltri_sb, "ones_r": ones_r, "ps_rk": ps_rk,
        "xn_ap_of": lambda c: (dr["xntok"], ("xntok_ext",), True),
    }
    record_slot_scatter(S, dr, B, 1,
                        lambda c: (maskw[:, c * E:(c + 1) * E], ("maskw", c)),
                        rank_src, dr["gbuf"], dr["prefT"])
    record_ffn_sparse_main(S, dr, B, 1)


def record_ffn_sparse_main(S, dr, B, layer):
    H = 2048 if layer == 0 else 4096
    NFB = H // 2048
    FCB = 16
    NFI = H // 128
    ACTF = AF.Gelu if layer == 0 else AF.Relu
    nc = S.nc
    sbf = B.buf
    psf = B.psum
    gbuf = dr["gbuf"]
    x_res = dr["x_res"]
    w1 = dr["w1"]; w2 = dr["w2"]; b1Te = dr["b1Te"]; b2 = dr["b2"]
    c_ident = dr["c_ident"]

    ident_bf = sbf("ident_bf2", [128, 128], BF16)
    S.step("gpsimd", lambda e: e.dma_start(ident_bf[:], c_ident[:, :]),
           writes=("ident_bf",), dma=True)
    b1st = sbf("b1st", [128, NFI], F32)
    b2row = sbf("b2row", [1, D], BF16)
    ones_r2 = sbf("ones_r2", [1, 128], BF16)
    S.step("vector", lambda e: e.memset(ones_r2[:], 1.0), writes=("ones_r2",))
    xgT = sbf("xgT", [128, DC * 512], BF16)
    h_sg = sbf("h_sg", [128, FCB * 512], BF16)
    w1_sb = sbf("w1_sb", [128, DC * FCB * 128], BF16)
    w2_sb = sbf("w2_sb", [128, FCB * D], BF16)
    grow = [sbf(f"grow{i}", [128, GW], BF16) for i in range(2)]
    mwc = sbf("mwc", [128, 32], F32)
    tokf = sbf("tokf", [128, 32], F32)
    toki = sbf("toki", [128, 32], I32)
    ysb = [sbf(f"ysb{i}", [128, D], F32) for i in range(2)]
    ps_tr = psf("ps_tr", [128, 1024], BF16)
    ps_h = [psf(f"ps_h{i}", [128, 512], F32) for i in range(2)]
    ps_y = [psf(f"ps_y{i}", [128, 512], F32) for i in range(2)]

    for v in range(E):
        nsub = BUD[v] // 128
        base = OFF[v]
        if nsub == 0:
            continue
        S.step("sync", lambda e, v=v: e.dma_start(
            b1st[:], b1Te[:, v * NFI:(v + 1) * NFI]),
            writes=("b1st",), dma=True, fam="b1st")
        S.step("gpsimd", lambda e, v=v: e.dma_start(b2row[:], b2[v:v + 1, :]),
               writes=("b2row",), dma=True, fam="b2row")
        for st in range(nsub):
            gsl = grow[st % 2]
            S.step("sync", lambda e, st=st, gsl=gsl, base=base: e.dma_start(
                gsl[:], gbuf[base + st * 128: base + (st + 1) * 128, :]),
                reads=(("gbuf", 0), ("gbuf", 1)), writes=(("grow", st % 2),),
                dma=True, fam=f"grow{st % 2}")
            S.step("vector", lambda e, st=st, gsl=gsl: e.tensor_copy(
                mwc[:, st:st + 1], gsl[:, D:D + 1]),
                reads=(("grow", st % 2),), writes=("mwc",))
            S.step("vector", lambda e, st=st, gsl=gsl: e.tensor_copy(
                tokf[:, st:st + 1], gsl[:, D + 1:D + 2]),
                reads=(("grow", st % 2),), writes=(("tokf", st),))
            S.step("vector", lambda e, st=st, gsl=gsl: e.scalar_tensor_tensor(
                tokf[:, st:st + 1], tokf[:, st:st + 1], 64.0,
                gsl[:, D + 2:D + 3], ALU.mult, ALU.add),
                reads=(("tokf", st), ("grow", st % 2)), writes=(("tokf", st),))
        S.step("vector", lambda e, nsub=nsub: e.tensor_copy(
            toki[:, 0:nsub], tokf[:, 0:nsub]),
            reads=tuple(("tokf", st) for st in range(nsub)), writes=("toki",))

        for fb in range(NFB):
            src1 = w1[v, :, fb * 2048:(fb + 1) * 2048].rearrange(
                "(dc p) (fc k) -> p dc fc k", p=128, k=128)
            S.step("gpsimd", lambda e, src1=src1: e.dma_start(
                w1_sb[:].rearrange("p (dc fc k) -> p dc fc k", fc=FCB, k=128),
                src1),
                writes=("w1_sb",), dma=True)
            src2 = w2[v, fb * 2048:(fb + 1) * 2048, :].rearrange(
                "(fc p) d -> p fc d", p=128)
            S.step("gpsimd", lambda e, src2=src2: e.dma_start(
                w2_sb[:].rearrange("p (fc d) -> p fc d", d=D), src2),
                writes=("w2_sb",), dma=True)
            nsg = (nsub + 3) // 4
            for sg in range(nsg):
                sgw = min(512, (nsub - sg * 4) * 128)
                for su in range(sgw // 128):
                    st = sg * 4 + su
                    gsl = grow[st % 2]
                    if fb == 0 and False:
                        pass
                    def trx(e, su=su, st=st, gsl=gsl):
                        for g in range(DC):
                            i = e.transpose(ps_tr[:, g * 128:(g + 1) * 128],
                                            gsl[:, g * 128:(g + 1) * 128], ident_bf[:])
                        return i
                    S.step("sync", lambda e, st=st, gsl=gsl, base=base: e.dma_start(
                        gsl[:], gbuf[base + st * 128: base + (st + 1) * 128, :]),
                        reads=(("gbuf", 0), ("gbuf", 1)),
                        writes=(("grow", st % 2),), dma=True, fam=f"grow{st % 2}")
                    S.step("tensor", trx, reads=(("grow", st % 2), "ident_bf"),
                           writes=("ps_tr",))
                    def trev(e, su=su):
                        for g in range(DC):
                            i = e.copy(xgT[:, g * 512 + su * 128: g * 512 + (su + 1) * 128],
                                       ps_tr[:, g * 128:(g + 1) * 128])
                        return i
                    S.step("scalar", trev, reads=("ps_tr",), writes=(("xgT", su),))
                xg_all = tuple(("xgT", su) for su in range(sgw // 128))
                for fc in range(FCB):
                    def hmm(e, fc=fc, sgw=sgw):
                        for dc in range(DC):
                            i = e.matmul(
                                ps_h[fc % 2][:, 0:sgw],
                                w1_sb[:, (dc * FCB + fc) * 128:(dc * FCB + fc + 1) * 128],
                                xgT[:, dc * 512: dc * 512 + sgw],
                                start=(dc == 0), stop=(dc == DC - 1))
                        return i
                    S.step("tensor", hmm, reads=xg_all + ("w1_sb",),
                           writes=(f"ps_h{fc % 2}",))
                    fi = fb * FCB + fc
                    S.step("scalar", lambda e, fc=fc, fi=fi, sgw=sgw: e.activation(
                        h_sg[:, fc * 512: fc * 512 + sgw], ps_h[fc % 2][:, 0:sgw],
                        ACTF, bias=b1st[:, fi:fi + 1]),
                        reads=(f"ps_h{fc % 2}", "b1st"), writes=(("h", fc),))
                h_all = tuple(("h", fc) for fc in range(FCB))
                for su in range(sgw // 128):
                    st = sg * 4 + su
                    for dt in range(2):
                        def ymm(e, su=su, dt=dt, fb=fb):
                            for fc in range(FCB):
                                last = (fc == FCB - 1) and (fb < NFB - 1)
                                i = e.matmul(
                                    ps_y[dt][:],
                                    h_sg[:, fc * 512 + su * 128: fc * 512 + su * 128 + 128],
                                    w2_sb[:, fc * D + dt * 512: fc * D + (dt + 1) * 512],
                                    start=(fc == 0), stop=last)
                            if fb == NFB - 1:
                                i = e.matmul(
                                    ps_y[dt][:], ones_r2[:],
                                    b2row[0:1, dt * 512:(dt + 1) * 512],
                                    start=False, stop=True)
                            return i
                        S.step("tensor", ymm,
                               reads=h_all + ("w2_sb", "ones_r2", "b2row"),
                               writes=(f"ps_y{dt}",))
                        S.step("vector", lambda e, st=st, dt=dt: e.tensor_scalar_mul(
                            ysb[st % 2][:, dt * 512:(dt + 1) * 512], ps_y[dt][:],
                            mwc[:, st:st + 1]),
                            reads=(f"ps_y{dt}", "mwc"), writes=(("ysb", st % 2, dt),))
                    S.step("gpsimd", lambda e, st=st: e.indirect_dma_start(
                        x_res[:, :], IndirectOffsetOnAxis(toki[:, st:st + 1], 0),
                        ysb[st % 2][:], None, compute_op=ALU.add),
                        reads=(("ysb", st % 2, 0), ("ysb", st % 2, 1), "toki", "xsc"),
                        writes=("xsc",), dma=True, fam=f"gadd{st % 2}")

def _mk_composite(build_fn, serialize=False):
    nc = bass.Bass("TRN2", target_bir_lowering=False, debug=False)
    with ExitStack() as ctx:
        S = Sched(nc, ctx, serialize=serialize)
        B = BufAlloc(nc, ctx)
        cache = {}

        def decl(name, shape, dt, kind=None):
            if name not in cache:
                if kind:
                    cache[name] = nc.dram_tensor(name, shape, dt, kind=kind)
                else:
                    cache[name] = nc.dram_tensor(name, shape, dt)
            return cache[name]

        build_fn(S, B, decl)
        S.build_block(ctx)
    return nc


def _ffn_dr(decl, layer, split, x_src, x_dst):
    H = 2048 if layer == 0 else 4096
    dr = {"c_ident": decl("c_ident", [128, 128], F32, "ExternalInput"),
          "x_in": x_src, "x_out": x_dst}
    if split in ("all", "count", "cnt1"):
        dr["wg"] = decl(f"wg_{layer}", [D, E], F32, "ExternalInput")
        if layer == 0:
            dr["rb"] = decl("rb_0", [1, E], F32, "ExternalInput")
    if split in ("all", "main"):
        dr["w1"] = decl(f"w1_{layer}", [E, D, H], F32, "ExternalInput")
        dr["b1T"] = decl(f"b1T_{layer}", [128, (H // 128) * E], F32, "ExternalInput")
        dr["w2"] = decl(f"w2_{layer}", [E, H, D], F32, "ExternalInput")
        dr["b2"] = decl(f"b2_{layer}", [E, D], F32, "ExternalInput")
    if split == "count":
        dr["xnT"] = decl("xnT_out", [D, T], BF16, "ExternalOutput")
        dr["mw_out"] = decl("mw_out", [128, NT * E], F32, "ExternalOutput")
        dr["sel_out"] = decl("sel_out", [128, NT * E], F32, "ExternalOutput")
        dr["pref_out"] = decl("pref_out", [8, NT + 1], F32, "ExternalOutput")
    elif split == "main":
        dr["xnT"] = decl("xnT_in", [D, T], BF16, "ExternalInput")
        dr["mw_in"] = decl("mw_in", [128, NT * E], F32, "ExternalInput")
        dr["sel_in"] = decl("sel_in", [128, NT * E], F32, "ExternalInput")
        dr["pref_in"] = decl("pref_in", [8, NT + 1], F32, "ExternalInput")
        dr["agcnt"] = decl("agcnt", [8, E], F32, "ExternalInput")
        dr["c_ltri"] = decl("c_ltri", [128, 128], F32, "ExternalInput")
        dr["c_corelt"] = decl("c_corelt", [8, 8], F32, "ExternalInput")
        dr["nthr"] = decl("scr_nthr", [NT + 1, E], F32)
    else:
        dr["xnT"] = decl("scr_xnT_ffn", [D, T], BF16)
    return dr


def _l0a_dr(decl, x_src, x_dst):
    return {
        "c_ident": decl("c_ident", [128, 128], F32, "ExternalInput"),
        "c_bmask": decl("c_bmask", [128, 128], F32, "ExternalInput"),
        "x_in": x_src, "x_out": x_dst,
        "wqkv": decl("wqkv", [D, 3 * D], F32, "ExternalInput"),
        "wout": decl("wout", [D, D], F32, "ExternalInput"),
        "bout": decl("bout", [1, D], F32, "ExternalInput"),
        "xnT": decl("scr_xnT_l0a", [D, T], BF16),
        "yB": decl("scr_yB", [T, D], F32),
    }


def _l1a_dr(decl, x_src, x_dst):
    dr = {"c_ident": decl("c_ident", [128, 128], F32, "ExternalInput"),
          "x_in": x_src, "x_out": x_dst}
    for nm in ("wkT", "wvT", "wrT", "woT"):
        dr[nm] = decl(nm, [D, D], F32, "ExternalInput")
    for nm in ("tmk5", "tmv5", "tmr5", "dec5", "fst5"):
        dr[nm] = decl(nm, [128, DC], F32, "ExternalInput")
    return dr


def build_pa(serialize=False):
    def b(S, B, decl):
        x_in = decl("x_in", [T, D], F32, "ExternalInput")
        B.kind = "ffn"
        S.prefix = "a0"
        record_ffn(S, _ffn_dr(decl, 0, "count", x_in, None), B, 0, split="count")
    return _mk_composite(b, serialize)


def build_pb(serialize=False):
    def b(S, B, decl):
        x_in = decl("x_in", [T, D], F32, "ExternalInput")
        x_out = decl("x_out", [T, D], F32, "ExternalOutput")
        x_scr = decl("x_scr", [T, D], F32)
        B.kind = "ffn"
        S.prefix = "b0"
        record_ffn(S, _ffn_dr(decl, 0, "main", x_in, x_scr), B, 0, split="main")
        S.barrier()
        B.kind = "l0a"
        S.prefix = "b1"
        record_l0a(S, _l0a_dr(decl, x_scr, x_out), B)
        S.barrier()
        B.kind = "ffn"
        S.prefix = "b2"
        record_ffn(S, _ffn_dr(decl, 0, "count", x_out, None), B, 0, split="count")
    return _mk_composite(b, serialize)


def build_pc(serialize=False):
    def b(S, B, decl):
        x_in = decl("x_in", [T, D], F32, "ExternalInput")
        x_out = decl("x_out", [T, D], F32, "ExternalOutput")
        x_scr = decl("x_scr", [T, D], F32)
        B.kind = "ffn"
        S.prefix = "c0"
        record_ffn(S, _ffn_dr(decl, 0, "main", x_in, x_scr), B, 0, split="main")
        S.barrier()
        S.prefix = "c1"
        record_ffn(S, _ffn_dr(decl, 1, "all", x_scr, x_scr), B, 1, split="all")
        S.barrier()
        B.kind = "l1a"
        S.prefix = "c2"
        record_l1a(S, _l1a_dr(decl, x_scr, x_scr), B)
        S.barrier()
        B.kind = "ffn"
        S.prefix = "c3"
        record_ffn(S, _ffn_dr(decl, 1, "all", x_scr, x_out), B, 1, split="all")
    return _mk_composite(b, serialize)



def host_consts_sparse():
    tok = np.arange(T, dtype=np.int64)
    hi = (tok // 64).astype(np.float32)
    lo = (tok % 64).astype(np.float32)
    tokhl = np.zeros((128, 2 * NT), np.float32)
    for c in range(NT):
        tokhl[:, 2 * c] = hi[c * 128:(c + 1) * 128]
        tokhl[:, 2 * c + 1] = lo[c * 128:(c + 1) * 128]
    return {"c_tokhl": tokhl.astype(BFNP)}


def _perm_inputs(pref, offl):
    counts = np.asarray(pref)[:, NT]
    perm = np.argsort(-counts, kind="stable")
    inv = np.empty(8, np.int64)
    inv[perm] = np.arange(8)
    offe = np.broadcast_to(
        np.asarray([offl[inv[e]] for e in range(8)], np.float32), (128, 8)).copy()
    return {"c_offe": offe,
            "prefT_in": np.ascontiguousarray(np.asarray(pref).T)}, perm


def _sparse_main_dr(decl, layer, x_res):
    H = 2048 if layer == 0 else 4096
    return {
        "c_offe": decl("c_offe", [128, 8], F32, "ExternalInput"),
        "c_tokhl": decl("c_tokhl", [128, 2 * NT], BF16, "ExternalInput"),
        "b1Te": decl(f"b1Te_{layer}", [128, E * (H // 128)], F32, "ExternalInput"),
        "gbuf": decl("gbuf_scr", [GROWS, GW], BF16, "ExternalOutput"),
        "x_res": x_res,
        "xntok": decl("xntok_in", [T, D], BF16, "ExternalInput"),
        "prefT": decl("prefT_in", [NT + 1, E], F32, "ExternalInput"),
        "mw_in": decl("mw_in", [128, NT * E], F32, "ExternalInput"),
        "sel_in": decl("sel_in", [128, NT * E], F32, "ExternalInput"),
        "c_ltri": decl("c_ltri", [128, 128], F32, "ExternalInput"),
        "c_ident": decl("c_ident", [128, 128], F32, "ExternalInput"),
        "w1": decl(f"w1_{layer}", [E, D, H], F32, "ExternalInput"),
        "w2": decl(f"w2_{layer}", [E, H, D], F32, "ExternalInput"),
        "b2": decl(f"b2_{layer}", [E, D], F32, "ExternalInput"),
    }


def _xcopy(S, src_t, dst_t):
    for i in range(4):
        S.step("sync", lambda e, i=i: e.dma_start(
            dst_t[i * 1024:(i + 1) * 1024, :], src_t[i * 1024:(i + 1) * 1024, :]),
            reads=(), writes=("xsc", (f"xcp{i}",)), dma=True, fam="xcopy")


def build_sa(serialize=False):
    def b(S, B, decl):
        x_in = decl("x_in", [T, D], F32, "ExternalInput")
        B.kind = "ffn"
        S.prefix = "a0"
        dr = _ffn_dr(decl, 0, "count", x_in, None)
        dr["xntok"] = decl("xntok_out", [T, D], BF16, "ExternalOutput")
        record_ffn(S, dr, B, 0, split="count")
    return _mk_composite(b, serialize)


def build_sb(serialize=False):
    def b(S, B, decl):
        x_in = decl("x_in", [T, D], F32, "ExternalInput")
        x_out = decl("x_out", [T, D], F32, "ExternalOutput")
        x_scr = decl("x_scr", [T, D], F32)
        B.kind = "ffn"
        S.prefix = "b0"
        _xcopy(S, x_in, x_scr)
        dr = _ffn_dr(decl, 0, "main", x_in, None)
        dr.update(_sparse_main_dr(decl, 0, x_scr))
        record_ffn(S, dr, B, 0, split="main", sparse=True)
        S.barrier()
        B.kind = "l0a"
        S.prefix = "b1"
        record_l0a(S, _l0a_dr(decl, x_scr, x_out), B)
        S.barrier()
        B.kind = "ffn"
        S.prefix = "b2"
        dr2 = _ffn_dr(decl, 0, "count", x_out, None)
        dr2["xntok"] = decl("xntok_out", [T, D], BF16, "ExternalOutput")
        record_ffn(S, dr2, B, 0, split="count")
    return _mk_composite(b, serialize)


def build_sc(serialize=False):
    def b(S, B, decl):
        x_in = decl("x_in", [T, D], F32, "ExternalInput")
        x_out = decl("x_out", [T, D], F32, "ExternalOutput")
        B.kind = "ffn"
        S.prefix = "c0"
        _xcopy(S, x_in, x_out)
        dr = _ffn_dr(decl, 0, "main", x_in, None)
        dr.update(_sparse_main_dr(decl, 0, x_out))
        record_ffn(S, dr, B, 0, split="main", sparse=True)
        S.barrier()
        S.prefix = "c1"
        dr2 = _ffn_dr(decl, 1, "count", x_out, None)
        dr2["xntok"] = decl("xntok_out", [T, D], BF16, "ExternalOutput")
        record_ffn(S, dr2, B, 1, split="count")
    return _mk_composite(b, serialize)


def build_sd(serialize=False):
    def b(S, B, decl):
        x_in = decl("x_in", [T, D], F32, "ExternalInput")
        x_out = decl("x_out", [T, D], F32, "ExternalOutput")
        x_scr = decl("x_scr", [T, D], F32)
        B.kind = "ffn"
        S.prefix = "d0"
        _xcopy(S, x_in, x_scr)
        dr = _sparse_main_dr(decl, 1, x_scr)
        record_ffn_sparse_l1main(S, dr, B)
        S.barrier()
        B.kind = "l1a"
        S.prefix = "d1"
        record_l1a(S, _l1a_dr(decl, x_scr, x_out), B)
        S.barrier()
        B.kind = "ffn"
        S.prefix = "d2"
        dr2 = _ffn_dr(decl, 1, "count", x_out, None)
        dr2["xntok"] = decl("xntok_out", [T, D], BF16, "ExternalOutput")
        record_ffn(S, dr2, B, 1, split="count")
    return _mk_composite(b, serialize)


def build_se(serialize=False):
    def b(S, B, decl):
        x_in = decl("x_in", [T, D], F32, "ExternalInput")
        x_out = decl("x_out", [T, D], F32, "ExternalOutput")
        B.kind = "ffn"
        S.prefix = "e0"
        _xcopy(S, x_in, x_out)
        dr = _sparse_main_dr(decl, 1, x_out)
        record_ffn_sparse_l1main(S, dr, B)
    return _mk_composite(b, serialize)



def _ffn_w(inputs, layer):
    if layer == 0:
        return {
            "w1_0": inputs["L0_ffn_w1"],
            "b1T_0": _b1T(np.asarray(inputs["L0_ffn_b1"], np.float32)),
            "w2_0": inputs["L0_ffn_w2"],
            "b2_0": np.ascontiguousarray(np.asarray(inputs["L0_ffn_b2"], np.float32)),
        }
    return {
        "wg_1": np.ascontiguousarray(np.asarray(inputs["L1_ffn_W_gate"], np.float32)),
        "w1_1": inputs["L1_ffn_W1"],
        "b1T_1": _b1T(np.asarray(inputs["L1_ffn_b1"], np.float32)),
        "w2_1": inputs["L1_ffn_W2"],
        "b2_1": np.ascontiguousarray(np.asarray(inputs["L1_ffn_b2"], np.float32)),
    }


def _cnt_w(inputs):
    return {
        "wg_0": np.ascontiguousarray(np.asarray(inputs["L0_ffn_router_w"], np.float32)),
        "rb_0": np.ascontiguousarray(
            np.asarray(inputs["L0_ffn_router_b"], np.float32)).reshape(1, E),
    }


def _l0a_w(inputs, core_id):
    cons = host_consts(core_id)
    return {
        "wqkv": np.ascontiguousarray(np.asarray(inputs["L0_attn_w_qkv"], np.float32)),
        "wout": np.ascontiguousarray(np.asarray(inputs["L0_attn_w_out"], np.float32)),
        "bout": np.ascontiguousarray(
            np.asarray(inputs["L0_attn_b_out"], np.float32)).reshape(1, D),
        "c_bmask": cons["c_bmask"],
    }


def _l1a_w(inputs):
    def five(v):
        return np.ascontiguousarray(np.asarray(v, np.float32).reshape(DC, 128).T)
    return {
        "wkT": np.ascontiguousarray(np.asarray(inputs["L1_attn_W_k"], np.float32).T),
        "wvT": np.ascontiguousarray(np.asarray(inputs["L1_attn_W_v"], np.float32).T),
        "wrT": np.ascontiguousarray(np.asarray(inputs["L1_attn_W_r"], np.float32).T),
        "woT": np.ascontiguousarray(np.asarray(inputs["L1_attn_W_o"], np.float32).T),
        "tmk5": five(inputs["L1_attn_time_mix_k"]),
        "tmv5": five(inputs["L1_attn_time_mix_v"]),
        "tmr5": five(inputs["L1_attn_time_mix_r"]),
        "dec5": five(inputs["L1_attn_time_decay"]),
        "fst5": five(inputs["L1_attn_time_first"]),
    }


def _count_glue(cres, b):
    return {
        "xnT_in": cres[b]["xnT_out"],
        "mw_in": cres[b]["mw_out"],
        "sel_in": cres[b]["sel_out"],
        "pref_in": cres[b]["pref_out"],
    }


_PROGS = {}


def _get(name, bud=None):
    builders = {"pa": build_pa, "pb": build_pb, "pc": build_pc,
                "sa": build_sa, "sb": build_sb, "sc": build_sc,
                "sd": build_sd, "se": build_se}
    key = (name, tuple(bud) if bud is not None else None)
    if key not in _PROGS:
        if bud is not None:
            _set_budget(bud)
        _PROGS[key] = builders[name]()
    return _PROGS[key]


def _kernel_dense(inputs, shards):
    cores = list(range(NCORES))
    maps = [{"x_in": shards[b], "c_ident": host_consts(b)["c_ident"],
             **_cnt_w(inputs)} for b in cores]
    cres = run_bass_kernel_spmd(_get("pa"), maps, core_ids=cores).results
    ag1 = np.ascontiguousarray(
        np.stack([np.asarray(cres[b]["pref_out"])[:, NT] for b in cores]))
    maps = []
    for b in cores:
        cons = host_consts(b)
        maps.append({"x_in": shards[b], "c_ident": cons["c_ident"],
                     "c_ltri": cons["c_ltri"], "c_corelt": cons["c_corelt"],
                     "agcnt": ag1, **_count_glue(cres, b), **_ffn_w(inputs, 0),
                     **_cnt_w(inputs), **_l0a_w(inputs, b)})
    bres = run_bass_kernel_spmd(_get("pb"), maps, core_ids=cores).results
    ag2 = np.ascontiguousarray(
        np.stack([np.asarray(bres[b]["pref_out"])[:, NT] for b in cores]))
    maps = []
    for b in cores:
        cons = host_consts(b)
        maps.append({"x_in": np.ascontiguousarray(np.asarray(bres[b]["x_out"], np.float32)),
                     "c_ident": cons["c_ident"], "c_ltri": cons["c_ltri"],
                     "c_corelt": cons["c_corelt"], "agcnt": ag2,
                     **_count_glue(bres, b), **_ffn_w(inputs, 0),
                     **_ffn_w(inputs, 1), **_l1a_w(inputs)})
    fres = run_bass_kernel_spmd(_get("pc"), maps, core_ids=cores).results
    return np.stack([np.asarray(fres[b]["x_out"], np.float32)
                     for b in cores]).astype(np.float32)


def _b1Te(b1, H):
    b1 = np.asarray(b1, np.float32).reshape(E, H // 128, 128)
    return np.ascontiguousarray(b1.transpose(2, 0, 1).reshape(128, -1))


def _sp_w(inputs, layer, perm):
    if layer == 0:
        w1 = np.asarray(inputs["L0_ffn_w1"], np.float32)[perm]
        w2 = np.asarray(inputs["L0_ffn_w2"], np.float32)[perm]
        b1 = np.asarray(inputs["L0_ffn_b1"], np.float32)[perm]
        b2 = np.asarray(inputs["L0_ffn_b2"], np.float32)[perm]
        return {"w1_0": np.ascontiguousarray(w1),
                "w2_0": np.ascontiguousarray(w2),
                "b2_0": np.ascontiguousarray(b2),
                "b1T_0": _b1T(b1),
                "b1Te_0": _b1Te(b1, 2048)}
    w1 = np.asarray(inputs["L1_ffn_W1"], np.float32)[perm]
    w2 = np.asarray(inputs["L1_ffn_W2"], np.float32)[perm]
    b1 = np.asarray(inputs["L1_ffn_b1"], np.float32)[perm]
    b2 = np.asarray(inputs["L1_ffn_b2"], np.float32)[perm]
    return {"w1_1": np.ascontiguousarray(w1),
            "w2_1": np.ascontiguousarray(w2),
            "b2_1": np.ascontiguousarray(b2),
            "b1Te_1": _b1Te(b1, 4096)}


def _budget_ok(prefs):
    bud = np.asarray(BUD, np.float32)
    for p in prefs:
        c = np.sort(np.asarray(p)[:, NT])[::-1]
        if (c > bud).any():
            return False
    return True


def kernel(**inputs):
    x = np.ascontiguousarray(np.asarray(inputs["L0_attn_x"], np.float32))
    shards = [np.ascontiguousarray(x[b]) for b in range(NCORES)]
    cores = list(range(NCORES))
    spc = host_consts_sparse()

    maps = [{"x_in": shards[b], "c_ident": host_consts(b)["c_ident"],
             **_cnt_w(inputs)} for b in cores]
    ares = run_bass_kernel_spmd(_get("sa"), maps, core_ids=cores).results
    prefs = [np.asarray(ares[b]["pref_out"]) for b in cores]
    ag1 = np.ascontiguousarray(np.stack([p[:, NT] for p in prefs]))

    def glue0(res, b, offl):
        p = np.asarray(res[b]["pref_out"])
        dev, perm = _perm_inputs(p, offl)
        return {"xnT_in": res[b]["xnT_out"], "xntok_in": res[b]["xntok_out"],
                "mw_in": res[b]["mw_out"], "sel_in": res[b]["sel_out"],
                "pref_in": p, **dev, **_sp_w(inputs, 0, perm)}

    def glue1(res, b, offl):
        p = np.asarray(res[b]["pref_out"])
        dev, perm = _perm_inputs(p, offl)
        return {"xntok_in": res[b]["xntok_out"], "mw_in": res[b]["mw_out"],
                "sel_in": res[b]["sel_out"], **dev, **_sp_w(inputs, 1, perm)}

    wg1 = np.ascontiguousarray(np.asarray(inputs["L1_ffn_W_gate"], np.float32))

    bud = _profile(prefs)
    offl = _off_of(bud)
    maps = []
    for b in cores:
        cons = host_consts(b)
        maps.append({"x_in": shards[b], "c_ident": cons["c_ident"],
                     "c_ltri": cons["c_ltri"], "c_corelt": cons["c_corelt"],
                     "agcnt": ag1, **glue0(ares, b, offl), **_cnt_w(inputs),
                     **_l0a_w(inputs, b), **spc})
    bres = run_bass_kernel_spmd(_get("sb", bud), maps, core_ids=cores).results
    prefs = [np.asarray(bres[b]["pref_out"]) for b in cores]
    ag2 = np.ascontiguousarray(np.stack([p[:, NT] for p in prefs]))

    bud = _profile(prefs)
    offl = _off_of(bud)
    maps = []
    for b in cores:
        cons = host_consts(b)
        maps.append({"x_in": np.ascontiguousarray(
                         np.asarray(bres[b]["x_out"], np.float32)),
                     "c_ident": cons["c_ident"], "c_ltri": cons["c_ltri"],
                     "c_corelt": cons["c_corelt"], "agcnt": ag2,
                     **glue0(bres, b, offl), "wg_1": wg1, **spc})
    crez = run_bass_kernel_spmd(_get("sc", bud), maps, core_ids=cores).results
    prefs = [np.asarray(crez[b]["pref_out"]) for b in cores]

    bud = _profile(prefs)
    offl = _off_of(bud)
    maps = []
    for b in cores:
        cons = host_consts(b)
        maps.append({"x_in": np.ascontiguousarray(
                         np.asarray(crez[b]["x_out"], np.float32)),
                     "c_ident": cons["c_ident"], "c_ltri": cons["c_ltri"],
                     **glue1(crez, b, offl), "wg_1": wg1,
                     **_l1a_w(inputs), **spc})
    dres = run_bass_kernel_spmd(_get("sd", bud), maps, core_ids=cores).results
    prefs = [np.asarray(dres[b]["pref_out"]) for b in cores]

    bud = _profile(prefs)
    offl = _off_of(bud)
    maps = []
    for b in cores:
        cons = host_consts(b)
        maps.append({"x_in": np.ascontiguousarray(
                         np.asarray(dres[b]["x_out"], np.float32)),
                     "c_ident": cons["c_ident"], "c_ltri": cons["c_ltri"],
                     **glue1(dres, b, offl), **spc})
    eres = run_bass_kernel_spmd(_get("se", bud), maps, core_ids=cores).results
    return np.stack([np.asarray(eres[b]["x_out"], np.float32)
                     for b in cores]).astype(np.float32)
